# version 01
# baseline (speedup 1.0000x reference)
import math
from contextlib import ExitStack

import numpy as np
import concourse.bass as bass
import concourse.mybir as mybir
from concourse.bass_utils import run_bass_kernel_spmd

F32 = mybir.dt.float32
BF16 = mybir.dt.bfloat16
I32 = mybir.dt.int32
AF = mybir.ActivationFunctionType
ALU = mybir.AluOpType
AX = mybir.AxisListType

T = 4096
D = 1024
NHC = 8
HD = 64
DC = 128
DR = 32
FF = 3072
NT = T // 128
OWN = 2048
NQT = T // 512
SCALE = 1.0 / math.sqrt(HD + DR)
MEXP = 24.0
S127SQ = (1.0 / 127.0) ** 2

ENGS = ("sync", "gpsimd", "vector", "scalar", "tensor")


class Sem:
    def __init__(self, ctx, nc, name):
        self.h = ctx.enter_context(nc.semaphore(name))
        self.v = 0
        self.name = name


class Emitter:
    def __init__(self, who, eng, sems):
        self.who = who
        self.eng = eng
        self.sems = sems

    def op(self, engine, fn, inc=None, n=1):
        if self.who == engine:
            instr = fn(self.eng)
            if inc is not None:
                instr.then_inc(inc.h, n)
        if inc is not None:
            inc.v += n
        return inc.v if inc is not None else None

    def dma(self, engine, fn, inc):
        return self.op(engine, fn, inc, 16)

    def wait(self, engine, sem, val=None):
        if self.who == engine:
            self.eng.wait_ge(sem.h, sem.v if val is None else val)

    def wait_all(self, engine):
        for s in self.sems:
            if s.v > 0 and self.who == engine:
                self.eng.wait_ge(s.h, s.v)


def run_block(nc, sems, walk):
    with nc.Block() as blk:
        snap = [s.v for s in sems]
        for engname in ENGS:
            def body(eng, _en=engname):
                for s, v in zip(sems, snap):
                    s.v = v
                walk(Emitter(_en, eng, sems))
            getattr(blk, engname)(body)


def build(debug_taps=(), phases=4):
    nc = bass.Bass("TRN2", target_bir_lowering=False, debug=False)

    x_seq = nc.dram_tensor("x_seq", [T, D], F32, kind="ExternalInput").ap()
    x_own = nc.dram_tensor("x_own", [OWN, D], F32, kind="ExternalInput").ap()
    wkv = nc.dram_tensor("wkv", [D, DC], F32, kind="ExternalInput").ap()
    wkup = nc.dram_tensor("wkup", [DC, NHC * HD], F32, kind="ExternalInput").ap()
    wvup = nc.dram_tensor("wvup", [DC, NHC * HD], F32, kind="ExternalInput").ap()
    wq = nc.dram_tensor("wq", [D, NHC * HD], F32, kind="ExternalInput").ap()
    wqr = nc.dram_tensor("wqr", [D, DR], F32, kind="ExternalInput").ap()
    wkr = nc.dram_tensor("wkr", [DC, DR], F32, kind="ExternalInput").ap()
    wo = nc.dram_tensor("wo", [NHC * HD, D], F32, kind="ExternalInput").ap()
    wgT = nc.dram_tensor("wgT", [D, FF], I32, kind="ExternalInput").ap()
    wuT = nc.dram_tensor("wuT", [D, FF], I32, kind="ExternalInput").ap()
    wdT = nc.dram_tensor("wdT", [FF, D], I32, kind="ExternalInput").ap()
    cosT = nc.dram_tensor("cosT", [DR // 2, T], F32, kind="ExternalInput").ap()
    sinT = nc.dram_tensor("sinT", [DR // 2, T], F32, kind="ExternalInput").ap()
    eye = nc.dram_tensor("eye", [128, 128], F32, kind="ExternalInput").ap()
    masks = nc.dram_tensor("masks", [128, 4 * 512], F32, kind="ExternalInput").ap()

    y = nc.dram_tensor("y", [OWN, D], F32, kind="ExternalOutput").ap()

    snd = nc.dram_tensor("snd", [T, D], BF16)
    rcv = nc.dram_tensor("rcv", [OWN, D], BF16)
    r1spill = nc.dram_tensor("r1spill", [OWN, D], F32)

    tap_out = {}

    def tap(name, shape, dtype):
        if name in debug_taps:
            tap_out[name] = nc.dram_tensor("tap_" + name, shape, dtype,
                                           kind="ExternalOutput").ap()
            return tap_out[name]
        return None

    with ExitStack() as ctx:
        sbt = lambda n, s, d: ctx.enter_context(nc.sbuf_tensor(n, s, d))

        COL = [sbt(f"COL{i}", [128, T], BF16) for i in range(4)]
        S01 = ExitStack()
        s01t = lambda n, s, d: S01.enter_context(nc.sbuf_tensor(n, s, d, side="right"))
        KT = [s01t(f"KT{i}", [128, T], BF16) for i in range(4)]
        QT = [s01t(f"QT{i}", [128, T], BF16) for i in range(4)]
        KH = s01t("KH", [112, T], BF16)
        QH = s01t("QH", [112, T], BF16)
        MSK = s01t("MSK", [128, 4 * 512], BF16)
        VS = s01t("VS", [128, NT * (NHC * 65)], BF16)
        EYEB = sbt("EYEB", [128, 128], BF16)
        EYEF = sbt("EYEF", [128, 128], F32)
        ONE1 = sbt("ONE1", [1, 64], BF16)
        C_SCALE = sbt("C_SCALE", [128, 1], F32)
        C_MEXP = sbt("C_MEXP", [128, 1], F32)
        C_HALF = sbt("C_HALF", [128, 1], F32)
        C_S127 = sbt("C_S127", [128, 1], F32)
        C_S127SQ = sbt("C_S127SQ", [128, 1], F32)

        sems = []

        def mk(name):
            s = Sem(ctx, nc, name)
            sems.append(s)
            return s

        dL = mk("dL")
        dG = mk("dG")
        dS = mk("dS")
        dX = [mk("dXa"), mk("dXb")]
        dA = [mk("dAa"), mk("dAb")]
        dP = [mk("dPa"), mk("dPb")]
        dW = [mk("dWa"), mk("dWb")]
        pe = mk("pe")
        ve = mk("ve")
        se = mk("se")
        cc = mk("cc")

        with ExitStack() as p0:
            sb0 = lambda n, s, d: p0.enter_context(nc.sbuf_tensor(n, s, d, side="right"))
            ps0 = lambda n, s, d: p0.enter_context(nc.psum_tensor(n, s, d))
            XT = [sb0(f"XT{i}", [128, D], F32) for i in range(2)]
            XHB = sb0("XHB", [128, D], BF16)
            XHT = sb0("XHT", [128, D], BF16)
            CKT = sb0("CKT", [128, 128], BF16)
            VAR = sb0("VAR", [128, 1], F32)
            RSTD = sb0("RSTD", [128, 1], F32)
            BNS = sb0("BNS", [128, 12], F32)
            BNA = sb0("BNA", [128, 2], F32)
            WKVB = sb0("WKVB", [128, 8 * 128], BF16)
            WKUP = sb0("WKUP", [128, 512], BF16)
            WVUP = sb0("WVUP", [128, 512], BF16)
            WQB = sb0("WQB", [128, 8 * 512], BF16)
            WQR = sb0("WQR", [128, 8 * 32], BF16)
            WKR = sb0("WKR", [128, 32], BF16)
            sbr = lambda n, s, d: p0.enter_context(nc.sbuf_tensor(n, s, d, side="right"))
            COS = sbr("COS", [DR // 2, T], BF16)
            SIN = sbr("SIN", [DR // 2, T], BF16)
            SROPE = sbr("SROPE", [16, 512], BF16)
            RT1 = sbr("RT1", [DR // 2, 128], BF16)
            RT2 = sbr("RT2", [DR // 2, 128], BF16)
            pT = ps0("pT", [128, 1024], BF16)
            pCK = ps0("pCK", [128, 128], F32)
            pK = ps0("pK", [128, 512], F32)
            pV = ps0("pV", [128, 512], F32)
            pQa = ps0("pQa", [128, 128], F32)
            pQb = ps0("pQb", [128, 128], F32)
            pROPE = ps0("pROPE", [16, 512], F32)

            t_xhb = tap("xhb", [128, D], BF16)
            t_kt0 = tap("kt0", [128, T], BF16)
            t_qt0 = tap("qt0", [128, T], BF16)
            t_kh = tap("kh", [112, T], BF16)
            t_qh = tap("qh", [112, T], BF16)
            t_vs = tap("vs", [128, NT * NHC * 65], BF16)

            def p0_walk(E):
                E.op("vector", lambda v: v.memset(ONE1[:], 1.0))
                E.op("vector", lambda v: v.memset(C_SCALE[:], SCALE))
                E.op("vector", lambda v: v.memset(C_MEXP[:], -MEXP))
                E.op("vector", lambda v: v.memset(C_HALF[:], -0.5))
                E.op("vector", lambda v: v.memset(C_S127[:], 1.0 / 127.0))
                E.op("vector", lambda v: v.memset(C_S127SQ[:], S127SQ))
                E.op("vector", lambda v: v.memset(KH[0:32, :], 0.0))
                E.op("vector", lambda v: v.memset(KH[32:64, :], 0.0))
                E.op("vector", lambda v: v.memset(KH[64:96, :], 0.0))
                E.op("vector", lambda v: v.memset(QH[0:32, :], 0.0))
                E.op("vector", lambda v: v.memset(QH[32:64, :], 0.0))
                E.op("vector", lambda v: v.memset(QH[64:96, :], 0.0), inc=ve)

                E.dma("gpsimd", lambda g: g.dma_start(out=COS[:], in_=cosT), inc=dG)
                E.dma("gpsimd", lambda g: g.dma_start(out=SIN[:], in_=sinT), inc=dG)
                E.dma("sync", lambda s: s.dma_start(out=EYEF[:], in_=eye), inc=dL)
                E.dma("gpsimd", lambda g: g.dma_start(out=EYEB[:], in_=eye), inc=dG)
                E.dma("gpsimd", lambda g: g.dma_start(out=MSK[:], in_=masks), inc=dG)
                E.dma("gpsimd", lambda g: g.dma_start(
                    out=WKVB[:].rearrange("p (c j) -> p c j", j=128),
                    in_=wkv.rearrange("(c p) j -> p c j", p=128)), inc=dG)
                E.dma("gpsimd", lambda g: g.dma_start(out=WKUP[:], in_=wkup), inc=dG)
                E.dma("gpsimd", lambda g: g.dma_start(out=WVUP[:], in_=wvup), inc=dG)
                E.dma("gpsimd", lambda g: g.dma_start(
                    out=WQB[:].rearrange("p (c j) -> p c j", j=512),
                    in_=wq.rearrange("(c p) j -> p c j", p=128)), inc=dG)
                E.dma("gpsimd", lambda g: g.dma_start(
                    out=WQR[:].rearrange("p (c j) -> p c j", j=32),
                    in_=wqr.rearrange("(c p) j -> p c j", p=128)), inc=dG)
                E.dma("gpsimd", lambda g: g.dma_start(out=WKR[:], in_=wkr), inc=dG)
                dl_w = dG.v

                apply_done = [0, 0]
                pe_T_prev = 0
                q_evac = [0, 0]
                xhb_tap = 0

                for t in range(NT):
                    buf = t % 2
                    E.wait("sync", ve, apply_done[buf])
                    E.dma("sync", lambda s, _t=t, _b=buf: s.dma_start(
                        out=XT[_b][:], in_=x_seq[128 * _t:128 * _t + 128, :]), inc=dX[buf])
                    dl_x = dX[buf].v
                    E.wait("vector", dX[buf], dl_x)
                    if t == 0:
                        E.wait("vector", dG, dl_w)
                    E.op("vector", lambda v, _b=buf: v.bn_stats(out=BNS[:, 0:6], in_=XT[_b][:, 0:512]))
                    E.op("vector", lambda v, _b=buf: v.bn_stats(out=BNS[:, 6:12], in_=XT[_b][:, 512:1024]), inc=ve)
                    E.wait("vector", ve)
                    E.op("vector", lambda v: v.bn_aggr(out=BNA[:], in_=BNS[:]), inc=ve)
                    E.wait("vector", ve)
                    E.op("vector", lambda v: v.tensor_scalar_add(out=VAR[:], in0=BNA[:, 1:2], scalar1=1e-5), inc=ve)
                    E.wait("scalar", ve)
                    E.op("scalar", lambda s: s.activation(out=RSTD[:], in_=VAR[:], func=AF.Ln), inc=se)
                    E.wait("scalar", se)
                    E.op("scalar", lambda s: s.activation(out=RSTD[:], in_=RSTD[:], func=AF.Exp, scale=C_HALF[:]), inc=se)
                    E.wait("vector", se)
                    E.wait("vector", pe, pe_T_prev)
                    if xhb_tap:
                        E.wait("vector", dS, xhb_tap)
                    E.op("vector", lambda v, _b=buf: v.tensor_scalar(
                        out=XHB[:], in0=XT[_b][:], scalar1=BNA[:, 0:1], scalar2=RSTD[:],
                        op0=ALU.subtract, op1=ALU.mult), inc=ve)
                    apply_done[buf] = ve.v
                    ve_xhb = ve.v
                    if t_xhb is not None and t == 0:
                        E.wait("sync", ve, ve_xhb)
                        E.dma("sync", lambda s: s.dma_start(out=t_xhb, in_=XHB[:]), inc=dS)
                        xhb_tap = dS.v
                    E.wait("tensor", ve, ve_xhb)
                    for c in range(8):
                        E.op("tensor", lambda tt, _c=c: tt.transpose(
                            out=pT[:, 128 * _c:128 * _c + 128],
                            in_=XHB[:, 128 * _c:128 * _c + 128],
                            identity=EYEB[:]), inc=(pe if c == 7 else None))
                    pe_T_prev = pe.v
                    E.wait("vector", pe, pe_T_prev)
                    E.op("vector", lambda v: v.tensor_copy(out=XHT[:], in_=pT[:]), inc=ve)
                    ve_xht = ve.v
                    E.wait("tensor", ve, ve_xht)
                    if t == 0:
                        E.wait("tensor", dG, dl_w)
                    for c in range(8):
                        E.op("tensor", lambda tt, _c=c: tt.matmul(
                            out=pCK[:], lhsT=WKVB[:, 128 * _c:128 * _c + 128],
                            rhs=XHT[:, 128 * _c:128 * _c + 128],
                            start=(_c == 0), stop=(_c == 7)), inc=(pe if c == 7 else None))
                    E.wait("vector", pe)
                    E.op("vector", lambda v: v.tensor_copy(out=CKT[:], in_=pCK[:]), inc=ve)
                    ve_ck = ve.v
                    E.wait("tensor", ve, ve_ck)
                    for c in range(4):
                        E.op("tensor", lambda tt, _c=c: tt.matmul(
                            out=pK[:, 128 * _c:128 * _c + 128],
                            lhsT=WKUP[:, 128 * _c:128 * _c + 128], rhs=CKT[:],
                            start=True, stop=True), inc=(pe if c == 3 else None))
                    pe_k = pe.v
                    E.op("tensor", lambda tt: tt.matmul(out=pROPE[:, 0:128], lhsT=WKR[:, 0:16],
                                                        rhs=CKT[:], start=True, stop=True))
                    E.op("tensor", lambda tt: tt.matmul(out=pROPE[:, 128:256], lhsT=WKR[:, 16:32],
                                                        rhs=CKT[:], start=True, stop=True), inc=pe)
                    pe_kr = pe.v
                    E.op("tensor", lambda tt: tt.matmul(out=pV[:], lhsT=CKT[:], rhs=WVUP[:],
                                                        start=True, stop=True), inc=pe)
                    pe_v = pe.v
                    E.wait("vector", pe, pe_k)
                    for c in range(4):
                        E.op("vector", lambda v, _c=c, _t=t: v.tensor_copy(
                            out=KT[_c][:, 128 * _t:128 * _t + 128],
                            in_=pK[:, 128 * _c:128 * _c + 128]))

                    E.wait("vector", pe, pe_v)
                    E.op("vector", lambda v, _t=t: v.tensor_copy(
                        out=VS[:].rearrange("p (u h c) -> p u h c", h=NHC, c=65)[:, _t, :, 0:64],
                        in_=pV[:].rearrange("p (h c) -> p h c", c=64)))
                    E.op("vector", lambda v, _t=t: v.memset(
                        VS[:].rearrange("p (u h c) -> p u h c", h=NHC, c=65)[:, _t, :, 64:65],
                        1.0), inc=ve)
                    for c in range(4):
                        pq = pQa if c % 2 == 0 else pQb
                        E.wait("tensor", ve, q_evac[c % 2])
                        for kk in range(8):
                            E.op("tensor", lambda tt, _c=c, _k=kk, _pq=pq: tt.matmul(
                                out=_pq[:],
                                lhsT=WQB[:, 512 * _k + 128 * _c:512 * _k + 128 * _c + 128],
                                rhs=XHT[:, 128 * _k:128 * _k + 128],
                                start=(_k == 0), stop=(_k == 7)), inc=(pe if kk == 7 else None))
                        E.wait("vector", pe)
                        E.op("vector", lambda v, _c=c, _t=t, _pq=pq: v.tensor_copy(
                            out=QT[_c][:, 128 * _t:128 * _t + 128], in_=_pq[:]), inc=ve)
                        q_evac[c % 2] = ve.v
                    for kk in range(8):
                        E.op("tensor", lambda tt, _k=kk: tt.matmul(
                            out=pROPE[:, 256:384], lhsT=WQR[:, 32 * _k:32 * _k + 16],
                            rhs=XHT[:, 128 * _k:128 * _k + 128],
                            start=(_k == 0), stop=(_k == 7)))
                    for kk in range(8):
                        E.op("tensor", lambda tt, _k=kk: tt.matmul(
                            out=pROPE[:, 384:512], lhsT=WQR[:, 32 * _k + 16:32 * _k + 32],
                            rhs=XHT[:, 128 * _k:128 * _k + 128],
                            start=(_k == 0), stop=(_k == 7)), inc=(pe if kk == 7 else None))
                    pe_rope = pe.v
                    E.wait("vector", pe, pe_rope)
                    E.op("vector", lambda v: v.tensor_copy(out=SROPE[:], in_=pROPE[:]), inc=ve)
                    E.wait("vector", ve)
                    cs = slice(128 * t, 128 * t + 128)
                    for (x1o, x2o, OUTT) in ((0, 128, KH), (256, 384, QH)):
                        E.op("vector", lambda v, _a=x1o, _c=cs: v.tensor_mul(
                            out=RT1[:], in0=SROPE[:, _a:_a + 128], in1=COS[:, _c]))
                        E.op("vector", lambda v, _b=x2o, _c=cs: v.tensor_mul(
                            out=RT2[:], in0=SROPE[:, _b:_b + 128], in1=SIN[:, _c]), inc=ve)
                        E.wait("vector", ve)
                        E.op("vector", lambda v, _O=OUTT, _c=cs: v.tensor_sub(
                            out=_O[0:16, _c], in0=RT1[:], in1=RT2[:]), inc=ve)
                        E.wait("vector", ve)
                        E.op("vector", lambda v, _b=x2o, _c=cs: v.tensor_mul(
                            out=RT1[:], in0=SROPE[:, _b:_b + 128], in1=COS[:, _c]))
                        E.op("vector", lambda v, _a=x1o, _c=cs: v.tensor_mul(
                            out=RT2[:], in0=SROPE[:, _a:_a + 128], in1=SIN[:, _c]), inc=ve)
                        E.wait("vector", ve)
                        E.op("vector", lambda v, _O=OUTT, _c=cs: v.tensor_add(
                            out=_O[96:112, _c], in0=RT1[:], in1=RT2[:]), inc=ve)
                        E.wait("vector", ve)


                if t_kt0 is not None:
                    E.wait("sync", ve)
                    E.wait("sync", pe)
                    E.dma("sync", lambda s: s.dma_start(out=t_kt0, in_=KT[0][:]), inc=dS)
                    E.dma("sync", lambda s: s.dma_start(out=t_qt0, in_=QT[0][:]), inc=dS)
                    if t_kh is not None:
                        E.dma("sync", lambda s: s.dma_start(out=t_kh, in_=KH[:]), inc=dS)
                    if t_qh is not None:
                        E.dma("sync", lambda s: s.dma_start(out=t_qh, in_=QH[:]), inc=dS)
                    E.dma("sync", lambda s: s.dma_start(out=t_vs, in_=VS[:]), inc=dS)
                    E.wait("sync", dS)

                for eng in ENGS:
                    E.wait_all(eng)

            run_block(nc, sems, p0_walk)

        if phases == 1:
            return nc, tap_out

        with ExitStack() as p1:
            sb1 = lambda n, s, d: p1.enter_context(nc.sbuf_tensor(n, s, d, side="right"))
            ps1 = lambda n, s, d: p1.enter_context(nc.psum_tensor(n, s, d))
            PB = [sb1(f"PB{i}", [128, 3 * 512], BF16) for i in range(2)]
            REC = sb1("REC", [1, 512], F32)
            RECB = sb1("RECB", [1, 512], BF16)
            BCS = sb1("BCS", [64, 512], F32)
            S_ = [ps1("S0", [128, 3 * 512], F32), ps1("S1", [128, 3 * 512], F32)]
            AVp = ps1("AVp", [128, 512], F32)
            BCp = ps1("BCp", [64, 512], F32)

            t_col0 = tap("col0", [128, T], BF16)
            t_col1 = tap("col1", [128, T], BF16)

            def p1_walk(E):
                div_done = 0
                last_exp = [0, 0]
                for h in range(NHC):
                    c, r = h // 2, 64 * (h % 2)
                    E.wait("vector", pe)
                    E.op("vector", lambda v, _c=c, _r=r: v.tensor_copy(
                        out=KH[32:64, :], in_=KT[_c][_r:_r + 32, :]))
                    E.op("vector", lambda v, _c=c, _r=r: v.tensor_copy(
                        out=KH[64:96, :], in_=KT[_c][_r + 32:_r + 64, :]))
                    E.op("vector", lambda v, _c=c, _r=r: v.tensor_copy(
                        out=QH[32:64, :], in_=QT[_c][_r:_r + 32, :]))
                    E.op("vector", lambda v, _c=c, _r=r: v.tensor_copy(
                        out=QH[64:96, :], in_=QT[_c][_r + 32:_r + 64, :]), inc=ve)
                    asm_done = ve.v
                    for qt in range(NQT):
                        nk = 4 * qt + 4
                        groups = [list(range(g, min(g + 3, nk))) for g in range(0, nk, 3)]
                        av_cnt = 0
                        for gi, grp in enumerate(groups):
                            par = gi % 2
                            E.wait("tensor", ve, asm_done)
                            E.wait("tensor", se, last_exp[par])
                            for j, kt in enumerate(grp):
                                diag = kt >= 4 * qt
                                last_j = (j == len(grp) - 1)
                                E.op("tensor", lambda tt, _kt=kt, _qt=qt, _p=par, _j=j, _d=diag: tt.matmul(
                                    out=S_[_p][:, 512 * _j:512 * _j + 512],
                                    lhsT=KH[:, 128 * _kt:128 * _kt + 128],
                                    rhs=QH[:, 512 * _qt:512 * _qt + 512],
                                    start=True, stop=(not _d)),
                                    inc=(pe if (last_j and not diag) else None))
                                if diag:
                                    dlt = kt - 4 * qt
                                    E.op("tensor", lambda tt, _p=par, _j=j, _dl=dlt: tt.matmul(
                                        out=S_[_p][:, 512 * _j:512 * _j + 512],
                                        lhsT=EYEB[:], rhs=MSK[:, 512 * _dl:512 * _dl + 512],
                                        start=False, stop=True),
                                        inc=(pe if last_j else None))
                            pe_sc = pe.v
                            E.wait("scalar", pe, pe_sc)
                            E.op("scalar", lambda s, _p=par, _n=len(grp): s.activation(
                                out=PB[_p][:, 0:512 * _n], in_=S_[_p][:, 0:512 * _n],
                                func=AF.Exp, scale=C_SCALE[:], bias=C_MEXP[:]), inc=se)
                            last_exp[par] = se.v
                            E.wait("tensor", se, last_exp[par])
                            if gi == 0:
                                E.wait("tensor", ve, div_done)
                            for j, kt in enumerate(grp):
                                first = (av_cnt == 0)
                                last = (av_cnt == nk - 1)
                                av_cnt += 1
                                E.op("tensor", lambda tt, _kt=kt, _h=h, _p=par, _j=j, _f=first, _l=last: tt.matmul(
                                    out=AVp[0:65, :],
                                    lhsT=VS[:, 520 * _kt + 65 * _h:520 * _kt + 65 * _h + 65],
                                    rhs=PB[_p][:, 512 * _j:512 * _j + 512],
                                    start=_f, stop=_l), inc=(pe if last else None))
                        pe_av = pe.v
                        E.wait("vector", pe, pe_av)
                        E.op("vector", lambda v: v.reciprocal(out=REC[:], in_=AVp[64:65, :]), inc=ve)
                        E.wait("vector", ve)
                        E.op("vector", lambda v: v.tensor_copy(out=RECB[:], in_=REC[:]), inc=ve)
                        E.wait("tensor", ve)
                        E.wait("tensor", se)
                        E.op("tensor", lambda tt: tt.matmul(out=BCp[:], lhsT=ONE1[:], rhs=RECB[:],
                                                            start=True, stop=True), inc=pe)
                        E.wait("scalar", pe)
                        E.wait("scalar", ve, div_done)
                        E.op("scalar", lambda s: s.copy(out=BCS[:], in_=BCp[:]), inc=se)
                        E.wait("vector", se)
                        E.op("vector", lambda v, _h=h, _qt=qt: v.scalar_tensor_tensor(
                            out=COL[_h // 2][64 * (_h % 2):64 * (_h % 2) + 64,
                                             512 * _qt:512 * _qt + 512],
                            in0=AVp[0:64, :], scalar=1.0, in1=BCS[:],
                            op0=ALU.mult, op1=ALU.mult), inc=ve)
                        div_done = ve.v

                if t_col0 is not None:
                    E.wait("sync", ve)
                    E.dma("sync", lambda s: s.dma_start(out=t_col0, in_=COL[0][:]), inc=dS)
                    if t_col1 is not None:
                        E.dma("sync", lambda s: s.dma_start(out=t_col1, in_=COL[1][:]), inc=dS)
                    E.wait("sync", dS)

                for eng in ENGS:
                    E.wait_all(eng)

            run_block(nc, sems, p1_walk)

        if phases == 2:
            return nc, tap_out

        S01.close()

        S23 = ExitStack()
        X2T = [S23.enter_context(nc.sbuf_tensor(f"X2T{i}", [128, OWN], BF16, side="right"))
               for i in range(8)]
        with ExitStack() as p2:
            sb2 = lambda n, s, d: p2.enter_context(nc.sbuf_tensor(n, s, d, side="right"))
            ps2 = lambda n, s, d: p2.enter_context(nc.psum_tensor(n, s, d))
            WOB = sb2("WOB", [128, 4 * 1024], BF16)
            PRT = [sb2(f"PRT{i}", [128, D], BF16) for i in range(2)]
            XO = [sb2(f"XO{i}", [128, D], F32) for i in range(2)]
            AT = [sb2(f"AT{i}", [128, D], BF16) for i in range(2)]
            R1 = sb2("R1", [128, D], F32)
            X2B = sb2("X2B", [128, D], BF16)
            VAR2 = sb2("VAR2", [128, 1], F32)
            RSTD2 = sb2("RSTD2", [128, 1], F32)
            BNS2 = sb2("BNS2", [128, 12], F32)
            BNA2 = sb2("BNA2", [128, 2], F32)
            pWO = [ps2(f"pWO{i}", [128, 1024], F32) for i in range(2)]
            pT2 = ps2("pT2", [128, 1024], BF16)

            t_r1 = tap("r1", [OWN, D], F32)
            t_x2 = tap("x2", [128, OWN], BF16)

            def p2_walk2(E):
                p2_state = {}
                for fc in range(4):
                    E.dma("gpsimd", lambda g, _f=fc: g.dma_start(
                        out=WOB[:, 1024 * _f:1024 * _f + 1024],
                        in_=wo[128 * _f:128 * _f + 128, :]), inc=dG)
                dl_wo = dG.v
                prt_free = [0, 0]
                evac_done = [0, 0]
                for t in range(NT):
                    buf = t % 2
                    E.wait("tensor", dG, dl_wo)
                    E.wait("tensor", ve, evac_done[buf])
                    for half in range(2):
                        for fc in range(4):
                            E.op("tensor", lambda t_, _f=fc, _t=t, _h=half, _b=buf: t_.matmul(
                                out=pWO[_b][:, 512 * _h:512 * _h + 512],
                                lhsT=COL[_f][:, 128 * _t:128 * _t + 128],
                                rhs=WOB[:, 1024 * _f + 512 * _h:1024 * _f + 512 * _h + 512],
                                start=(_f == 0), stop=(_f == 3)),
                                inc=(pe if (half == 1 and fc == 3) else None))
                    pe_wo = pe.v
                    E.wait("vector", pe, pe_wo)
                    E.wait("vector", dP[buf], prt_free[buf])
                    E.op("vector", lambda v, _b=buf: v.tensor_copy(out=PRT[_b][:], in_=pWO[_b][:]), inc=ve)
                    evac_done[buf] = ve.v
                    E.wait("sync", ve, evac_done[buf])
                    E.dma("sync", lambda s, _t=t, _b=buf: s.dma_start(
                        out=snd.ap()[128 * _t:128 * _t + 128, :], in_=PRT[_b][:]), inc=dP[buf])
                    prt_free[buf] = dP[buf].v
                E.wait("gpsimd", dP[0])
                E.wait("gpsimd", dP[1])
                E.op("gpsimd", lambda g: g.collective_compute(
                    "ReduceScatter", ALU.add,
                    replica_groups=[[0, 1], [2, 3], [4, 5], [6, 7]],
                    ins=[snd.ap().opt()], outs=[rcv.ap().opt()]), inc=cc)

                r1_done = 0
                xo_free = [0, 0]
                at_free = [0, 0]
                pe_t2 = 0
                for t in range(16):
                    buf = t % 2
                    E.wait("sync", ve, xo_free[buf])
                    E.dma("sync", lambda s, _t=t, _b=buf: s.dma_start(
                        out=XO[_b][:], in_=x_own[128 * _t:128 * _t + 128, :]), inc=dX[buf])
                    E.wait("sync", cc)
                    E.wait("sync", ve, at_free[buf])
                    E.dma("sync", lambda s, _t=t, _b=buf: s.dma_start(
                        out=AT[_b][:], in_=rcv.ap()[128 * _t:128 * _t + 128, :]), inc=dA[buf])
                    E.wait("vector", dX[buf])
                    E.wait("vector", dA[buf])
                    E.wait("vector", dS, r1_done)
                    E.op("vector", lambda v, _b=buf: v.tensor_add(out=R1[:], in0=XO[_b][:], in1=AT[_b][:]), inc=ve)
                    xo_free[buf] = ve.v
                    at_free[buf] = ve.v
                    E.wait("sync", ve)
                    E.dma("sync", lambda s, _t=t: s.dma_start(
                        out=r1spill.ap()[128 * _t:128 * _t + 128, :], in_=R1[:]), inc=dS)
                    r1_done = dS.v
                    if t_r1 is not None:
                        E.dma("sync", lambda s, _t=t: s.dma_start(
                            out=t_r1[128 * _t:128 * _t + 128, :], in_=R1[:]), inc=dS)
                        r1_done = dS.v
                    E.wait("vector", ve)
                    E.op("vector", lambda v: v.bn_stats(out=BNS2[:, 0:6], in_=R1[:, 0:512]))
                    E.op("vector", lambda v: v.bn_stats(out=BNS2[:, 6:12], in_=R1[:, 512:1024]), inc=ve)
                    E.wait("vector", ve)
                    E.op("vector", lambda v: v.bn_aggr(out=BNA2[:], in_=BNS2[:]), inc=ve)
                    E.wait("vector", ve)
                    E.op("vector", lambda v: v.tensor_scalar_add(out=VAR2[:], in0=BNA2[:, 1:2], scalar1=1e-5), inc=ve)
                    E.wait("scalar", ve)
                    E.op("scalar", lambda s: s.activation(out=RSTD2[:], in_=VAR2[:], func=AF.Ln), inc=se)
                    E.wait("scalar", se)
                    E.op("scalar", lambda s: s.activation(out=RSTD2[:], in_=RSTD2[:], func=AF.Exp, scale=C_HALF[:]), inc=se)
                    E.wait("vector", se)
                    E.wait("vector", pe, pe_t2)
                    E.op("vector", lambda v: v.tensor_scalar(
                        out=X2B[:], in0=R1[:], scalar1=BNA2[:, 0:1], scalar2=RSTD2[:],
                        op0=ALU.subtract, op1=ALU.mult), inc=ve)
                    ve_x2 = ve.v
                    E.wait("tensor", ve, ve_x2)
                    for c in range(8):
                        E.op("tensor", lambda t_, _c=c: t_.transpose(
                            out=pT2[:, 128 * _c:128 * _c + 128],
                            in_=X2B[:, 128 * _c:128 * _c + 128],
                            identity=EYEB[:]), inc=(pe if c == 7 else None))
                    pe_t2 = pe.v
                    E.wait("vector", pe, pe_t2)
                    for c in range(8):
                        E.op("vector", lambda v, _c=c, _t=t: v.tensor_copy(
                            out=X2T[_c][:, 128 * _t:128 * _t + 128],
                            in_=pT2[:, 128 * _c:128 * _c + 128]),
                            inc=(ve if c == 7 else None))

                if t_x2 is not None:
                    E.wait("sync", ve)
                    E.dma("sync", lambda s: s.dma_start(out=t_x2, in_=X2T[0][:]), inc=dS)
                    E.wait("sync", dS)

                for eng in ENGS:
                    E.wait_all(eng)

            run_block(nc, sems, p2_walk2)

        if phases == 3:
            return nc, tap_out

        with ExitStack() as p3:
            sb3 = lambda n, s, d: p3.enter_context(nc.sbuf_tensor(n, s, d, side="right"))
            ps3 = lambda n, s, d: p3.enter_context(nc.psum_tensor(n, s, d))
            HB = p3.enter_context(nc.sbuf_tensor("HB", [128, 24 * OWN], BF16, side="left"))

            p3a = ExitStack()
            WGU = [p3a.enter_context(nc.sbuf_tensor(f"WGU{i}", [128, 2 * 1024], BF16, side="right")) for i in range(2)]
            GS = p3a.enter_context(nc.sbuf_tensor("GS", [128, 512], BF16, side="right"))
            US = p3a.enter_context(nc.sbuf_tensor("US", [128, 512], BF16, side="right"))
            pG = p3a.enter_context(nc.psum_tensor("pG", [128, 512], F32))
            pU = p3a.enter_context(nc.psum_tensor("pU", [128, 512], F32))

            def p3a_walk(E):
                wgu_free = [0, 0]
                silu_done = 0
                umul_done = 0
                for hc in range(24):
                    wb = hc % 2
                    E.wait("gpsimd", pe, wgu_free[wb])
                    E.dma("gpsimd", lambda g, _h=hc, _b=wb: g.dma_start(
                        out=WGU[_b][:, 0:1024].rearrange("p (c j) -> p c j", j=128),
                        in_=wgT.rearrange("(c p) j -> p c j", p=128)[:, :, 128 * _h:128 * _h + 128]), inc=dW[wb])
                    E.dma("gpsimd", lambda g, _h=hc, _b=wb: g.dma_start(
                        out=WGU[_b][:, 1024:2048].rearrange("p (c j) -> p c j", j=128),
                        in_=wuT.rearrange("(c p) j -> p c j", p=128)[:, :, 128 * _h:128 * _h + 128]), inc=dW[wb])
                    dl_gu = dW[wb].v
                    for tt in range(4):
                        E.wait("tensor", dW[wb], dl_gu)
                        E.wait("tensor", se, silu_done)
                        for c in range(8):
                            E.op("tensor", lambda t_, _c=c, _t=tt, _b=wb: t_.matmul(
                                out=pG[:], lhsT=WGU[_b][:, 128 * _c:128 * _c + 128],
                                rhs=X2T[_c][:, 512 * _t:512 * _t + 512],
                                start=(_c == 0), stop=(_c == 7)), inc=(pe if c == 7 else None))
                        pe_g = pe.v
                        E.wait("tensor", ve, umul_done)
                        for c in range(8):
                            E.op("tensor", lambda t_, _c=c, _t=tt, _b=wb: t_.matmul(
                                out=pU[:], lhsT=WGU[_b][:, 1024 + 128 * _c:1024 + 128 * _c + 128],
                                rhs=X2T[_c][:, 512 * _t:512 * _t + 512],
                                start=(_c == 0), stop=(_c == 7)), inc=(pe if c == 7 else None))
                        pe_u = pe.v
                        if tt == 3:
                            wgu_free[wb] = pe.v
                        E.wait("scalar", pe, pe_g)
                        E.wait("scalar", ve, umul_done)
                        E.op("scalar", lambda s: s.activation(out=GS[:], in_=pG[:], func=AF.Silu,
                                                              scale=C_S127[:]), inc=se)
                        silu_done = se.v
                        E.wait("vector", pe, pe_u)
                        E.op("vector", lambda v: v.tensor_copy(out=US[:], in_=pU[:]), inc=ve)
                        E.wait("vector", se, silu_done)
                        E.wait("vector", ve)
                        E.op("vector", lambda v, _h=hc, _t=tt: v.tensor_mul(
                            out=HB[:, OWN * _h + 512 * _t:OWN * _h + 512 * _t + 512],
                            in0=GS[:], in1=US[:]), inc=ve)
                        umul_done = ve.v
                for eng in ENGS:
                    E.wait_all(eng)

            run_block(nc, sems, p3a_walk)
            p3a.close()
            S23.close()

            WDA = sb3("WDA", [128, 24 * 1024], BF16)
            sbl = lambda n, s, d: p3.enter_context(nc.sbuf_tensor(n, s, d, side="left"))
            FDT = sbl("FDT", [128, 4096], F32)
            R1T = [sbl(f"R1T{i}", [128, D], F32) for i in range(2)]
            OStg = sbl("OStg", [128, D], F32)
            pD = ps3("pD", [128, 3072], F32)
            pR = ps3("pR", [128, 1024], F32)

            def p3b_walk(E):
                for hc in range(24):
                    E.dma("gpsimd", lambda g, _h=hc: g.dma_start(
                        out=WDA[:, 1024 * _h:1024 * _h + 1024],
                        in_=wdT[128 * _h:128 * _h + 128, :]), inc=dG)
                dl_wd = dG.v

                fdt_free = 0
                r1t_free = [0, 0]
                ost_free = 0
                stt_done = 0
                for tt in range(4):
                    E.wait("tensor", dG, dl_wd)
                    E.wait("tensor", se, fdt_free)
                    E.wait("tensor", ve, stt_done)
                    for hc in range(24):
                        for dc in range(8):
                            dst = pD[:, 512 * dc:512 * dc + 512] if dc < 6 else \
                                  pR[:, 512 * (dc - 6):512 * (dc - 6) + 512]
                            E.op("tensor", lambda t_, _h=hc, _d=dc, _t=tt, _dst=dst: t_.matmul(
                                out=_dst,
                                lhsT=WDA[:, 1024 * _h + 128 * _d:1024 * _h + 128 * _d + 128],
                                rhs=HB[:, OWN * _h + 512 * _t:OWN * _h + 512 * _t + 512],
                                start=(_h == 0), stop=(_h == 23)),
                                inc=(pe if (hc == 23 and dc == 7) else None))
                    pe_d = pe.v
                    E.wait("scalar", pe, pe_d)
                    for dc in range(8):
                        src = pD[:, 512 * dc:512 * dc + 512] if dc < 6 else \
                              pR[:, 512 * (dc - 6):512 * (dc - 6) + 512]
                        E.op("scalar", lambda s, _d=dc, _src=src: s.activation(
                            out=FDT[:, 512 * _d:512 * _d + 512], in_=_src,
                            func=AF.Copy, scale=C_S127SQ[:]),
                            inc=(se if dc == 7 else None))
                    fdt_free = se.v
                    for s4 in range(4):
                        sb_ = s4 % 2
                        E.wait("sync", ve, r1t_free[sb_])
                        E.dma("sync", lambda s, _t=tt, _s=s4, _b=sb_: s.dma_start(
                            out=R1T[_b][:],
                            in_=r1spill.ap()[512 * _t + 128 * _s:512 * _t + 128 * _s + 128, :]),
                            inc=dX[sb_])
                        dl_r1 = (sb_, dX[sb_].v)
                        E.wait("tensor", se, fdt_free)
                        if s4 >= 2:
                            E.wait("tensor", ve)
                        for dc in range(8):
                            E.op("tensor", lambda t_, _d=dc, _s=s4: t_.transpose(
                                out=pD[:, 1024 * (_s % 2) + 128 * _d:1024 * (_s % 2) + 128 * _d + 128],
                                in_=FDT[:, 512 * _d + 128 * _s:512 * _d + 128 * _s + 128],
                                identity=EYEF[:]), inc=(pe if dc == 7 else None))
                        pe_tr = pe.v
                        E.wait("vector", pe, pe_tr)
                        E.wait("vector", dX[dl_r1[0]], dl_r1[1])
                        E.wait("vector", dS, ost_free)
                        E.op("vector", lambda v, _s=s4, _b=sb_: v.scalar_tensor_tensor(
                            out=OStg[:], in0=pD[:, 1024 * (_s % 2):1024 * (_s % 2) + 1024],
                            scalar=1.0, in1=R1T[_b][:],
                            op0=ALU.mult, op1=ALU.add), inc=ve)
                        ve_o = ve.v
                        r1t_free[sb_] = ve_o
                        E.wait("sync", ve, ve_o)
                        E.dma("sync", lambda s, _t=tt, _s=s4: s.dma_start(
                            out=y[512 * _t + 128 * _s:512 * _t + 128 * _s + 128, :],
                            in_=OStg[:]), inc=dS)
                        ost_free = dS.v
                    stt_done = ve.v

                for eng in ENGS:
                    E.wait_all(eng)

            run_block(nc, sems, p3b_walk)

    return nc, tap_out


def _host_inputs(core, inputs):
    b, p = core // 2, core % 2
    h0 = NHC * p * HD
    x = np.ascontiguousarray(np.asarray(inputs["L0_attn_x"])[b])
    out = dict(
        x_seq=x,
        x_own=np.ascontiguousarray(x[OWN * p:OWN * p + OWN]),
        wkv=np.ascontiguousarray(np.asarray(inputs["L0_attn_W_kv_down"])),
        wkup=np.ascontiguousarray(np.asarray(inputs["L0_attn_W_k_up"])[:, h0:h0 + NHC * HD]),
        wvup=np.ascontiguousarray(np.asarray(inputs["L0_attn_W_v_up"])[:, h0:h0 + NHC * HD]),
        wq=np.ascontiguousarray(np.asarray(inputs["L0_attn_W_q"])[:, h0:h0 + NHC * HD]),
        wqr=np.ascontiguousarray(np.asarray(inputs["L0_attn_W_q_rope"])),
        wkr=np.ascontiguousarray(np.asarray(inputs["L0_attn_W_k_rope"])),
        wo=np.ascontiguousarray(np.asarray(inputs["L0_attn_W_o"])[h0:h0 + NHC * HD, :]),
        wgT=np.ascontiguousarray(np.asarray(inputs["L0_ffn_w_gate"]).T),
        wuT=np.ascontiguousarray(np.asarray(inputs["L0_ffn_w_up"]).T),
        wdT=np.ascontiguousarray(np.asarray(inputs["L0_ffn_w_down"]).T),
    )
    inv_freq = 1.0 / 10000.0 ** (np.arange(0, DR, 2, dtype=np.float32) / DR)
    tpos = np.arange(T, dtype=np.float32)
    freqs = np.outer(tpos, inv_freq).astype(np.float32)
    out["cosT"] = np.ascontiguousarray(np.cos(freqs).T.astype(np.float32))
    out["sinT"] = np.ascontiguousarray(np.sin(freqs).T.astype(np.float32))
    out["eye"] = np.eye(128, dtype=np.float32)
    masks_ = np.zeros((128, 4, 512), dtype=np.float32)
    k_loc = np.arange(128)[:, None]
    q_loc = np.arange(512)[None, :]
    for i, dlt in enumerate((0, 128, 256, 384)):
        masks_[:, i, :] = np.where(k_loc + dlt > q_loc, -1e9, 0.0)
    out["masks"] = np.ascontiguousarray(masks_.reshape(128, 4 * 512))
    return out


def kernel(**inputs):
    nc, _ = build()
    in_maps = [_host_inputs(c, inputs) for c in range(8)]
    res = run_bass_kernel_spmd(nc, in_maps, core_ids=list(range(8)))
    out = np.zeros((4, T, D), dtype=np.float32)
    for c in range(8):
        b, p = c // 2, c % 2
        out[b, OWN * p:OWN * p + OWN] = np.asarray(res.results[c]["y"], dtype=np.float32)
    return out


# version 02
# speedup vs baseline: 1.3660x; 1.3660x over previous
import math
from contextlib import ExitStack

import numpy as np
import concourse.bass as bass
import concourse.mybir as mybir
from concourse.bass_utils import run_bass_kernel_spmd

F32 = mybir.dt.float32
BF16 = mybir.dt.bfloat16
I32 = mybir.dt.int32
AF = mybir.ActivationFunctionType
ALU = mybir.AluOpType
AX = mybir.AxisListType

T = 4096
D = 1024
NHC = 8
HD = 64
DC = 128
DR = 32
FF = 3072
NT = T // 128
OWN = 2048
NQT = T // 512
SCALE = 1.0 / math.sqrt(HD + DR)
MEXP = 24.0
S127SQ = (1.0 / 127.0) ** 2

ENGS = ("sync", "gpsimd", "vector", "scalar", "tensor")


class Sem:
    def __init__(self, ctx, nc, name):
        self.h = ctx.enter_context(nc.semaphore(name))
        self.v = 0
        self.name = name


class Emitter:
    def __init__(self, who, eng, sems):
        self.who = who
        self.eng = eng
        self.sems = sems

    def op(self, engine, fn, inc=None, n=1):
        if self.who == engine:
            instr = fn(self.eng)
            if inc is not None:
                instr.then_inc(inc.h, n)
        if inc is not None:
            inc.v += n
        return inc.v if inc is not None else None

    def dma(self, engine, fn, inc):
        return self.op(engine, fn, inc, 16)

    def wait(self, engine, sem, val=None):
        if self.who == engine:
            self.eng.wait_ge(sem.h, sem.v if val is None else val)

    def wait_all(self, engine):
        for s in self.sems:
            if s.v > 0 and self.who == engine:
                self.eng.wait_ge(s.h, s.v)


def run_block(nc, sems, walk):
    with nc.Block() as blk:
        snap = [s.v for s in sems]
        for engname in ENGS:
            def body(eng, _en=engname):
                for s, v in zip(sems, snap):
                    s.v = v
                walk(Emitter(_en, eng, sems))
            getattr(blk, engname)(body)


def build(debug_taps=(), phases=4):
    nc = bass.Bass("TRN2", target_bir_lowering=False, debug=False)

    x_seq = nc.dram_tensor("x_seq", [T, D], F32, kind="ExternalInput").ap()
    x_own = nc.dram_tensor("x_own", [OWN, D], F32, kind="ExternalInput").ap()
    wkv = nc.dram_tensor("wkv", [D, DC], F32, kind="ExternalInput").ap()
    wkup = nc.dram_tensor("wkup", [DC, NHC * HD], F32, kind="ExternalInput").ap()
    wvup = nc.dram_tensor("wvup", [DC, NHC * HD], F32, kind="ExternalInput").ap()
    wq = nc.dram_tensor("wq", [D, NHC * HD], F32, kind="ExternalInput").ap()
    wqr = nc.dram_tensor("wqr", [D, DR], F32, kind="ExternalInput").ap()
    wkr = nc.dram_tensor("wkr", [DC, DR], F32, kind="ExternalInput").ap()
    wo = nc.dram_tensor("wo", [NHC * HD, D], F32, kind="ExternalInput").ap()
    wgT = nc.dram_tensor("wgT", [D, FF], I32, kind="ExternalInput").ap()
    wuT = nc.dram_tensor("wuT", [D, FF], I32, kind="ExternalInput").ap()
    wdT = nc.dram_tensor("wdT", [FF, D], I32, kind="ExternalInput").ap()
    cosT = nc.dram_tensor("cosT", [DR // 2, T], F32, kind="ExternalInput").ap()
    sinT = nc.dram_tensor("sinT", [DR // 2, T], F32, kind="ExternalInput").ap()
    eye = nc.dram_tensor("eye", [128, 128], F32, kind="ExternalInput").ap()
    masks = nc.dram_tensor("masks", [128, 4 * 512], F32, kind="ExternalInput").ap()

    y = nc.dram_tensor("y", [OWN, D], F32, kind="ExternalOutput").ap()

    snd = nc.dram_tensor("snd", [T, D], BF16)
    rcv = nc.dram_tensor("rcv", [OWN, D], BF16)
    r1spill = nc.dram_tensor("r1spill", [OWN, D], F32)

    tap_out = {}

    def tap(name, shape, dtype):
        if name in debug_taps:
            tap_out[name] = nc.dram_tensor("tap_" + name, shape, dtype,
                                           kind="ExternalOutput").ap()
            return tap_out[name]
        return None

    with ExitStack() as ctx:
        sbt = lambda n, s, d: ctx.enter_context(nc.sbuf_tensor(n, s, d))

        COL = [sbt(f"COL{i}", [128, T], BF16) for i in range(4)]
        S01 = ExitStack()
        s01t = lambda n, s, d: S01.enter_context(nc.sbuf_tensor(n, s, d, side="right"))
        KT = [s01t(f"KT{i}", [128, T], BF16) for i in range(4)]
        QT = [s01t(f"QT{i}", [128, T], BF16) for i in range(4)]
        KH = s01t("KH", [112, T], BF16)
        QH = s01t("QH", [112, T], BF16)
        MSK = s01t("MSK", [128, 4 * 512], BF16)
        VS = s01t("VS", [128, NT * (NHC * 65)], BF16)
        EYEB = sbt("EYEB", [128, 128], BF16)
        EYEF = sbt("EYEF", [128, 128], F32)
        ONE1 = sbt("ONE1", [1, 64], BF16)
        C_SCALE = sbt("C_SCALE", [128, 1], F32)
        C_MEXP = sbt("C_MEXP", [128, 1], F32)
        C_HALF = sbt("C_HALF", [128, 1], F32)
        C_S127 = sbt("C_S127", [128, 1], F32)
        C_S127SQ = sbt("C_S127SQ", [128, 1], F32)

        sems = []

        def mk(name):
            s = Sem(ctx, nc, name)
            sems.append(s)
            return s

        dL = mk("dL")
        dG = mk("dG")
        dS = mk("dS")
        dX = [mk("dXa"), mk("dXb")]
        dA = [mk("dAa"), mk("dAb")]
        dP = [mk("dPa"), mk("dPb")]
        dW = [mk("dWa"), mk("dWb")]
        pe = mk("pe")
        ve = mk("ve")
        se = mk("se")
        cc = mk("cc")

        with ExitStack() as p0:
            sb0 = lambda n, s, d: p0.enter_context(nc.sbuf_tensor(n, s, d, side="right"))
            ps0 = lambda n, s, d: p0.enter_context(nc.psum_tensor(n, s, d))
            XT = [sb0(f"XT{i}", [128, D], F32) for i in range(2)]
            XHB = [sb0(f"XHB{i}", [128, D], BF16) for i in range(2)]
            XHT = [sb0(f"XHT{i}", [128, D], BF16) for i in range(2)]
            CKT = [sb0(f"CKT{i}", [128, 128], BF16) for i in range(2)]
            VAR = [sb0(f"VAR{i}", [128, 1], F32) for i in range(2)]
            RSTD = [sb0(f"RSTD{i}", [128, 1], F32) for i in range(2)]
            BNS = [sb0(f"BNS{i}", [128, 12], F32) for i in range(2)]
            BNA = [sb0(f"BNA{i}", [128, 2], F32) for i in range(2)]
            WKVB = sb0("WKVB", [128, 8 * 128], BF16)
            WKUP = sb0("WKUP", [128, 512], BF16)
            WVUP = sb0("WVUP", [128, 512], BF16)
            WQB = sb0("WQB", [128, 8 * 512], BF16)
            WQR = sb0("WQR", [128, 8 * 32], BF16)
            WKR = sb0("WKR", [128, 32], BF16)
            sbr = lambda n, s, d: p0.enter_context(nc.sbuf_tensor(n, s, d, side="right"))
            COS = sbr("COS", [DR // 2, T], BF16)
            SIN = sbr("SIN", [DR // 2, T], BF16)
            SROPE = sbr("SROPE", [16, 512], BF16)
            RT1 = sbr("RT1", [DR // 2, 128], BF16)
            RT2 = sbr("RT2", [DR // 2, 128], BF16)
            pT = [ps0(f"pT{i}", [128, 1024], BF16) for i in range(2)]
            pCK = ps0("pCK", [128, 128], F32)
            pK = ps0("pK", [128, 512], F32)
            pV = ps0("pV", [128, 512], F32)
            pQa = ps0("pQa", [128, 128], F32)
            pQb = ps0("pQb", [128, 128], F32)
            pROPE = ps0("pROPE", [16, 512], F32)

            t_xhb = tap("xhb", [128, D], BF16)
            t_kt0 = tap("kt0", [128, T], BF16)
            t_qt0 = tap("qt0", [128, T], BF16)
            t_kh = tap("kh", [112, T], BF16)
            t_qh = tap("qh", [112, T], BF16)
            t_vs = tap("vs", [128, NT * NHC * 65], BF16)

            def p0_walk(E):
                E.op("vector", lambda v: v.memset(ONE1[:], 1.0))
                E.op("vector", lambda v: v.memset(C_SCALE[:], SCALE))
                E.op("vector", lambda v: v.memset(C_MEXP[:], -MEXP))
                E.op("vector", lambda v: v.memset(C_HALF[:], -0.5))
                E.op("vector", lambda v: v.memset(C_S127[:], 1.0 / 127.0))
                E.op("vector", lambda v: v.memset(C_S127SQ[:], S127SQ))
                E.op("vector", lambda v: v.memset(KH[0:32, :], 0.0))
                E.op("vector", lambda v: v.memset(KH[32:64, :], 0.0))
                E.op("vector", lambda v: v.memset(KH[64:96, :], 0.0))
                E.op("vector", lambda v: v.memset(QH[0:32, :], 0.0))
                E.op("vector", lambda v: v.memset(QH[32:64, :], 0.0))
                E.op("vector", lambda v: v.memset(QH[64:96, :], 0.0), inc=ve)

                E.dma("gpsimd", lambda g: g.dma_start(out=COS[:], in_=cosT), inc=dG)
                E.dma("gpsimd", lambda g: g.dma_start(out=SIN[:], in_=sinT), inc=dG)
                E.dma("sync", lambda s: s.dma_start(out=EYEF[:], in_=eye), inc=dL)
                E.dma("gpsimd", lambda g: g.dma_start(out=EYEB[:], in_=eye), inc=dG)
                E.dma("gpsimd", lambda g: g.dma_start(out=MSK[:], in_=masks), inc=dG)
                E.dma("gpsimd", lambda g: g.dma_start(
                    out=WKVB[:].rearrange("p (c j) -> p c j", j=128),
                    in_=wkv.rearrange("(c p) j -> p c j", p=128)), inc=dG)
                E.dma("gpsimd", lambda g: g.dma_start(out=WKUP[:], in_=wkup), inc=dG)
                E.dma("gpsimd", lambda g: g.dma_start(out=WVUP[:], in_=wvup), inc=dG)
                E.dma("gpsimd", lambda g: g.dma_start(
                    out=WQB[:].rearrange("p (c j) -> p c j", j=512),
                    in_=wq.rearrange("(c p) j -> p c j", p=128)), inc=dG)
                E.dma("gpsimd", lambda g: g.dma_start(
                    out=WQR[:].rearrange("p (c j) -> p c j", j=32),
                    in_=wqr.rearrange("(c p) j -> p c j", p=128)), inc=dG)
                E.dma("gpsimd", lambda g: g.dma_start(out=WKR[:], in_=wkr), inc=dG)
                dl_w = dG.v

                apply_done = [0, 0]
                pe_T = [0, 0]
                xht_evac = [0, 0]
                se_rstd = [0, 0]
                q_evac = [0, 0]
                xhb_tap = 0

                def stage_ln(t):
                    buf = t % 2
                    E.wait("sync", ve, apply_done[buf])
                    E.dma("sync", lambda s, _t=t, _b=buf: s.dma_start(
                        out=XT[_b][:], in_=x_seq[128 * _t:128 * _t + 128, :]), inc=dX[buf])
                    dl_x = dX[buf].v
                    E.wait("vector", dX[buf], dl_x)
                    if t == 0:
                        E.wait("vector", dG, dl_w)
                    E.op("vector", lambda v, _b=buf: v.bn_stats(out=BNS[_b][:, 0:6], in_=XT[_b][:, 0:512]))
                    E.op("vector", lambda v, _b=buf: v.bn_stats(out=BNS[_b][:, 6:12], in_=XT[_b][:, 512:1024]), inc=ve)
                    E.wait("vector", ve)
                    E.op("vector", lambda v, _b=buf: v.bn_aggr(out=BNA[_b][:], in_=BNS[_b][:]), inc=ve)
                    E.wait("vector", ve)
                    E.op("vector", lambda v, _b=buf: v.tensor_scalar_add(
                        out=VAR[_b][:], in0=BNA[_b][:, 1:2], scalar1=1e-5), inc=ve)
                    E.wait("scalar", ve)
                    E.op("scalar", lambda s, _b=buf: s.activation(out=RSTD[_b][:], in_=VAR[_b][:], func=AF.Ln), inc=se)
                    E.wait("scalar", se)
                    E.op("scalar", lambda s, _b=buf: s.activation(out=RSTD[_b][:], in_=RSTD[_b][:],
                                                                  func=AF.Exp, scale=C_HALF[:]), inc=se)
                    se_rstd[buf] = se.v

                def stage_proj(t):
                    nonlocal xhb_tap
                    buf = t % 2
                    E.wait("vector", se, se_rstd[buf])
                    E.wait("vector", pe, pe_T[buf])
                    if xhb_tap:
                        E.wait("vector", dS, xhb_tap)
                    E.op("vector", lambda v, _b=buf: v.tensor_scalar(
                        out=XHB[_b][:], in0=XT[_b][:], scalar1=BNA[_b][:, 0:1], scalar2=RSTD[_b][:],
                        op0=ALU.subtract, op1=ALU.mult), inc=ve)
                    apply_done[buf] = ve.v
                    ve_xhb = ve.v
                    if t_xhb is not None and t == 0:
                        E.wait("sync", ve, ve_xhb)
                        E.dma("sync", lambda s: s.dma_start(out=t_xhb, in_=XHB[0][:]), inc=dS)
                        xhb_tap = dS.v
                    E.wait("tensor", ve, ve_xhb)
                    E.wait("tensor", ve, xht_evac[buf])
                    for c in range(8):
                        E.op("tensor", lambda tt, _c=c, _b=buf: tt.transpose(
                            out=pT[_b][:, 128 * _c:128 * _c + 128],
                            in_=XHB[_b][:, 128 * _c:128 * _c + 128],
                            identity=EYEB[:]), inc=(pe if c == 7 else None))
                    pe_T[buf] = pe.v
                    E.wait("vector", pe, pe_T[buf])
                    E.op("vector", lambda v, _b=buf: v.tensor_copy(out=XHT[_b][:], in_=pT[_b][:]), inc=ve)
                    xht_evac[buf] = ve.v
                    ve_xht = ve.v
                    E.wait("tensor", ve, ve_xht)
                    for c in range(8):
                        E.op("tensor", lambda tt, _c=c, _b=buf: tt.matmul(
                            out=pCK[:], lhsT=WKVB[:, 128 * _c:128 * _c + 128],
                            rhs=XHT[_b][:, 128 * _c:128 * _c + 128],
                            start=(_c == 0), stop=(_c == 7)), inc=(pe if c == 7 else None))
                    E.wait("vector", pe)
                    E.op("vector", lambda v, _b=buf: v.tensor_copy(out=CKT[_b][:], in_=pCK[:]), inc=ve)
                    ve_ck = ve.v
                    E.wait("tensor", ve, ve_ck)
                    for c in range(4):
                        E.op("tensor", lambda tt, _c=c, _b=buf: tt.matmul(
                            out=pK[:, 128 * _c:128 * _c + 128],
                            lhsT=WKUP[:, 128 * _c:128 * _c + 128], rhs=CKT[_b][:],
                            start=True, stop=True), inc=(pe if c == 3 else None))
                    pe_k = pe.v
                    E.op("tensor", lambda tt, _b=buf: tt.matmul(out=pROPE[:, 0:128], lhsT=WKR[:, 0:16],
                                                                rhs=CKT[_b][:], start=True, stop=True))
                    E.op("tensor", lambda tt, _b=buf: tt.matmul(out=pROPE[:, 128:256], lhsT=WKR[:, 16:32],
                                                                rhs=CKT[_b][:], start=True, stop=True), inc=pe)
                    E.op("tensor", lambda tt, _b=buf: tt.matmul(out=pV[:], lhsT=CKT[_b][:], rhs=WVUP[:],
                                                                start=True, stop=True), inc=pe)
                    pe_v = pe.v
                    E.wait("vector", pe, pe_k)
                    for c in range(4):
                        E.op("vector", lambda v, _c=c, _t=t: v.tensor_copy(
                            out=KT[_c][:, 128 * _t:128 * _t + 128],
                            in_=pK[:, 128 * _c:128 * _c + 128]))
                    E.wait("vector", pe, pe_v)
                    E.op("vector", lambda v, _t=t: v.tensor_copy(
                        out=VS[:].rearrange("p (u h c) -> p u h c", h=NHC, c=65)[:, _t, :, 0:64],
                        in_=pV[:].rearrange("p (h c) -> p h c", c=64)))
                    E.op("vector", lambda v, _t=t: v.memset(
                        VS[:].rearrange("p (u h c) -> p u h c", h=NHC, c=65)[:, _t, :, 64:65],
                        1.0), inc=ve)
                    for c in range(4):
                        pq = pQa if c % 2 == 0 else pQb
                        E.wait("tensor", ve, q_evac[c % 2])
                        for kk in range(8):
                            E.op("tensor", lambda tt, _c=c, _k=kk, _pq=pq, _b=buf: tt.matmul(
                                out=_pq[:],
                                lhsT=WQB[:, 512 * _k + 128 * _c:512 * _k + 128 * _c + 128],
                                rhs=XHT[_b][:, 128 * _k:128 * _k + 128],
                                start=(_k == 0), stop=(_k == 7)), inc=(pe if kk == 7 else None))
                        E.wait("vector", pe)
                        E.op("vector", lambda v, _c=c, _t=t, _pq=pq: v.tensor_copy(
                            out=QT[_c][:, 128 * _t:128 * _t + 128], in_=_pq[:]), inc=ve)
                        q_evac[c % 2] = ve.v
                    for kk in range(8):
                        E.op("tensor", lambda tt, _k=kk, _b=buf: tt.matmul(
                            out=pROPE[:, 256:384], lhsT=WQR[:, 32 * _k:32 * _k + 16],
                            rhs=XHT[_b][:, 128 * _k:128 * _k + 128],
                            start=(_k == 0), stop=(_k == 7)))
                    for kk in range(8):
                        E.op("tensor", lambda tt, _k=kk, _b=buf: tt.matmul(
                            out=pROPE[:, 384:512], lhsT=WQR[:, 32 * _k + 16:32 * _k + 32],
                            rhs=XHT[_b][:, 128 * _k:128 * _k + 128],
                            start=(_k == 0), stop=(_k == 7)), inc=(pe if kk == 7 else None))
                    pe_rope = pe.v
                    E.wait("vector", pe, pe_rope)
                    E.op("vector", lambda v: v.tensor_copy(out=SROPE[:], in_=pROPE[:]), inc=ve)
                    E.wait("vector", ve)
                    cs = slice(128 * t, 128 * t + 128)
                    for (x1o, x2o, OUTT) in ((0, 128, KH), (256, 384, QH)):
                        E.op("vector", lambda v, _a=x1o, _c=cs: v.tensor_mul(
                            out=RT1[:], in0=SROPE[:, _a:_a + 128], in1=COS[:, _c]))
                        E.op("vector", lambda v, _b2=x2o, _c=cs: v.tensor_mul(
                            out=RT2[:], in0=SROPE[:, _b2:_b2 + 128], in1=SIN[:, _c]), inc=ve)
                        E.wait("vector", ve)
                        E.op("vector", lambda v, _O=OUTT, _c=cs: v.tensor_sub(
                            out=_O[0:16, _c], in0=RT1[:], in1=RT2[:]), inc=ve)
                        E.wait("vector", ve)
                        E.op("vector", lambda v, _b2=x2o, _c=cs: v.tensor_mul(
                            out=RT1[:], in0=SROPE[:, _b2:_b2 + 128], in1=COS[:, _c]))
                        E.op("vector", lambda v, _a=x1o, _c=cs: v.tensor_mul(
                            out=RT2[:], in0=SROPE[:, _a:_a + 128], in1=SIN[:, _c]), inc=ve)
                        E.wait("vector", ve)
                        E.op("vector", lambda v, _O=OUTT, _c=cs: v.tensor_add(
                            out=_O[96:112, _c], in0=RT1[:], in1=RT2[:]), inc=ve)
                        E.wait("vector", ve)

                for t in range(NT + 1):
                    if t < NT:
                        stage_ln(t)
                    if t >= 1:
                        stage_proj(t - 1)

                if t_kt0 is not None:
                    E.wait("sync", ve)
                    E.wait("sync", pe)
                    E.dma("sync", lambda s: s.dma_start(out=t_kt0, in_=KT[0][:]), inc=dS)
                    E.dma("sync", lambda s: s.dma_start(out=t_qt0, in_=QT[0][:]), inc=dS)
                    if t_kh is not None:
                        E.dma("sync", lambda s: s.dma_start(out=t_kh, in_=KH[:]), inc=dS)
                    if t_qh is not None:
                        E.dma("sync", lambda s: s.dma_start(out=t_qh, in_=QH[:]), inc=dS)
                    E.dma("sync", lambda s: s.dma_start(out=t_vs, in_=VS[:]), inc=dS)
                    E.wait("sync", dS)

                for eng in ENGS:
                    E.wait_all(eng)

            run_block(nc, sems, p0_walk)

        if phases == 1:
            return nc, tap_out

        with ExitStack() as p1:
            sb1 = lambda n, s, d: p1.enter_context(nc.sbuf_tensor(n, s, d, side="right"))
            ps1 = lambda n, s, d: p1.enter_context(nc.psum_tensor(n, s, d))
            KH2 = [KH, sb1("KHb", [112, T], BF16)]
            QH2 = [QH, sb1("QHb", [112, T], BF16)]
            PB = [sb1(f"PB{i}", [128, 3 * 512], BF16) for i in range(2)]
            RECB = sb1("RECB", [1, 512], BF16)
            BCS = sb1("BCS", [64, 512], F32)
            S_ = [ps1("S0", [128, 3 * 512], F32), ps1("S1", [128, 3 * 512], F32)]
            AVp = ps1("AVp", [128, 1024], F32)

            t_col0 = tap("col0", [128, T], BF16)
            t_col1 = tap("col1", [128, T], BF16)

            def p1_walk(E):
                E.op("vector", lambda v: v.tensor_copy(out=KH2[1][0:32, :], in_=KH2[0][0:32, :]))
                E.op("vector", lambda v: v.tensor_copy(out=KH2[1][96:112, :], in_=KH2[0][96:112, :]))
                E.op("vector", lambda v: v.tensor_copy(out=QH2[1][0:32, :], in_=QH2[0][0:32, :]))
                E.op("vector", lambda v: v.tensor_copy(out=QH2[1][96:112, :], in_=QH2[0][96:112, :]), inc=ve)

                div_done = [0, 0]
                last_exp = [0, 0]
                asm_pe_gate = [0, 0]
                asm_done = [0, 0]
                qt_par = 0

                def assemble(h):
                    hb = h % 2
                    c, r = h // 2, 64 * (h % 2)
                    E.wait("vector", pe, asm_pe_gate[hb])
                    for (DST, SRC) in ((KH2[hb], KT[c]), (QH2[hb], QT[c])):
                        E.op("vector", lambda v, _D=DST, _S=SRC, _r=r: v.tensor_copy(
                            out=_D[32:64, :], in_=_S[_r:_r + 32, :]))
                        E.op("vector", lambda v, _D=DST, _S=SRC, _r=r: v.tensor_copy(
                            out=_D[64:96, :], in_=_S[_r + 32:_r + 64, :]))
                    E.op("vector", lambda v: v.engine_nop(), inc=ve)
                    asm_done[hb] = ve.v

                assemble(0)
                for h in range(NHC):
                    hb = h % 2
                    KHc, QHc = KH2[hb], QH2[hb]
                    if h + 1 < NHC:
                        assemble(h + 1)
                    for qt in range(NQT):
                        ah = qt_par = (h * NQT + qt) % 2
                        av0 = 512 * ah
                        nk = 4 * qt + 4
                        groups = [list(range(g, min(g + 3, nk))) for g in range(0, nk, 3)]
                        ng = len(groups)
                        pe_sc = {}
                        av_cnt = 0

                        def emit_scores(gi):
                            grp = groups[gi]
                            par = gi % 2
                            E.wait("tensor", ve, asm_done[hb])
                            E.wait("tensor", se, last_exp[par])
                            for j, kt in enumerate(grp):
                                diag = kt >= 4 * qt
                                last_j = (j == len(grp) - 1)
                                E.op("tensor", lambda tt, _kt=kt, _qt=qt, _p=par, _j=j, _d=diag, _K=KHc, _Q=QHc: tt.matmul(
                                    out=S_[_p][:, 512 * _j:512 * _j + 512],
                                    lhsT=_K[:, 128 * _kt:128 * _kt + 128],
                                    rhs=_Q[:, 512 * _qt:512 * _qt + 512],
                                    start=True, stop=(not _d)),
                                    inc=(pe if (last_j and not diag) else None))
                                if diag:
                                    dlt = kt - 4 * qt
                                    E.op("tensor", lambda tt, _p=par, _j=j, _dl=dlt: tt.matmul(
                                        out=S_[_p][:, 512 * _j:512 * _j + 512],
                                        lhsT=EYEB[:], rhs=MSK[:, 512 * _dl:512 * _dl + 512],
                                        start=False, stop=True),
                                        inc=(pe if last_j else None))
                            pe_sc[gi] = pe.v

                        def emit_exp(gi):
                            grp = groups[gi]
                            par = gi % 2
                            E.wait("scalar", pe, pe_sc[gi])
                            E.op("scalar", lambda s, _p=par, _n=len(grp): s.activation(
                                out=PB[_p][:, 0:512 * _n], in_=S_[_p][:, 0:512 * _n],
                                func=AF.Exp, scale=C_SCALE[:], bias=C_MEXP[:]), inc=se)
                            last_exp[par] = se.v

                        def emit_av(gi):
                            nonlocal av_cnt
                            grp = groups[gi]
                            par = gi % 2
                            E.wait("tensor", se, last_exp[par])
                            if gi == 0:
                                E.wait("tensor", ve, div_done[ah])
                            for j, kt in enumerate(grp):
                                first = (av_cnt == 0)
                                last = (av_cnt == nk - 1)
                                av_cnt += 1
                                E.op("tensor", lambda tt, _kt=kt, _h=h, _p=par, _j=j, _f=first, _l=last, _a=av0: tt.matmul(
                                    out=AVp[0:65, _a:_a + 512],
                                    lhsT=VS[:, 520 * _kt + 65 * _h:520 * _kt + 65 * _h + 65],
                                    rhs=PB[_p][:, 512 * _j:512 * _j + 512],
                                    start=_f, stop=_l), inc=(pe if last else None))

                        emit_scores(0)
                        emit_exp(0)
                        for gi in range(1, ng):
                            emit_scores(gi)
                            emit_av(gi - 1)
                            emit_exp(gi)
                        emit_av(ng - 1)
                        pe_av = pe.v
                        E.wait("vector", pe, pe_av)

                        def _recip_bf16(v, a):
                            with nc.allow_low_precision(reason="bf16 store of fp32 reciprocal"):
                                return v.reciprocal(out=RECB[:], in_=AVp[64:65, a:a + 512])
                        E.op("vector", lambda v, _a=av0: _recip_bf16(v, _a), inc=ve)
                        E.wait("tensor", ve)
                        E.op("tensor", lambda tt, _a=av0: tt.matmul(
                            out=AVp[64:128, _a:_a + 512], lhsT=ONE1[:], rhs=RECB[:],
                            start=True, stop=True), inc=pe)
                        E.wait("vector", pe)
                        E.op("vector", lambda v, _a=av0: v.tensor_copy(
                            out=BCS[:], in_=AVp[64:128, _a:_a + 512]), inc=ve)
                        E.wait("vector", ve)
                        E.op("vector", lambda v, _h=h, _qt=qt, _a=av0: v.scalar_tensor_tensor(
                            out=COL[_h // 2][64 * (_h % 2):64 * (_h % 2) + 64,
                                             512 * _qt:512 * _qt + 512],
                            in0=AVp[0:64, _a:_a + 512], scalar=1.0, in1=BCS[:],
                            op0=ALU.mult, op1=ALU.mult), inc=ve)
                        div_done[ah] = ve.v
                    asm_pe_gate[hb] = pe.v

                if t_col0 is not None:
                    E.wait("sync", ve)
                    E.dma("sync", lambda s: s.dma_start(out=t_col0, in_=COL[0][:]), inc=dS)
                    if t_col1 is not None:
                        E.dma("sync", lambda s: s.dma_start(out=t_col1, in_=COL[1][:]), inc=dS)
                    E.wait("sync", dS)

                for eng in ENGS:
                    E.wait_all(eng)

            run_block(nc, sems, p1_walk)

        if phases == 2:
            return nc, tap_out

        S01.close()

        S23 = ExitStack()
        X2T = [S23.enter_context(nc.sbuf_tensor(f"X2T{i}", [128, OWN], BF16, side="right"))
               for i in range(8)]
        with ExitStack() as p2:
            sb2 = lambda n, s, d: p2.enter_context(nc.sbuf_tensor(n, s, d, side="right"))
            ps2 = lambda n, s, d: p2.enter_context(nc.psum_tensor(n, s, d))
            WOB = sb2("WOB", [128, 4 * 1024], BF16)
            PRT = [sb2(f"PRT{i}", [128, D], BF16) for i in range(2)]
            XO = [sb2(f"XO{i}", [128, D], F32) for i in range(2)]
            AT = [sb2(f"AT{i}", [128, D], BF16) for i in range(2)]
            R1 = sb2("R1", [128, D], F32)
            X2B = sb2("X2B", [128, D], BF16)
            VAR2 = sb2("VAR2", [128, 1], F32)
            RSTD2 = sb2("RSTD2", [128, 1], F32)
            BNS2 = sb2("BNS2", [128, 12], F32)
            BNA2 = sb2("BNA2", [128, 2], F32)
            pWO = [ps2(f"pWO{i}", [128, 1024], F32) for i in range(2)]
            pT2 = ps2("pT2", [128, 1024], BF16)

            t_r1 = tap("r1", [OWN, D], F32)
            t_x2 = tap("x2", [128, OWN], BF16)

            def p2_walk2(E):
                p2_state = {}
                for fc in range(4):
                    E.dma("gpsimd", lambda g, _f=fc: g.dma_start(
                        out=WOB[:, 1024 * _f:1024 * _f + 1024],
                        in_=wo[128 * _f:128 * _f + 128, :]), inc=dG)
                dl_wo = dG.v
                prt_free = [0, 0]
                evac_done = [0, 0]
                for t in range(NT):
                    buf = t % 2
                    E.wait("tensor", dG, dl_wo)
                    E.wait("tensor", ve, evac_done[buf])
                    for half in range(2):
                        for fc in range(4):
                            E.op("tensor", lambda t_, _f=fc, _t=t, _h=half, _b=buf: t_.matmul(
                                out=pWO[_b][:, 512 * _h:512 * _h + 512],
                                lhsT=COL[_f][:, 128 * _t:128 * _t + 128],
                                rhs=WOB[:, 1024 * _f + 512 * _h:1024 * _f + 512 * _h + 512],
                                start=(_f == 0), stop=(_f == 3)),
                                inc=(pe if (half == 1 and fc == 3) else None))
                    pe_wo = pe.v
                    E.wait("vector", pe, pe_wo)
                    E.wait("vector", dP[buf], prt_free[buf])
                    E.op("vector", lambda v, _b=buf: v.tensor_copy(out=PRT[_b][:], in_=pWO[_b][:]), inc=ve)
                    evac_done[buf] = ve.v
                    E.wait("sync", ve, evac_done[buf])
                    E.dma("sync", lambda s, _t=t, _b=buf: s.dma_start(
                        out=snd.ap()[128 * _t:128 * _t + 128, :], in_=PRT[_b][:]), inc=dP[buf])
                    prt_free[buf] = dP[buf].v
                E.wait("gpsimd", dP[0])
                E.wait("gpsimd", dP[1])
                E.op("gpsimd", lambda g: g.collective_compute(
                    "ReduceScatter", ALU.add,
                    replica_groups=[[0, 1], [2, 3], [4, 5], [6, 7]],
                    ins=[snd.ap().opt()], outs=[rcv.ap().opt()]), inc=cc)

                r1_done = 0
                xo_free = [0, 0]
                at_free = [0, 0]
                pe_t2 = 0
                for t in range(16):
                    buf = t % 2
                    E.wait("sync", ve, xo_free[buf])
                    E.dma("sync", lambda s, _t=t, _b=buf: s.dma_start(
                        out=XO[_b][:], in_=x_own[128 * _t:128 * _t + 128, :]), inc=dX[buf])
                    E.wait("sync", cc)
                    E.wait("sync", ve, at_free[buf])
                    E.dma("sync", lambda s, _t=t, _b=buf: s.dma_start(
                        out=AT[_b][:], in_=rcv.ap()[128 * _t:128 * _t + 128, :]), inc=dA[buf])
                    E.wait("vector", dX[buf])
                    E.wait("vector", dA[buf])
                    E.wait("vector", dS, r1_done)
                    E.op("vector", lambda v, _b=buf: v.tensor_add(out=R1[:], in0=XO[_b][:], in1=AT[_b][:]), inc=ve)
                    xo_free[buf] = ve.v
                    at_free[buf] = ve.v
                    E.wait("sync", ve)
                    E.dma("sync", lambda s, _t=t: s.dma_start(
                        out=r1spill.ap()[128 * _t:128 * _t + 128, :], in_=R1[:]), inc=dS)
                    r1_done = dS.v
                    if t_r1 is not None:
                        E.dma("sync", lambda s, _t=t: s.dma_start(
                            out=t_r1[128 * _t:128 * _t + 128, :], in_=R1[:]), inc=dS)
                        r1_done = dS.v
                    E.wait("vector", ve)
                    E.op("vector", lambda v: v.bn_stats(out=BNS2[:, 0:6], in_=R1[:, 0:512]))
                    E.op("vector", lambda v: v.bn_stats(out=BNS2[:, 6:12], in_=R1[:, 512:1024]), inc=ve)
                    E.wait("vector", ve)
                    E.op("vector", lambda v: v.bn_aggr(out=BNA2[:], in_=BNS2[:]), inc=ve)
                    E.wait("vector", ve)
                    E.op("vector", lambda v: v.tensor_scalar_add(out=VAR2[:], in0=BNA2[:, 1:2], scalar1=1e-5), inc=ve)
                    E.wait("scalar", ve)
                    E.op("scalar", lambda s: s.activation(out=RSTD2[:], in_=VAR2[:], func=AF.Ln), inc=se)
                    E.wait("scalar", se)
                    E.op("scalar", lambda s: s.activation(out=RSTD2[:], in_=RSTD2[:], func=AF.Exp, scale=C_HALF[:]), inc=se)
                    E.wait("vector", se)
                    E.wait("vector", pe, pe_t2)
                    E.op("vector", lambda v: v.tensor_scalar(
                        out=X2B[:], in0=R1[:], scalar1=BNA2[:, 0:1], scalar2=RSTD2[:],
                        op0=ALU.subtract, op1=ALU.mult), inc=ve)
                    ve_x2 = ve.v
                    E.wait("tensor", ve, ve_x2)
                    for c in range(8):
                        E.op("tensor", lambda t_, _c=c: t_.transpose(
                            out=pT2[:, 128 * _c:128 * _c + 128],
                            in_=X2B[:, 128 * _c:128 * _c + 128],
                            identity=EYEB[:]), inc=(pe if c == 7 else None))
                    pe_t2 = pe.v
                    E.wait("vector", pe, pe_t2)
                    for c in range(8):
                        E.op("vector", lambda v, _c=c, _t=t: v.tensor_copy(
                            out=X2T[_c][:, 128 * _t:128 * _t + 128],
                            in_=pT2[:, 128 * _c:128 * _c + 128]),
                            inc=(ve if c == 7 else None))

                if t_x2 is not None:
                    E.wait("sync", ve)
                    E.dma("sync", lambda s: s.dma_start(out=t_x2, in_=X2T[0][:]), inc=dS)
                    E.wait("sync", dS)

                for eng in ENGS:
                    E.wait_all(eng)

            run_block(nc, sems, p2_walk2)

        if phases == 3:
            return nc, tap_out

        with ExitStack() as p3:
            sb3 = lambda n, s, d: p3.enter_context(nc.sbuf_tensor(n, s, d, side="right"))
            ps3 = lambda n, s, d: p3.enter_context(nc.psum_tensor(n, s, d))
            HB = p3.enter_context(nc.sbuf_tensor("HB", [128, 24 * OWN], BF16, side="left"))

            p3a = ExitStack()
            WGU = [p3a.enter_context(nc.sbuf_tensor(f"WGU{i}", [128, 2 * 1024], BF16, side="right")) for i in range(2)]
            GS = p3a.enter_context(nc.sbuf_tensor("GS", [128, 512], BF16, side="right"))
            US = p3a.enter_context(nc.sbuf_tensor("US", [128, 512], BF16, side="right"))
            pG = p3a.enter_context(nc.psum_tensor("pG", [128, 512], F32))
            pU = p3a.enter_context(nc.psum_tensor("pU", [128, 512], F32))

            def p3a_walk(E):
                wgu_free = [0, 0]
                silu_done = 0
                umul_done = 0
                for hc in range(24):
                    wb = hc % 2
                    E.wait("gpsimd", pe, wgu_free[wb])
                    E.dma("gpsimd", lambda g, _h=hc, _b=wb: g.dma_start(
                        out=WGU[_b][:, 0:1024].rearrange("p (c j) -> p c j", j=128),
                        in_=wgT.rearrange("(c p) j -> p c j", p=128)[:, :, 128 * _h:128 * _h + 128]), inc=dW[wb])
                    E.dma("gpsimd", lambda g, _h=hc, _b=wb: g.dma_start(
                        out=WGU[_b][:, 1024:2048].rearrange("p (c j) -> p c j", j=128),
                        in_=wuT.rearrange("(c p) j -> p c j", p=128)[:, :, 128 * _h:128 * _h + 128]), inc=dW[wb])
                    dl_gu = dW[wb].v
                    for tt in range(4):
                        E.wait("tensor", dW[wb], dl_gu)
                        E.wait("tensor", se, silu_done)
                        for c in range(8):
                            E.op("tensor", lambda t_, _c=c, _t=tt, _b=wb: t_.matmul(
                                out=pG[:], lhsT=WGU[_b][:, 128 * _c:128 * _c + 128],
                                rhs=X2T[_c][:, 512 * _t:512 * _t + 512],
                                start=(_c == 0), stop=(_c == 7)), inc=(pe if c == 7 else None))
                        pe_g = pe.v
                        E.wait("tensor", ve, umul_done)
                        for c in range(8):
                            E.op("tensor", lambda t_, _c=c, _t=tt, _b=wb: t_.matmul(
                                out=pU[:], lhsT=WGU[_b][:, 1024 + 128 * _c:1024 + 128 * _c + 128],
                                rhs=X2T[_c][:, 512 * _t:512 * _t + 512],
                                start=(_c == 0), stop=(_c == 7)), inc=(pe if c == 7 else None))
                        pe_u = pe.v
                        if tt == 3:
                            wgu_free[wb] = pe.v
                        E.wait("scalar", pe, pe_g)
                        E.wait("scalar", ve, umul_done)
                        E.op("scalar", lambda s: s.activation(out=GS[:], in_=pG[:], func=AF.Silu,
                                                              scale=C_S127[:]), inc=se)
                        silu_done = se.v
                        E.wait("vector", pe, pe_u)
                        E.op("vector", lambda v: v.tensor_copy(out=US[:], in_=pU[:]), inc=ve)
                        E.wait("vector", se, silu_done)
                        E.wait("vector", ve)
                        E.op("vector", lambda v, _h=hc, _t=tt: v.tensor_mul(
                            out=HB[:, OWN * _h + 512 * _t:OWN * _h + 512 * _t + 512],
                            in0=GS[:], in1=US[:]), inc=ve)
                        umul_done = ve.v
                for eng in ENGS:
                    E.wait_all(eng)

            run_block(nc, sems, p3a_walk)
            p3a.close()
            S23.close()

            WDA = sb3("WDA", [128, 24 * 1024], BF16)
            sbl = lambda n, s, d: p3.enter_context(nc.sbuf_tensor(n, s, d, side="left"))
            FDT = sbl("FDT", [128, 4096], F32)
            R1T = [sbl(f"R1T{i}", [128, D], F32) for i in range(2)]
            OStg = sbl("OStg", [128, D], F32)
            pD = ps3("pD", [128, 3072], F32)
            pR = ps3("pR", [128, 1024], F32)

            def p3b_walk(E):
                for hc in range(24):
                    E.dma("gpsimd", lambda g, _h=hc: g.dma_start(
                        out=WDA[:, 1024 * _h:1024 * _h + 1024],
                        in_=wdT[128 * _h:128 * _h + 128, :]), inc=dG)
                dl_wd = dG.v

                fdt_free = 0
                r1t_free = [0, 0]
                ost_free = 0
                stt_done = 0
                for tt in range(4):
                    E.wait("tensor", dG, dl_wd)
                    E.wait("tensor", se, fdt_free)
                    E.wait("tensor", ve, stt_done)
                    for hc in range(24):
                        for dc in range(8):
                            dst = pD[:, 512 * dc:512 * dc + 512] if dc < 6 else \
                                  pR[:, 512 * (dc - 6):512 * (dc - 6) + 512]
                            E.op("tensor", lambda t_, _h=hc, _d=dc, _t=tt, _dst=dst: t_.matmul(
                                out=_dst,
                                lhsT=WDA[:, 1024 * _h + 128 * _d:1024 * _h + 128 * _d + 128],
                                rhs=HB[:, OWN * _h + 512 * _t:OWN * _h + 512 * _t + 512],
                                start=(_h == 0), stop=(_h == 23)),
                                inc=(pe if (hc == 23 and dc == 7) else None))
                    pe_d = pe.v
                    E.wait("scalar", pe, pe_d)
                    for dc in range(8):
                        src = pD[:, 512 * dc:512 * dc + 512] if dc < 6 else \
                              pR[:, 512 * (dc - 6):512 * (dc - 6) + 512]
                        E.op("scalar", lambda s, _d=dc, _src=src: s.activation(
                            out=FDT[:, 512 * _d:512 * _d + 512], in_=_src,
                            func=AF.Copy, scale=C_S127SQ[:]),
                            inc=(se if dc == 7 else None))
                    fdt_free = se.v
                    for s4 in range(4):
                        sb_ = s4 % 2
                        E.wait("sync", ve, r1t_free[sb_])
                        E.dma("sync", lambda s, _t=tt, _s=s4, _b=sb_: s.dma_start(
                            out=R1T[_b][:],
                            in_=r1spill.ap()[512 * _t + 128 * _s:512 * _t + 128 * _s + 128, :]),
                            inc=dX[sb_])
                        dl_r1 = (sb_, dX[sb_].v)
                        E.wait("tensor", se, fdt_free)
                        if s4 >= 2:
                            E.wait("tensor", ve)
                        for dc in range(8):
                            E.op("tensor", lambda t_, _d=dc, _s=s4: t_.transpose(
                                out=pD[:, 1024 * (_s % 2) + 128 * _d:1024 * (_s % 2) + 128 * _d + 128],
                                in_=FDT[:, 512 * _d + 128 * _s:512 * _d + 128 * _s + 128],
                                identity=EYEF[:]), inc=(pe if dc == 7 else None))
                        pe_tr = pe.v
                        E.wait("vector", pe, pe_tr)
                        E.wait("vector", dX[dl_r1[0]], dl_r1[1])
                        E.wait("vector", dS, ost_free)
                        E.op("vector", lambda v, _s=s4, _b=sb_: v.scalar_tensor_tensor(
                            out=OStg[:], in0=pD[:, 1024 * (_s % 2):1024 * (_s % 2) + 1024],
                            scalar=1.0, in1=R1T[_b][:],
                            op0=ALU.mult, op1=ALU.add), inc=ve)
                        ve_o = ve.v
                        r1t_free[sb_] = ve_o
                        E.wait("sync", ve, ve_o)
                        E.dma("sync", lambda s, _t=tt, _s=s4: s.dma_start(
                            out=y[512 * _t + 128 * _s:512 * _t + 128 * _s + 128, :],
                            in_=OStg[:]), inc=dS)
                        ost_free = dS.v
                    stt_done = ve.v

                for eng in ENGS:
                    E.wait_all(eng)

            run_block(nc, sems, p3b_walk)

    return nc, tap_out


def _host_inputs(core, inputs):
    b, p = core // 2, core % 2
    h0 = NHC * p * HD
    x = np.ascontiguousarray(np.asarray(inputs["L0_attn_x"])[b])
    out = dict(
        x_seq=x,
        x_own=np.ascontiguousarray(x[OWN * p:OWN * p + OWN]),
        wkv=np.ascontiguousarray(np.asarray(inputs["L0_attn_W_kv_down"])),
        wkup=np.ascontiguousarray(np.asarray(inputs["L0_attn_W_k_up"])[:, h0:h0 + NHC * HD]),
        wvup=np.ascontiguousarray(np.asarray(inputs["L0_attn_W_v_up"])[:, h0:h0 + NHC * HD]),
        wq=np.ascontiguousarray(np.asarray(inputs["L0_attn_W_q"])[:, h0:h0 + NHC * HD]),
        wqr=np.ascontiguousarray(np.asarray(inputs["L0_attn_W_q_rope"])),
        wkr=np.ascontiguousarray(np.asarray(inputs["L0_attn_W_k_rope"])),
        wo=np.ascontiguousarray(np.asarray(inputs["L0_attn_W_o"])[h0:h0 + NHC * HD, :]),
        wgT=np.ascontiguousarray(np.asarray(inputs["L0_ffn_w_gate"]).T),
        wuT=np.ascontiguousarray(np.asarray(inputs["L0_ffn_w_up"]).T),
        wdT=np.ascontiguousarray(np.asarray(inputs["L0_ffn_w_down"]).T),
    )
    inv_freq = 1.0 / 10000.0 ** (np.arange(0, DR, 2, dtype=np.float32) / DR)
    tpos = np.arange(T, dtype=np.float32)
    freqs = np.outer(tpos, inv_freq).astype(np.float32)
    out["cosT"] = np.ascontiguousarray(np.cos(freqs).T.astype(np.float32))
    out["sinT"] = np.ascontiguousarray(np.sin(freqs).T.astype(np.float32))
    out["eye"] = np.eye(128, dtype=np.float32)
    masks_ = np.zeros((128, 4, 512), dtype=np.float32)
    k_loc = np.arange(128)[:, None]
    q_loc = np.arange(512)[None, :]
    for i, dlt in enumerate((0, 128, 256, 384)):
        masks_[:, i, :] = np.where(k_loc + dlt > q_loc, -1e9, 0.0)
    out["masks"] = np.ascontiguousarray(masks_.reshape(128, 4 * 512))
    return out


def kernel(**inputs):
    nc, _ = build()
    in_maps = [_host_inputs(c, inputs) for c in range(8)]
    res = run_bass_kernel_spmd(nc, in_maps, core_ids=list(range(8)))
    out = np.zeros((4, T, D), dtype=np.float32)
    for c in range(8):
        b, p = c // 2, c % 2
        out[b, OWN * p:OWN * p + OWN] = np.asarray(res.results[c]["y"], dtype=np.float32)
    return out


# version 04
# speedup vs baseline: 1.4484x; 1.0604x over previous
import math
from contextlib import ExitStack

import numpy as np
import concourse.bass as bass
import concourse.mybir as mybir
from concourse.bass_utils import run_bass_kernel_spmd

F32 = mybir.dt.float32
BF16 = mybir.dt.bfloat16
I32 = mybir.dt.int32
AF = mybir.ActivationFunctionType
ALU = mybir.AluOpType
AX = mybir.AxisListType

T = 4096
D = 1024
NHC = 8
HD = 64
DC = 128
DR = 32
FF = 3072
NT = T // 128
OWN = 2048
NQT = T // 512
SCALE = 1.0 / math.sqrt(HD + DR)
MEXP = 24.0
S127SQ = (1.0 / 127.0) ** 2

ENGS = ("sync", "gpsimd", "vector", "scalar", "tensor")


class Sem:
    def __init__(self, ctx, nc, name):
        self.h = ctx.enter_context(nc.semaphore(name))
        self.v = 0
        self.name = name


class Emitter:
    def __init__(self, who, eng, sems):
        self.who = who
        self.eng = eng
        self.sems = sems

    def op(self, engine, fn, inc=None, n=1):
        if self.who == engine:
            instr = fn(self.eng)
            if inc is not None:
                instr.then_inc(inc.h, n)
        if inc is not None:
            inc.v += n
        return inc.v if inc is not None else None

    def dma(self, engine, fn, inc):
        return self.op(engine, fn, inc, 16)

    def wait(self, engine, sem, val=None):
        if self.who == engine:
            self.eng.wait_ge(sem.h, sem.v if val is None else val)

    def wait_all(self, engine):
        for s in self.sems:
            if s.v > 0 and self.who == engine:
                self.eng.wait_ge(s.h, s.v)


def run_block(nc, sems, walk):
    with nc.Block() as blk:
        snap = [s.v for s in sems]
        for engname in ENGS:
            def body(eng, _en=engname):
                for s, v in zip(sems, snap):
                    s.v = v
                walk(Emitter(_en, eng, sems))
            getattr(blk, engname)(body)


def build(debug_taps=(), phases=4):
    nc = bass.Bass("TRN2", target_bir_lowering=False, debug=False)

    x_seq = nc.dram_tensor("x_seq", [T, D], F32, kind="ExternalInput").ap()
    x_own = nc.dram_tensor("x_own", [OWN, D], F32, kind="ExternalInput").ap()
    wkv = nc.dram_tensor("wkv", [D, DC], F32, kind="ExternalInput").ap()
    wkup = nc.dram_tensor("wkup", [DC, NHC * HD], F32, kind="ExternalInput").ap()
    wvup = nc.dram_tensor("wvup", [DC, NHC * HD], F32, kind="ExternalInput").ap()
    wq = nc.dram_tensor("wq", [D, NHC * HD], F32, kind="ExternalInput").ap()
    wqr = nc.dram_tensor("wqr", [D, DR], F32, kind="ExternalInput").ap()
    wkr = nc.dram_tensor("wkr", [DC, DR], F32, kind="ExternalInput").ap()
    wo = nc.dram_tensor("wo", [NHC * HD, D], F32, kind="ExternalInput").ap()
    wgT = nc.dram_tensor("wgT", [D, FF], I32, kind="ExternalInput").ap()
    wuT = nc.dram_tensor("wuT", [D, FF], I32, kind="ExternalInput").ap()
    wdT = nc.dram_tensor("wdT", [FF, D], I32, kind="ExternalInput").ap()
    cosT = nc.dram_tensor("cosT", [DR // 2, T], F32, kind="ExternalInput").ap()
    sinT = nc.dram_tensor("sinT", [DR // 2, T], F32, kind="ExternalInput").ap()
    eye = nc.dram_tensor("eye", [128, 128], F32, kind="ExternalInput").ap()
    masks = nc.dram_tensor("masks", [128, 4 * 512], F32, kind="ExternalInput").ap()

    y = nc.dram_tensor("y", [OWN, D], F32, kind="ExternalOutput").ap()

    snd = nc.dram_tensor("snd", [T, D], BF16)
    rcv = nc.dram_tensor("rcv", [OWN, D], BF16)
    r1spill = nc.dram_tensor("r1spill", [OWN, D], F32)

    tap_out = {}

    def tap(name, shape, dtype):
        if name in debug_taps:
            tap_out[name] = nc.dram_tensor("tap_" + name, shape, dtype,
                                           kind="ExternalOutput").ap()
            return tap_out[name]
        return None

    with ExitStack() as ctx:
        sbt = lambda n, s, d: ctx.enter_context(nc.sbuf_tensor(n, s, d))

        COL = [sbt(f"COL{i}", [128, T], BF16) for i in range(4)]
        S01 = ExitStack()
        s01t = lambda n, s, d: S01.enter_context(nc.sbuf_tensor(n, s, d, side="right"))
        KT = [s01t(f"KT{i}", [128, T], BF16) for i in range(4)]
        QT = [s01t(f"QT{i}", [128, T], BF16) for i in range(4)]
        KH = s01t("KH", [112, T], BF16)
        QH = s01t("QH", [112, T], BF16)
        MSK = s01t("MSK", [128, 4 * 512], BF16)
        VS = s01t("VS", [128, NT * (NHC * 65)], BF16)
        EYEB = sbt("EYEB", [128, 128], BF16)
        EYEF = sbt("EYEF", [128, 128], F32)
        ONE1 = sbt("ONE1", [1, 64], BF16)
        C_SCALE = sbt("C_SCALE", [128, 1], F32)
        C_MEXP = sbt("C_MEXP", [128, 1], F32)
        C_HALF = sbt("C_HALF", [128, 1], F32)
        C_S127 = sbt("C_S127", [128, 1], F32)
        C_S127SQ = sbt("C_S127SQ", [128, 1], F32)

        sems = []

        def mk(name):
            s = Sem(ctx, nc, name)
            sems.append(s)
            return s

        dL = mk("dL")
        dG = mk("dG")
        dS = mk("dS")
        dX = [mk("dXa"), mk("dXb")]
        dA = [mk("dAa"), mk("dAb")]
        dP = [mk("dPa"), mk("dPb")]
        dW = [mk("dWa"), mk("dWb")]
        pe = mk("pe")
        ve = mk("ve")
        se = mk("se")
        cc = mk("cc")
        gp = mk("gp")

        with ExitStack() as p0:
            sb0 = lambda n, s, d: p0.enter_context(nc.sbuf_tensor(n, s, d, side="right"))
            ps0 = lambda n, s, d: p0.enter_context(nc.psum_tensor(n, s, d))
            XT = [sb0(f"XT{i}", [128, D], F32) for i in range(2)]
            XHB = [sb0(f"XHB{i}", [128, D], BF16) for i in range(2)]
            XHT = [sb0(f"XHT{i}", [128, D], BF16) for i in range(2)]
            CKT = [sb0(f"CKT{i}", [128, 128], BF16) for i in range(2)]
            VAR = [sb0(f"VAR{i}", [128, 1], F32) for i in range(2)]
            RSTD = [sb0(f"RSTD{i}", [128, 1], F32) for i in range(2)]
            BNS = [sb0(f"BNS{i}", [128, 12], F32) for i in range(2)]
            BNA = [sb0(f"BNA{i}", [128, 2], F32) for i in range(2)]
            WKVB = sb0("WKVB", [128, 8 * 128], BF16)
            WKUP = sb0("WKUP", [128, 512], BF16)
            WVUP = sb0("WVUP", [128, 512], BF16)
            WQB = sb0("WQB", [128, 8 * 512], BF16)
            WQR = sb0("WQR", [128, 8 * 32], BF16)
            WKR = sb0("WKR", [128, 32], BF16)
            sbr = lambda n, s, d: p0.enter_context(nc.sbuf_tensor(n, s, d, side="right"))
            COS = sbr("COS", [DR // 2, T], BF16)
            SIN = sbr("SIN", [DR // 2, T], BF16)
            SROPE = sbr("SROPE", [16, 512], BF16)
            RT1 = sbr("RT1", [DR // 2, 128], BF16)
            RT2 = sbr("RT2", [DR // 2, 128], BF16)
            pT = [ps0(f"pT{i}", [128, 1024], BF16) for i in range(2)]
            pCK = ps0("pCK", [128, 128], F32)
            pK = ps0("pK", [128, 512], F32)
            pV = ps0("pV", [128, 512], F32)
            pQa = ps0("pQa", [128, 128], F32)
            pQb = ps0("pQb", [128, 128], F32)
            pROPE = ps0("pROPE", [16, 512], F32)

            t_xhb = tap("xhb", [128, D], BF16)
            t_kt0 = tap("kt0", [128, T], BF16)
            t_qt0 = tap("qt0", [128, T], BF16)
            t_kh = tap("kh", [112, T], BF16)
            t_qh = tap("qh", [112, T], BF16)
            t_vs = tap("vs", [128, NT * NHC * 65], BF16)

            def p0_walk(E):
                E.op("vector", lambda v: v.memset(ONE1[:], 1.0))
                E.op("vector", lambda v: v.memset(C_SCALE[:], SCALE))
                E.op("vector", lambda v: v.memset(C_MEXP[:], -MEXP))
                E.op("vector", lambda v: v.memset(C_HALF[:], -0.5))
                E.op("vector", lambda v: v.memset(C_S127[:], 1.0 / 127.0))
                E.op("vector", lambda v: v.memset(C_S127SQ[:], S127SQ))
                E.op("vector", lambda v: v.memset(KH[0:32, :], 0.0))
                E.op("vector", lambda v: v.memset(KH[32:64, :], 0.0))
                E.op("vector", lambda v: v.memset(KH[64:96, :], 0.0))
                E.op("vector", lambda v: v.memset(QH[0:32, :], 0.0))
                E.op("vector", lambda v: v.memset(QH[32:64, :], 0.0))
                E.op("vector", lambda v: v.memset(QH[64:96, :], 0.0), inc=ve)

                E.dma("gpsimd", lambda g: g.dma_start(out=COS[:], in_=cosT), inc=dG)
                E.dma("gpsimd", lambda g: g.dma_start(out=SIN[:], in_=sinT), inc=dG)
                E.dma("sync", lambda s: s.dma_start(out=EYEF[:], in_=eye), inc=dL)
                E.dma("gpsimd", lambda g: g.dma_start(out=EYEB[:], in_=eye), inc=dG)
                E.dma("gpsimd", lambda g: g.dma_start(out=MSK[:], in_=masks), inc=dG)
                E.dma("gpsimd", lambda g: g.dma_start(
                    out=WKVB[:].rearrange("p (c j) -> p c j", j=128),
                    in_=wkv.rearrange("(c p) j -> p c j", p=128)), inc=dG)
                E.dma("gpsimd", lambda g: g.dma_start(out=WKUP[:], in_=wkup), inc=dG)
                E.dma("gpsimd", lambda g: g.dma_start(out=WVUP[:], in_=wvup), inc=dG)
                E.dma("gpsimd", lambda g: g.dma_start(
                    out=WQB[:].rearrange("p (c j) -> p c j", j=512),
                    in_=wq.rearrange("(c p) j -> p c j", p=128)), inc=dG)
                E.dma("gpsimd", lambda g: g.dma_start(
                    out=WQR[:].rearrange("p (c j) -> p c j", j=32),
                    in_=wqr.rearrange("(c p) j -> p c j", p=128)), inc=dG)
                E.dma("gpsimd", lambda g: g.dma_start(out=WKR[:], in_=wkr), inc=dG)
                dl_w = dG.v

                apply_done = [0, 0]
                srope_evac = [0]
                rope_done = [0]
                pe_T = [0, 0]
                xht_evac = [0, 0]
                se_rstd = [0, 0]
                q_evac = [0, 0]
                xhb_tap = 0

                def stage_ln(t):
                    buf = t % 2
                    E.wait("sync", ve, apply_done[buf])
                    E.dma("sync", lambda s, _t=t, _b=buf: s.dma_start(
                        out=XT[_b][:], in_=x_seq[128 * _t:128 * _t + 128, :]), inc=dX[buf])
                    dl_x = dX[buf].v
                    E.wait("vector", dX[buf], dl_x)
                    if t == 0:
                        E.wait("vector", dG, dl_w)
                    E.op("vector", lambda v, _b=buf: v.bn_stats(out=BNS[_b][:, 0:6], in_=XT[_b][:, 0:512]))
                    E.op("vector", lambda v, _b=buf: v.bn_stats(out=BNS[_b][:, 6:12], in_=XT[_b][:, 512:1024]), inc=ve)
                    E.wait("vector", ve)
                    E.op("vector", lambda v, _b=buf: v.bn_aggr(out=BNA[_b][:], in_=BNS[_b][:]), inc=ve)
                    E.wait("vector", ve)
                    E.op("vector", lambda v, _b=buf: v.tensor_scalar_add(
                        out=VAR[_b][:], in0=BNA[_b][:, 1:2], scalar1=1e-5), inc=ve)
                    E.wait("scalar", ve)
                    E.op("scalar", lambda s, _b=buf: s.activation(out=RSTD[_b][:], in_=VAR[_b][:], func=AF.Ln), inc=se)
                    E.wait("scalar", se)
                    E.op("scalar", lambda s, _b=buf: s.activation(out=RSTD[_b][:], in_=RSTD[_b][:],
                                                                  func=AF.Exp, scale=C_HALF[:]), inc=se)
                    se_rstd[buf] = se.v

                def stage_proj(t):
                    nonlocal xhb_tap
                    buf = t % 2
                    E.wait("vector", se, se_rstd[buf])
                    E.wait("vector", pe, pe_T[buf])
                    if xhb_tap:
                        E.wait("vector", dS, xhb_tap)
                    E.op("vector", lambda v, _b=buf: v.tensor_scalar(
                        out=XHB[_b][:], in0=XT[_b][:], scalar1=BNA[_b][:, 0:1], scalar2=RSTD[_b][:],
                        op0=ALU.subtract, op1=ALU.mult), inc=ve)
                    apply_done[buf] = ve.v
                    ve_xhb = ve.v
                    if t_xhb is not None and t == 0:
                        E.wait("sync", ve, ve_xhb)
                        E.dma("sync", lambda s: s.dma_start(out=t_xhb, in_=XHB[0][:]), inc=dS)
                        xhb_tap = dS.v
                    E.wait("tensor", ve, ve_xhb)
                    E.wait("tensor", ve, xht_evac[buf])
                    for c in range(8):
                        E.op("tensor", lambda tt, _c=c, _b=buf: tt.transpose(
                            out=pT[_b][:, 128 * _c:128 * _c + 128],
                            in_=XHB[_b][:, 128 * _c:128 * _c + 128],
                            identity=EYEB[:]), inc=(pe if c == 7 else None))
                    pe_T[buf] = pe.v
                    E.wait("vector", pe, pe_T[buf])
                    E.op("vector", lambda v, _b=buf: v.tensor_copy(out=XHT[_b][:], in_=pT[_b][:]), inc=ve)
                    xht_evac[buf] = ve.v
                    ve_xht = ve.v
                    E.wait("tensor", ve, ve_xht)
                    for c in range(8):
                        E.op("tensor", lambda tt, _c=c, _b=buf: tt.matmul(
                            out=pCK[:], lhsT=WKVB[:, 128 * _c:128 * _c + 128],
                            rhs=XHT[_b][:, 128 * _c:128 * _c + 128],
                            start=(_c == 0), stop=(_c == 7)), inc=(pe if c == 7 else None))
                    E.wait("vector", pe)
                    E.op("vector", lambda v, _b=buf: v.tensor_copy(out=CKT[_b][:], in_=pCK[:]), inc=ve)
                    ve_ck = ve.v
                    E.wait("tensor", ve, ve_ck)
                    for c in range(4):
                        E.op("tensor", lambda tt, _c=c, _b=buf: tt.matmul(
                            out=pK[:, 128 * _c:128 * _c + 128],
                            lhsT=WKUP[:, 128 * _c:128 * _c + 128], rhs=CKT[_b][:],
                            start=True, stop=True), inc=(pe if c == 3 else None))
                    pe_k = pe.v
                    E.wait("tensor", se, srope_evac[0])
                    E.op("tensor", lambda tt, _b=buf: tt.matmul(out=pROPE[:, 0:128], lhsT=WKR[:, 0:16],
                                                                rhs=CKT[_b][:], start=True, stop=True))
                    E.op("tensor", lambda tt, _b=buf: tt.matmul(out=pROPE[:, 128:256], lhsT=WKR[:, 16:32],
                                                                rhs=CKT[_b][:], start=True, stop=True), inc=pe)
                    E.op("tensor", lambda tt, _b=buf: tt.matmul(out=pV[:], lhsT=CKT[_b][:], rhs=WVUP[:],
                                                                start=True, stop=True), inc=pe)
                    pe_v = pe.v
                    E.wait("scalar", pe, pe_k)
                    for c in range(4):
                        E.op("scalar", lambda s, _c=c, _t=t: s.copy(
                            out=KT[_c][:, 128 * _t:128 * _t + 128],
                            in_=pK[:, 128 * _c:128 * _c + 128]), inc=(se if c == 3 else None))
                    k_evac_se = se.v
                    E.wait("scalar", pe, pe_v)
                    E.op("scalar", lambda s, _t=t: s.copy(
                        out=VS[:].rearrange("p (u h c) -> p u h c", h=NHC, c=65)[:, _t, :, 0:64],
                        in_=pV[:].rearrange("p (h c) -> p h c", c=64)), inc=se)
                    E.op("vector", lambda v, _t=t: v.memset(
                        VS[:].rearrange("p (u h c) -> p u h c", h=NHC, c=65)[:, _t, :, 64:65],
                        1.0), inc=ve)
                    for c in range(4):
                        pq = pQa if c % 2 == 0 else pQb
                        E.wait("tensor", se, q_evac[c % 2])
                        for kk in range(8):
                            E.op("tensor", lambda tt, _c=c, _k=kk, _pq=pq, _b=buf: tt.matmul(
                                out=_pq[:],
                                lhsT=WQB[:, 512 * _k + 128 * _c:512 * _k + 128 * _c + 128],
                                rhs=XHT[_b][:, 128 * _k:128 * _k + 128],
                                start=(_k == 0), stop=(_k == 7)), inc=(pe if kk == 7 else None))
                        E.wait("scalar", pe)
                        E.op("scalar", lambda s, _c=c, _t=t, _pq=pq: s.copy(
                            out=QT[_c][:, 128 * _t:128 * _t + 128], in_=_pq[:]), inc=se)
                        q_evac[c % 2] = se.v
                    for kk in range(8):
                        E.op("tensor", lambda tt, _k=kk, _b=buf: tt.matmul(
                            out=pROPE[:, 256:384], lhsT=WQR[:, 32 * _k:32 * _k + 16],
                            rhs=XHT[_b][:, 128 * _k:128 * _k + 128],
                            start=(_k == 0), stop=(_k == 7)))
                    for kk in range(8):
                        E.op("tensor", lambda tt, _k=kk, _b=buf: tt.matmul(
                            out=pROPE[:, 384:512], lhsT=WQR[:, 32 * _k + 16:32 * _k + 32],
                            rhs=XHT[_b][:, 128 * _k:128 * _k + 128],
                            start=(_k == 0), stop=(_k == 7)), inc=(pe if kk == 7 else None))
                    pe_rope = pe.v
                    E.wait("scalar", pe, pe_rope)
                    E.wait("scalar", ve, rope_done[0])
                    E.op("scalar", lambda s: s.copy(out=SROPE[:], in_=pROPE[:]), inc=se)
                    srope_evac[0] = se.v
                    E.wait("vector", se, srope_evac[0])
                    cs = slice(128 * t, 128 * t + 128)
                    for (x1o, x2o, OUTT) in ((0, 128, KH), (256, 384, QH)):
                        E.wait("vector", ve)
                        E.op("vector", lambda v, _a=x1o, _c=cs: v.tensor_mul(
                            out=RT1[:], in0=SROPE[:, _a:_a + 128], in1=COS[:, _c]))
                        E.op("vector", lambda v, _b2=x2o, _c=cs: v.tensor_mul(
                            out=RT2[:], in0=SROPE[:, _b2:_b2 + 128], in1=SIN[:, _c]), inc=ve)
                        E.wait("vector", ve)
                        E.op("vector", lambda v, _O=OUTT, _c=cs: v.tensor_sub(
                            out=_O[0:16, _c], in0=RT1[:], in1=RT2[:]), inc=ve)
                        E.wait("vector", ve)
                        E.op("vector", lambda v, _b2=x2o, _c=cs: v.tensor_mul(
                            out=RT1[:], in0=SROPE[:, _b2:_b2 + 128], in1=COS[:, _c]))
                        E.op("vector", lambda v, _a=x1o, _c=cs: v.tensor_mul(
                            out=RT2[:], in0=SROPE[:, _a:_a + 128], in1=SIN[:, _c]), inc=ve)
                        E.wait("vector", ve)
                        E.op("vector", lambda v, _O=OUTT, _c=cs: v.tensor_add(
                            out=_O[96:112, _c], in0=RT1[:], in1=RT2[:]), inc=ve)
                    E.op("vector", lambda v: v.engine_nop(), inc=ve)
                    rope_done[0] = ve.v

                for t in range(NT + 1):
                    if t < NT:
                        stage_ln(t)
                    if t >= 1:
                        stage_proj(t - 1)

                if t_kt0 is not None:
                    E.wait("sync", ve)
                    E.wait("sync", pe)
                    E.wait("sync", gp)
                    E.wait("sync", se)
                    E.dma("sync", lambda s: s.dma_start(out=t_kt0, in_=KT[0][:]), inc=dS)
                    E.dma("sync", lambda s: s.dma_start(out=t_qt0, in_=QT[0][:]), inc=dS)
                    if t_kh is not None:
                        E.dma("sync", lambda s: s.dma_start(out=t_kh, in_=KH[:]), inc=dS)
                    if t_qh is not None:
                        E.dma("sync", lambda s: s.dma_start(out=t_qh, in_=QH[:]), inc=dS)
                    E.dma("sync", lambda s: s.dma_start(out=t_vs, in_=VS[:]), inc=dS)
                    E.wait("sync", dS)

                for eng in ENGS:
                    E.wait_all(eng)

            run_block(nc, sems, p0_walk)

        if phases == 1:
            return nc, tap_out

        with ExitStack() as p1:
            sb1 = lambda n, s, d: p1.enter_context(nc.sbuf_tensor(n, s, d, side="right"))
            ps1 = lambda n, s, d: p1.enter_context(nc.psum_tensor(n, s, d))
            KH2 = [KH, sb1("KHb", [112, T], BF16)]
            QH2 = [QH, sb1("QHb", [112, T], BF16)]
            PB = [sb1(f"PB{i}", [128, 2 * 512], BF16) for i in range(3)]
            RECB = sb1("RECB", [1, 512], BF16)
            BCS = sb1("BCS", [64, 512], F32)
            S_ = [ps1(f"S{i}", [128, 2 * 512], F32) for i in range(3)]
            AVp = ps1("AVp", [128, 1024], F32)

            t_col0 = tap("col0", [128, T], BF16)
            t_col1 = tap("col1", [128, T], BF16)

            def p1_walk(E):
                E.op("vector", lambda v: v.tensor_copy(out=KH2[1][0:32, :], in_=KH2[0][0:32, :]))
                E.op("vector", lambda v: v.tensor_copy(out=KH2[1][96:112, :], in_=KH2[0][96:112, :]))
                E.op("vector", lambda v: v.tensor_copy(out=QH2[1][0:32, :], in_=QH2[0][0:32, :]))
                E.op("vector", lambda v: v.tensor_copy(out=QH2[1][96:112, :], in_=QH2[0][96:112, :]), inc=ve)

                div_done = [0, 0]
                last_exp = [0, 0, 0]
                asm_pe_gate = [0, 0]
                asm_done = [0, 0]
                qt_par = 0

                def assemble(h):
                    hb = h % 2
                    c, r = h // 2, 64 * (h % 2)
                    E.wait("vector", pe, asm_pe_gate[hb])
                    for (DST, SRC) in ((KH2[hb], KT[c]), (QH2[hb], QT[c])):
                        E.op("vector", lambda v, _D=DST, _S=SRC, _r=r: v.tensor_copy(
                            out=_D[32:64, :], in_=_S[_r:_r + 32, :]))
                        E.op("vector", lambda v, _D=DST, _S=SRC, _r=r: v.tensor_copy(
                            out=_D[64:96, :], in_=_S[_r + 32:_r + 64, :]))
                    E.op("vector", lambda v: v.engine_nop(), inc=ve)
                    asm_done[hb] = ve.v

                assemble(0)
                for h in range(NHC):
                    hb = h % 2
                    KHc, QHc = KH2[hb], QH2[hb]
                    if h + 1 < NHC:
                        assemble(h + 1)
                    gctr = 0
                    for qt in range(NQT):
                        ah = qt_par = (h * NQT + qt) % 2
                        av0 = 512 * ah
                        nk = 4 * qt + 4
                        groups = [list(range(g, min(g + 2, nk))) for g in range(0, nk, 2)]
                        ng = len(groups)
                        gpar = [(gctr + i) % 3 for i in range(ng)]
                        gctr = (gctr + ng) % 3
                        pe_sc = {}
                        av_cnt = 0

                        def emit_scores(gi):
                            grp = groups[gi]
                            par = gpar[gi]
                            E.wait("tensor", ve, asm_done[hb])
                            E.wait("tensor", se, last_exp[par])
                            for j, kt in enumerate(grp):
                                diag = kt >= 4 * qt
                                last_j = (j == len(grp) - 1)
                                E.op("tensor", lambda tt, _kt=kt, _qt=qt, _p=par, _j=j, _d=diag, _K=KHc, _Q=QHc: tt.matmul(
                                    out=S_[_p][:, 512 * _j:512 * _j + 512],
                                    lhsT=_K[:, 128 * _kt:128 * _kt + 128],
                                    rhs=_Q[:, 512 * _qt:512 * _qt + 512],
                                    start=True, stop=(not _d)),
                                    inc=(pe if (last_j and not diag) else None))
                                if diag:
                                    dlt = kt - 4 * qt
                                    E.op("tensor", lambda tt, _p=par, _j=j, _dl=dlt: tt.matmul(
                                        out=S_[_p][:, 512 * _j:512 * _j + 512],
                                        lhsT=EYEB[:], rhs=MSK[:, 512 * _dl:512 * _dl + 512],
                                        start=False, stop=True),
                                        inc=(pe if last_j else None))
                            pe_sc[gi] = pe.v

                        def emit_exp(gi):
                            grp = groups[gi]
                            par = gpar[gi]
                            E.wait("scalar", pe, pe_sc[gi])
                            E.op("scalar", lambda s, _p=par, _n=len(grp): s.activation(
                                out=PB[_p][:, 0:512 * _n], in_=S_[_p][:, 0:512 * _n],
                                func=AF.Exp, scale=C_SCALE[:], bias=C_MEXP[:]), inc=se)
                            last_exp[par] = se.v

                        def emit_av(gi):
                            nonlocal av_cnt
                            grp = groups[gi]
                            par = gpar[gi]
                            E.wait("tensor", se, last_exp[par])
                            if gi == 0:
                                E.wait("tensor", ve, div_done[ah])
                            for j, kt in enumerate(grp):
                                first = (av_cnt == 0)
                                last = (av_cnt == nk - 1)
                                av_cnt += 1
                                E.op("tensor", lambda tt, _kt=kt, _h=h, _p=par, _j=j, _f=first, _l=last, _a=av0: tt.matmul(
                                    out=AVp[0:65, _a:_a + 512],
                                    lhsT=VS[:, 520 * _kt + 65 * _h:520 * _kt + 65 * _h + 65],
                                    rhs=PB[_p][:, 512 * _j:512 * _j + 512],
                                    start=_f, stop=_l), inc=(pe if last else None))

                        emit_scores(0)
                        emit_exp(0)
                        if ng > 1:
                            emit_scores(1)
                            emit_exp(1)
                        for gi in range(2, ng):
                            emit_scores(gi)
                            emit_av(gi - 2)
                            emit_exp(gi)
                        if ng > 1:
                            emit_av(ng - 2)
                        emit_av(ng - 1)
                        pe_av = pe.v
                        E.wait("vector", pe, pe_av)

                        def _recip_bf16(v, a):
                            with nc.allow_low_precision(reason="bf16 store of fp32 reciprocal"):
                                return v.reciprocal(out=RECB[:], in_=AVp[64:65, a:a + 512])
                        E.op("vector", lambda v, _a=av0: _recip_bf16(v, _a), inc=ve)
                        E.wait("tensor", ve)
                        E.op("tensor", lambda tt, _a=av0: tt.matmul(
                            out=AVp[64:128, _a:_a + 512], lhsT=ONE1[:], rhs=RECB[:],
                            start=True, stop=True), inc=pe)
                        E.wait("vector", pe)
                        E.op("vector", lambda v, _a=av0: v.tensor_copy(
                            out=BCS[:], in_=AVp[64:128, _a:_a + 512]), inc=ve)
                        E.wait("vector", ve)
                        E.op("vector", lambda v, _h=h, _qt=qt, _a=av0: v.scalar_tensor_tensor(
                            out=COL[_h // 2][64 * (_h % 2):64 * (_h % 2) + 64,
                                             512 * _qt:512 * _qt + 512],
                            in0=AVp[0:64, _a:_a + 512], scalar=1.0, in1=BCS[:],
                            op0=ALU.mult, op1=ALU.mult), inc=ve)
                        div_done[ah] = ve.v
                    asm_pe_gate[hb] = pe.v

                if t_col0 is not None:
                    E.wait("sync", ve)
                    E.dma("sync", lambda s: s.dma_start(out=t_col0, in_=COL[0][:]), inc=dS)
                    if t_col1 is not None:
                        E.dma("sync", lambda s: s.dma_start(out=t_col1, in_=COL[1][:]), inc=dS)
                    E.wait("sync", dS)

                for eng in ENGS:
                    E.wait_all(eng)

            run_block(nc, sems, p1_walk)

        if phases == 2:
            return nc, tap_out

        S01.close()

        S23 = ExitStack()
        X2T = [S23.enter_context(nc.sbuf_tensor(f"X2T{i}", [128, OWN], BF16, side="right"))
               for i in range(8)]
        with ExitStack() as p2:
            sb2 = lambda n, s, d: p2.enter_context(nc.sbuf_tensor(n, s, d, side="right"))
            ps2 = lambda n, s, d: p2.enter_context(nc.psum_tensor(n, s, d))
            WOB = sb2("WOB", [128, 4 * 1024], BF16)
            PRT = [sb2(f"PRT{i}", [128, D], BF16) for i in range(2)]
            XO = [sb2(f"XO{i}", [128, D], F32) for i in range(2)]
            AT = [sb2(f"AT{i}", [128, D], BF16) for i in range(2)]
            R1 = sb2("R1", [128, D], F32)
            X2B = sb2("X2B", [128, D], BF16)
            VAR2 = sb2("VAR2", [128, 1], F32)
            RSTD2 = sb2("RSTD2", [128, 1], F32)
            BNS2 = sb2("BNS2", [128, 12], F32)
            BNA2 = sb2("BNA2", [128, 2], F32)
            pWO = [ps2(f"pWO{i}", [128, 1024], F32) for i in range(2)]
            pT2 = ps2("pT2", [128, 1024], BF16)

            t_r1 = tap("r1", [OWN, D], F32)
            t_x2 = tap("x2", [128, OWN], BF16)

            def p2_walk2(E):
                p2_state = {}
                for fc in range(4):
                    E.dma("gpsimd", lambda g, _f=fc: g.dma_start(
                        out=WOB[:, 1024 * _f:1024 * _f + 1024],
                        in_=wo[128 * _f:128 * _f + 128, :]), inc=dG)
                dl_wo = dG.v
                prt_free = [0, 0]
                evac_done = [0, 0]
                for t in range(NT):
                    buf = t % 2
                    E.wait("tensor", dG, dl_wo)
                    E.wait("tensor", ve, evac_done[buf])
                    for half in range(2):
                        for fc in range(4):
                            E.op("tensor", lambda t_, _f=fc, _t=t, _h=half, _b=buf: t_.matmul(
                                out=pWO[_b][:, 512 * _h:512 * _h + 512],
                                lhsT=COL[_f][:, 128 * _t:128 * _t + 128],
                                rhs=WOB[:, 1024 * _f + 512 * _h:1024 * _f + 512 * _h + 512],
                                start=(_f == 0), stop=(_f == 3)),
                                inc=(pe if (half == 1 and fc == 3) else None))
                    pe_wo = pe.v
                    E.wait("vector", pe, pe_wo)
                    E.wait("vector", dP[buf], prt_free[buf])
                    E.op("vector", lambda v, _b=buf: v.tensor_copy(out=PRT[_b][:], in_=pWO[_b][:]), inc=ve)
                    evac_done[buf] = ve.v
                    E.wait("sync", ve, evac_done[buf])
                    E.dma("sync", lambda s, _t=t, _b=buf: s.dma_start(
                        out=snd.ap()[128 * _t:128 * _t + 128, :], in_=PRT[_b][:]), inc=dP[buf])
                    prt_free[buf] = dP[buf].v
                E.wait("gpsimd", dP[0])
                E.wait("gpsimd", dP[1])
                E.op("gpsimd", lambda g: g.collective_compute(
                    "ReduceScatter", ALU.add,
                    replica_groups=[[0, 1], [2, 3], [4, 5], [6, 7]],
                    ins=[snd.ap().opt()], outs=[rcv.ap().opt()]), inc=cc)

                r1_done = 0
                xo_free = [0, 0]
                at_free = [0, 0]
                pe_t2 = 0
                for t in range(16):
                    buf = t % 2
                    E.wait("sync", ve, xo_free[buf])
                    E.dma("sync", lambda s, _t=t, _b=buf: s.dma_start(
                        out=XO[_b][:], in_=x_own[128 * _t:128 * _t + 128, :]), inc=dX[buf])
                    E.wait("sync", cc)
                    E.wait("sync", ve, at_free[buf])
                    E.dma("sync", lambda s, _t=t, _b=buf: s.dma_start(
                        out=AT[_b][:], in_=rcv.ap()[128 * _t:128 * _t + 128, :]), inc=dA[buf])
                    E.wait("vector", dX[buf])
                    E.wait("vector", dA[buf])
                    E.wait("vector", dS, r1_done)
                    E.op("vector", lambda v, _b=buf: v.tensor_add(out=R1[:], in0=XO[_b][:], in1=AT[_b][:]), inc=ve)
                    xo_free[buf] = ve.v
                    at_free[buf] = ve.v
                    E.wait("sync", ve)
                    E.dma("sync", lambda s, _t=t: s.dma_start(
                        out=r1spill.ap()[128 * _t:128 * _t + 128, :], in_=R1[:]), inc=dS)
                    r1_done = dS.v
                    if t_r1 is not None:
                        E.dma("sync", lambda s, _t=t: s.dma_start(
                            out=t_r1[128 * _t:128 * _t + 128, :], in_=R1[:]), inc=dS)
                        r1_done = dS.v
                    E.wait("vector", ve)
                    E.op("vector", lambda v: v.bn_stats(out=BNS2[:, 0:6], in_=R1[:, 0:512]))
                    E.op("vector", lambda v: v.bn_stats(out=BNS2[:, 6:12], in_=R1[:, 512:1024]), inc=ve)
                    E.wait("vector", ve)
                    E.op("vector", lambda v: v.bn_aggr(out=BNA2[:], in_=BNS2[:]), inc=ve)
                    E.wait("vector", ve)
                    E.op("vector", lambda v: v.tensor_scalar_add(out=VAR2[:], in0=BNA2[:, 1:2], scalar1=1e-5), inc=ve)
                    E.wait("scalar", ve)
                    E.op("scalar", lambda s: s.activation(out=RSTD2[:], in_=VAR2[:], func=AF.Ln), inc=se)
                    E.wait("scalar", se)
                    E.op("scalar", lambda s: s.activation(out=RSTD2[:], in_=RSTD2[:], func=AF.Exp, scale=C_HALF[:]), inc=se)
                    E.wait("vector", se)
                    E.wait("vector", pe, pe_t2)
                    E.op("vector", lambda v: v.tensor_scalar(
                        out=X2B[:], in0=R1[:], scalar1=BNA2[:, 0:1], scalar2=RSTD2[:],
                        op0=ALU.subtract, op1=ALU.mult), inc=ve)
                    ve_x2 = ve.v
                    E.wait("tensor", ve, ve_x2)
                    for c in range(8):
                        E.op("tensor", lambda t_, _c=c: t_.transpose(
                            out=pT2[:, 128 * _c:128 * _c + 128],
                            in_=X2B[:, 128 * _c:128 * _c + 128],
                            identity=EYEB[:]), inc=(pe if c == 7 else None))
                    pe_t2 = pe.v
                    E.wait("vector", pe, pe_t2)
                    for c in range(8):
                        E.op("vector", lambda v, _c=c, _t=t: v.tensor_copy(
                            out=X2T[_c][:, 128 * _t:128 * _t + 128],
                            in_=pT2[:, 128 * _c:128 * _c + 128]),
                            inc=(ve if c == 7 else None))

                if t_x2 is not None:
                    E.wait("sync", ve)
                    E.dma("sync", lambda s: s.dma_start(out=t_x2, in_=X2T[0][:]), inc=dS)
                    E.wait("sync", dS)

                for eng in ENGS:
                    E.wait_all(eng)

            run_block(nc, sems, p2_walk2)

        if phases == 3:
            return nc, tap_out

        with ExitStack() as p3:
            sb3 = lambda n, s, d: p3.enter_context(nc.sbuf_tensor(n, s, d, side="right"))
            ps3 = lambda n, s, d: p3.enter_context(nc.psum_tensor(n, s, d))
            HB = p3.enter_context(nc.sbuf_tensor("HB", [128, 24 * OWN], BF16, side="left"))

            WDA1 = p3.enter_context(nc.sbuf_tensor("WDA1", [128, 12 * 1024], BF16, side="left"))
            p3a = ExitStack()
            WGU = [p3a.enter_context(nc.sbuf_tensor(f"WGU{i}", [128, 2 * 1024], BF16, side="right")) for i in range(2)]
            GS = p3a.enter_context(nc.sbuf_tensor("GS", [128, 512], BF16, side="right"))
            US = p3a.enter_context(nc.sbuf_tensor("US", [128, 512], BF16, side="right"))
            pG = p3a.enter_context(nc.psum_tensor("pG", [128, 512], F32))
            pU = p3a.enter_context(nc.psum_tensor("pU", [128, 512], F32))

            dl_wd_early = [0]

            def p3a_walk(E):
                wgu_free = [0, 0]
                silu_done = 0
                umul_done = 0
                for hc in range(12):
                    E.dma("gpsimd", lambda g, _h=hc: g.dma_start(
                        out=WDA1[:, 1024 * _h:1024 * _h + 1024],
                        in_=wdT[128 * _h:128 * _h + 128, :]), inc=dG)
                dl_wd_early[0] = dG.v
                for hc in range(24):
                    wb = hc % 2
                    E.wait("gpsimd", pe, wgu_free[wb])
                    E.dma("gpsimd", lambda g, _h=hc, _b=wb: g.dma_start(
                        out=WGU[_b][:, 0:1024].rearrange("p (c j) -> p c j", j=128),
                        in_=wgT.rearrange("(c p) j -> p c j", p=128)[:, :, 128 * _h:128 * _h + 128]), inc=dW[wb])
                    E.dma("gpsimd", lambda g, _h=hc, _b=wb: g.dma_start(
                        out=WGU[_b][:, 1024:2048].rearrange("p (c j) -> p c j", j=128),
                        in_=wuT.rearrange("(c p) j -> p c j", p=128)[:, :, 128 * _h:128 * _h + 128]), inc=dW[wb])
                    dl_gu = dW[wb].v
                    for tt in range(4):
                        E.wait("tensor", dW[wb], dl_gu)
                        E.wait("tensor", se, silu_done)
                        for c in range(8):
                            E.op("tensor", lambda t_, _c=c, _t=tt, _b=wb: t_.matmul(
                                out=pG[:], lhsT=WGU[_b][:, 128 * _c:128 * _c + 128],
                                rhs=X2T[_c][:, 512 * _t:512 * _t + 512],
                                start=(_c == 0), stop=(_c == 7)), inc=(pe if c == 7 else None))
                        pe_g = pe.v
                        E.wait("tensor", ve, umul_done)
                        for c in range(8):
                            E.op("tensor", lambda t_, _c=c, _t=tt, _b=wb: t_.matmul(
                                out=pU[:], lhsT=WGU[_b][:, 1024 + 128 * _c:1024 + 128 * _c + 128],
                                rhs=X2T[_c][:, 512 * _t:512 * _t + 512],
                                start=(_c == 0), stop=(_c == 7)), inc=(pe if c == 7 else None))
                        pe_u = pe.v
                        if tt == 3:
                            wgu_free[wb] = pe.v
                        E.wait("scalar", pe, pe_g)
                        E.wait("scalar", ve, umul_done)
                        E.op("scalar", lambda s: s.activation(out=GS[:], in_=pG[:], func=AF.Silu,
                                                              scale=C_S127[:]), inc=se)
                        silu_done = se.v
                        E.wait("vector", pe, pe_u)
                        E.op("vector", lambda v: v.tensor_copy(out=US[:], in_=pU[:]), inc=ve)
                        E.wait("vector", se, silu_done)
                        E.wait("vector", ve)
                        E.op("vector", lambda v, _h=hc, _t=tt: v.tensor_mul(
                            out=HB[:, OWN * _h + 512 * _t:OWN * _h + 512 * _t + 512],
                            in0=GS[:], in1=US[:]), inc=ve)
                        umul_done = ve.v
                for eng in ENGS:
                    E.wait_all(eng)

            run_block(nc, sems, p3a_walk)
            p3a.close()
            S23.close()

            WDA2 = p3.enter_context(nc.sbuf_tensor("WDA2", [128, 12 * 1024], BF16, side="right"))
            sbl = lambda n, s, d: p3.enter_context(nc.sbuf_tensor(n, s, d, side="left"))
            FDT = sbl("FDT", [128, 4096], F32)
            R1T = [sbl(f"R1T{i}", [128, D], F32) for i in range(2)]
            OStg = sbl("OStg", [128, D], F32)
            pD = ps3("pD", [128, 3072], F32)
            pR = ps3("pR", [128, 1024], F32)

            def p3b_walk(E):
                for hc in range(12):
                    E.dma("gpsimd", lambda g, _h=hc: g.dma_start(
                        out=WDA2[:, 1024 * _h:1024 * _h + 1024],
                        in_=wdT[128 * (_h + 12):128 * (_h + 12) + 128, :]), inc=dG)
                dl_wd1 = dl_wd_early[0]
                dl_wd = dG.v

                fdt_free = 0
                r1t_free = [0, 0]
                ost_free = 0
                stt_done = 0
                for tt in range(4):
                    E.wait("tensor", dG, dl_wd1)
                    E.wait("tensor", se, fdt_free)
                    E.wait("tensor", ve, stt_done)
                    for hc in range(24):
                        if hc == 12:
                            E.wait("tensor", dG, dl_wd)
                        WDAx = WDA1 if hc < 12 else WDA2
                        hco = hc % 12
                        for dc in range(8):
                            dst = pD[:, 512 * dc:512 * dc + 512] if dc < 6 else \
                                  pR[:, 512 * (dc - 6):512 * (dc - 6) + 512]
                            E.op("tensor", lambda t_, _h=hc, _ho=hco, _W=WDAx, _d=dc, _t=tt, _dst=dst: t_.matmul(
                                out=_dst,
                                lhsT=_W[:, 1024 * _ho + 128 * _d:1024 * _ho + 128 * _d + 128],
                                rhs=HB[:, OWN * _h + 512 * _t:OWN * _h + 512 * _t + 512],
                                start=(_h == 0), stop=(_h == 23)),
                                inc=(pe if (hc == 23 and dc == 7) else None))
                    pe_d = pe.v
                    E.wait("scalar", pe, pe_d)
                    for dc in range(8):
                        src = pD[:, 512 * dc:512 * dc + 512] if dc < 6 else \
                              pR[:, 512 * (dc - 6):512 * (dc - 6) + 512]
                        E.op("scalar", lambda s, _d=dc, _src=src: s.activation(
                            out=FDT[:, 512 * _d:512 * _d + 512], in_=_src,
                            func=AF.Copy, scale=C_S127SQ[:]),
                            inc=(se if dc == 7 else None))
                    fdt_free = se.v
                    for s4 in range(4):
                        sb_ = s4 % 2
                        E.wait("sync", ve, r1t_free[sb_])
                        E.dma("sync", lambda s, _t=tt, _s=s4, _b=sb_: s.dma_start(
                            out=R1T[_b][:],
                            in_=r1spill.ap()[512 * _t + 128 * _s:512 * _t + 128 * _s + 128, :]),
                            inc=dX[sb_])
                        dl_r1 = (sb_, dX[sb_].v)
                        E.wait("tensor", se, fdt_free)
                        if s4 >= 2:
                            E.wait("tensor", ve)
                        for dc in range(8):
                            E.op("tensor", lambda t_, _d=dc, _s=s4: t_.transpose(
                                out=pD[:, 1024 * (_s % 2) + 128 * _d:1024 * (_s % 2) + 128 * _d + 128],
                                in_=FDT[:, 512 * _d + 128 * _s:512 * _d + 128 * _s + 128],
                                identity=EYEF[:]), inc=(pe if dc == 7 else None))
                        pe_tr = pe.v
                        E.wait("vector", pe, pe_tr)
                        E.wait("vector", dX[dl_r1[0]], dl_r1[1])
                        E.wait("vector", dS, ost_free)
                        E.op("vector", lambda v, _s=s4, _b=sb_: v.scalar_tensor_tensor(
                            out=OStg[:], in0=pD[:, 1024 * (_s % 2):1024 * (_s % 2) + 1024],
                            scalar=1.0, in1=R1T[_b][:],
                            op0=ALU.mult, op1=ALU.add), inc=ve)
                        ve_o = ve.v
                        r1t_free[sb_] = ve_o
                        E.wait("sync", ve, ve_o)
                        E.dma("sync", lambda s, _t=tt, _s=s4: s.dma_start(
                            out=y[512 * _t + 128 * _s:512 * _t + 128 * _s + 128, :],
                            in_=OStg[:]), inc=dS)
                        ost_free = dS.v
                    stt_done = ve.v

                for eng in ENGS:
                    E.wait_all(eng)

            run_block(nc, sems, p3b_walk)

    return nc, tap_out


def _host_inputs(core, inputs):
    b, p = core // 2, core % 2
    h0 = NHC * p * HD
    x = np.ascontiguousarray(np.asarray(inputs["L0_attn_x"])[b])
    out = dict(
        x_seq=x,
        x_own=np.ascontiguousarray(x[OWN * p:OWN * p + OWN]),
        wkv=np.ascontiguousarray(np.asarray(inputs["L0_attn_W_kv_down"])),
        wkup=np.ascontiguousarray(np.asarray(inputs["L0_attn_W_k_up"])[:, h0:h0 + NHC * HD]),
        wvup=np.ascontiguousarray(np.asarray(inputs["L0_attn_W_v_up"])[:, h0:h0 + NHC * HD]),
        wq=np.ascontiguousarray(np.asarray(inputs["L0_attn_W_q"])[:, h0:h0 + NHC * HD]),
        wqr=np.ascontiguousarray(np.asarray(inputs["L0_attn_W_q_rope"])),
        wkr=np.ascontiguousarray(np.asarray(inputs["L0_attn_W_k_rope"])),
        wo=np.ascontiguousarray(np.asarray(inputs["L0_attn_W_o"])[h0:h0 + NHC * HD, :]),
        wgT=np.ascontiguousarray(np.asarray(inputs["L0_ffn_w_gate"]).T),
        wuT=np.ascontiguousarray(np.asarray(inputs["L0_ffn_w_up"]).T),
        wdT=np.ascontiguousarray(np.asarray(inputs["L0_ffn_w_down"]).T),
    )
    inv_freq = 1.0 / 10000.0 ** (np.arange(0, DR, 2, dtype=np.float32) / DR)
    tpos = np.arange(T, dtype=np.float32)
    freqs = np.outer(tpos, inv_freq).astype(np.float32)
    out["cosT"] = np.ascontiguousarray(np.cos(freqs).T.astype(np.float32))
    out["sinT"] = np.ascontiguousarray(np.sin(freqs).T.astype(np.float32))
    out["eye"] = np.eye(128, dtype=np.float32)
    masks_ = np.zeros((128, 4, 512), dtype=np.float32)
    k_loc = np.arange(128)[:, None]
    q_loc = np.arange(512)[None, :]
    for i, dlt in enumerate((0, 128, 256, 384)):
        masks_[:, i, :] = np.where(k_loc + dlt > q_loc, -1e9, 0.0)
    out["masks"] = np.ascontiguousarray(masks_.reshape(128, 4 * 512))
    return out


def kernel(**inputs):
    nc, _ = build()
    in_maps = [_host_inputs(c, inputs) for c in range(8)]
    res = run_bass_kernel_spmd(nc, in_maps, core_ids=list(range(8)))
    out = np.zeros((4, T, D), dtype=np.float32)
    for c in range(8):
        b, p = c // 2, c % 2
        out[b, OWN * p:OWN * p + OWN] = np.asarray(res.results[c]["y"], dtype=np.float32)
    return out


# version 05
# speedup vs baseline: 1.5310x; 1.0570x over previous
import math
from contextlib import ExitStack

import numpy as np
import concourse.bass as bass
import concourse.mybir as mybir
from concourse.bass_utils import run_bass_kernel_spmd

F32 = mybir.dt.float32
BF16 = mybir.dt.bfloat16
I32 = mybir.dt.int32
AF = mybir.ActivationFunctionType
ALU = mybir.AluOpType
AX = mybir.AxisListType

T = 4096
D = 1024
NHC = 8
HD = 64
DC = 128
DR = 32
FF = 3072
NT = T // 128
OWN = 2048
NQT = T // 512
SCALE = 1.0 / math.sqrt(HD + DR)
MEXP = 24.0
S127SQ = (1.0 / 127.0) ** 2

ENGS = ("sync", "gpsimd", "vector", "scalar", "tensor")


class Sem:
    def __init__(self, ctx, nc, name):
        self.h = ctx.enter_context(nc.semaphore(name))
        self.v = 0
        self.name = name


class Emitter:
    def __init__(self, who, eng, sems):
        self.who = who
        self.eng = eng
        self.sems = sems

    def op(self, engine, fn, inc=None, n=1):
        if self.who == engine:
            instr = fn(self.eng)
            if inc is not None:
                instr.then_inc(inc.h, n)
        if inc is not None:
            inc.v += n
        return inc.v if inc is not None else None

    def dma(self, engine, fn, inc):
        return self.op(engine, fn, inc, 16)

    def wait(self, engine, sem, val=None):
        if self.who == engine:
            self.eng.wait_ge(sem.h, sem.v if val is None else val)

    def wait_all(self, engine):
        for s in self.sems:
            if s.v > 0 and self.who == engine:
                self.eng.wait_ge(s.h, s.v)


def run_block(nc, sems, walk):
    with nc.Block() as blk:
        snap = [s.v for s in sems]
        for engname in ENGS:
            def body(eng, _en=engname):
                for s, v in zip(sems, snap):
                    s.v = v
                walk(Emitter(_en, eng, sems))
            getattr(blk, engname)(body)


def build(debug_taps=(), phases=4):
    nc = bass.Bass("TRN2", target_bir_lowering=False, debug=False)

    x_seq = nc.dram_tensor("x_seq", [T, D], F32, kind="ExternalInput").ap()
    x_own = nc.dram_tensor("x_own", [OWN, D], F32, kind="ExternalInput").ap()
    wkv = nc.dram_tensor("wkv", [D, DC], F32, kind="ExternalInput").ap()
    wkup = nc.dram_tensor("wkup", [DC, NHC * HD], F32, kind="ExternalInput").ap()
    wvup = nc.dram_tensor("wvup", [DC, NHC * HD], F32, kind="ExternalInput").ap()
    wq = nc.dram_tensor("wq", [D, NHC * HD], F32, kind="ExternalInput").ap()
    wqr = nc.dram_tensor("wqr", [D, DR], F32, kind="ExternalInput").ap()
    wkr = nc.dram_tensor("wkr", [DC, DR], F32, kind="ExternalInput").ap()
    wo = nc.dram_tensor("wo", [NHC * HD, D], F32, kind="ExternalInput").ap()
    wgT = nc.dram_tensor("wgT", [D, FF], I32, kind="ExternalInput").ap()
    wuT = nc.dram_tensor("wuT", [D, FF], I32, kind="ExternalInput").ap()
    wdT = nc.dram_tensor("wdT", [FF, D], I32, kind="ExternalInput").ap()
    cosT = nc.dram_tensor("cosT", [DR // 2, T], F32, kind="ExternalInput").ap()
    sinT = nc.dram_tensor("sinT", [DR // 2, T], F32, kind="ExternalInput").ap()
    eye = nc.dram_tensor("eye", [128, 128], F32, kind="ExternalInput").ap()
    masks = nc.dram_tensor("masks", [128, 4 * 512], F32, kind="ExternalInput").ap()

    y = nc.dram_tensor("y", [OWN, D], F32, kind="ExternalOutput").ap()

    snd = nc.dram_tensor("snd", [T, D], BF16)
    rcv = nc.dram_tensor("rcv", [OWN, D], BF16)
    r1spill = nc.dram_tensor("r1spill", [OWN, D], F32)

    tap_out = {}

    def tap(name, shape, dtype):
        if name in debug_taps:
            tap_out[name] = nc.dram_tensor("tap_" + name, shape, dtype,
                                           kind="ExternalOutput").ap()
            return tap_out[name]
        return None

    with ExitStack() as ctx:
        sbt = lambda n, s, d: ctx.enter_context(nc.sbuf_tensor(n, s, d))

        COL = [sbt(f"COL{i}", [128, T], BF16) for i in range(4)]
        S01 = ExitStack()
        s01t = lambda n, s, d: S01.enter_context(nc.sbuf_tensor(n, s, d, side="right"))
        KT = [s01t(f"KT{i}", [128, T], BF16) for i in range(4)]
        QT = [s01t(f"QT{i}", [128, T], BF16) for i in range(4)]
        KH = s01t("KH", [112, T], BF16)
        QH = s01t("QH", [112, T], BF16)
        MSK = s01t("MSK", [128, 4 * 512], BF16)
        VS = s01t("VS", [128, NT * (NHC * 65)], BF16)
        EYEB = sbt("EYEB", [128, 128], BF16)
        EYEF = sbt("EYEF", [128, 128], F32)
        ONE1 = sbt("ONE1", [1, 64], BF16)
        C_SCALE = sbt("C_SCALE", [128, 1], F32)
        C_MEXP = sbt("C_MEXP", [128, 1], F32)
        C_HALF = sbt("C_HALF", [128, 1], F32)
        C_S127 = sbt("C_S127", [128, 1], F32)
        C_S127SQ = sbt("C_S127SQ", [128, 1], F32)

        sems = []

        def mk(name):
            s = Sem(ctx, nc, name)
            sems.append(s)
            return s

        dL = mk("dL")
        dG = mk("dG")
        dS = mk("dS")
        dX = [mk("dXa"), mk("dXb")]
        dA = [mk("dAa"), mk("dAb")]
        dP = [mk("dPa"), mk("dPb")]
        dW = [mk("dWa"), mk("dWb")]
        pe = mk("pe")
        ve = mk("ve")
        se = mk("se")
        cc = mk("cc")
        cc2 = mk("cc2")
        gp = mk("gp")

        with ExitStack() as p0:
            sb0 = lambda n, s, d: p0.enter_context(nc.sbuf_tensor(n, s, d, side="right"))
            ps0 = lambda n, s, d: p0.enter_context(nc.psum_tensor(n, s, d))
            XT = [sb0(f"XT{i}", [128, D], F32) for i in range(2)]
            XHB = [sb0(f"XHB{i}", [128, D], BF16) for i in range(2)]
            XHT = [sb0(f"XHT{i}", [128, D], BF16) for i in range(2)]
            CKT = [sb0(f"CKT{i}", [128, 128], BF16) for i in range(2)]
            VAR = [sb0(f"VAR{i}", [128, 1], F32) for i in range(2)]
            RSTD = [sb0(f"RSTD{i}", [128, 1], F32) for i in range(2)]
            BNS = [sb0(f"BNS{i}", [128, 12], F32) for i in range(2)]
            BNA = [sb0(f"BNA{i}", [128, 2], F32) for i in range(2)]
            WKVB = sb0("WKVB", [128, 8 * 128], BF16)
            WKUP = sb0("WKUP", [128, 512], BF16)
            WVUP = sb0("WVUP", [128, 512], BF16)
            WQB = sb0("WQB", [128, 8 * 512], BF16)
            WQR = sb0("WQR", [128, 8 * 32], BF16)
            WKR = sb0("WKR", [128, 32], BF16)
            sbr = lambda n, s, d: p0.enter_context(nc.sbuf_tensor(n, s, d, side="right"))
            COS = sbr("COS", [DR // 2, T], BF16)
            SIN = sbr("SIN", [DR // 2, T], BF16)
            SROPE = sbr("SROPE", [16, 512], BF16)
            RT1 = sbr("RT1", [DR // 2, 128], BF16)
            RT2 = sbr("RT2", [DR // 2, 128], BF16)
            pT = [ps0(f"pT{i}", [128, 1024], BF16) for i in range(2)]
            pCK = ps0("pCK", [128, 128], F32)
            pK = ps0("pK", [128, 512], F32)
            pV = ps0("pV", [128, 512], F32)
            pQa = ps0("pQa", [128, 128], F32)
            pQb = ps0("pQb", [128, 128], F32)
            pROPE = ps0("pROPE", [16, 512], F32)

            t_xhb = tap("xhb", [128, D], BF16)
            t_kt0 = tap("kt0", [128, T], BF16)
            t_qt0 = tap("qt0", [128, T], BF16)
            t_kh = tap("kh", [112, T], BF16)
            t_qh = tap("qh", [112, T], BF16)
            t_vs = tap("vs", [128, NT * NHC * 65], BF16)

            def p0_walk(E):
                E.op("vector", lambda v: v.memset(ONE1[:], 1.0))
                E.op("vector", lambda v: v.memset(C_SCALE[:], SCALE))
                E.op("vector", lambda v: v.memset(C_MEXP[:], -MEXP))
                E.op("vector", lambda v: v.memset(C_HALF[:], -0.5))
                E.op("vector", lambda v: v.memset(C_S127[:], 1.0 / 127.0))
                E.op("vector", lambda v: v.memset(C_S127SQ[:], S127SQ))
                E.op("vector", lambda v: v.memset(KH[0:32, :], 0.0))
                E.op("vector", lambda v: v.memset(KH[32:64, :], 0.0))
                E.op("vector", lambda v: v.memset(KH[64:96, :], 0.0))
                E.op("vector", lambda v: v.memset(QH[0:32, :], 0.0))
                E.op("vector", lambda v: v.memset(QH[32:64, :], 0.0))
                E.op("vector", lambda v: v.memset(QH[64:96, :], 0.0), inc=ve)

                E.dma("gpsimd", lambda g: g.dma_start(out=COS[:], in_=cosT), inc=dG)
                E.dma("gpsimd", lambda g: g.dma_start(out=SIN[:], in_=sinT), inc=dG)
                E.dma("sync", lambda s: s.dma_start(out=EYEF[:], in_=eye), inc=dL)
                E.dma("gpsimd", lambda g: g.dma_start(out=EYEB[:], in_=eye), inc=dG)
                E.dma("gpsimd", lambda g: g.dma_start(out=MSK[:], in_=masks), inc=dG)
                E.dma("gpsimd", lambda g: g.dma_start(
                    out=WKVB[:].rearrange("p (c j) -> p c j", j=128),
                    in_=wkv.rearrange("(c p) j -> p c j", p=128)), inc=dG)
                E.dma("gpsimd", lambda g: g.dma_start(out=WKUP[:], in_=wkup), inc=dG)
                E.dma("gpsimd", lambda g: g.dma_start(out=WVUP[:], in_=wvup), inc=dG)
                E.dma("gpsimd", lambda g: g.dma_start(
                    out=WQB[:].rearrange("p (c j) -> p c j", j=512),
                    in_=wq.rearrange("(c p) j -> p c j", p=128)), inc=dG)
                E.dma("gpsimd", lambda g: g.dma_start(
                    out=WQR[:].rearrange("p (c j) -> p c j", j=32),
                    in_=wqr.rearrange("(c p) j -> p c j", p=128)), inc=dG)
                E.dma("gpsimd", lambda g: g.dma_start(out=WKR[:], in_=wkr), inc=dG)
                dl_w = dG.v

                st = {
                    "apply": [0, 0],
                    "peT": [0, 0],
                    "xht": [0, 0],
                    "rstd": [0, 0],
                    "ckmm": [0, 0],
                    "ckt": [0, 0],
                    "kvq_pe": [0, 0],
                    "kevac": 0,
                    "q_evac": [0, 0],
                    "srope": 0,
                    "rope_ve": 0,
                    "xhb_tap": 0,
                }

                def stage_ln(t):
                    buf = t % 2
                    E.wait("sync", ve, st["apply"][buf])
                    E.dma("sync", lambda s, _t=t, _b=buf: s.dma_start(
                        out=XT[_b][:], in_=x_seq[128 * _t:128 * _t + 128, :]), inc=dX[buf])
                    dl_x = dX[buf].v
                    E.wait("vector", dX[buf], dl_x)
                    if t == 0:
                        E.wait("vector", dG, dl_w)
                    E.op("vector", lambda v, _b=buf: v.bn_stats(out=BNS[_b][:, 0:6], in_=XT[_b][:, 0:512]))
                    E.op("vector", lambda v, _b=buf: v.bn_stats(out=BNS[_b][:, 6:12], in_=XT[_b][:, 512:1024]), inc=ve)
                    E.wait("vector", ve)
                    E.op("vector", lambda v, _b=buf: v.bn_aggr(out=BNA[_b][:], in_=BNS[_b][:]), inc=ve)
                    E.wait("vector", ve)
                    E.op("vector", lambda v, _b=buf: v.tensor_scalar_add(
                        out=VAR[_b][:], in0=BNA[_b][:, 1:2], scalar1=1e-5), inc=ve)
                    E.wait("scalar", ve)
                    E.op("scalar", lambda s, _b=buf: s.activation(out=RSTD[_b][:], in_=VAR[_b][:], func=AF.Ln), inc=se)
                    E.wait("scalar", se)
                    E.op("scalar", lambda s, _b=buf: s.activation(out=RSTD[_b][:], in_=RSTD[_b][:],
                                                                  func=AF.Exp, scale=C_HALF[:]), inc=se)
                    st["rstd"][buf] = se.v

                def stage_T(t):
                    buf = t % 2
                    E.wait("vector", se, st["rstd"][buf])
                    E.wait("vector", pe, st["peT"][buf])
                    if st["xhb_tap"]:
                        E.wait("vector", dS, st["xhb_tap"])
                    E.op("vector", lambda v, _b=buf: v.tensor_scalar(
                        out=XHB[_b][:], in0=XT[_b][:], scalar1=BNA[_b][:, 0:1], scalar2=RSTD[_b][:],
                        op0=ALU.subtract, op1=ALU.mult), inc=ve)
                    st["apply"][buf] = ve.v
                    if t_xhb is not None and t == 0:
                        E.wait("sync", ve, st["apply"][0])
                        E.dma("sync", lambda s: s.dma_start(out=t_xhb, in_=XHB[0][:]), inc=dS)
                        st["xhb_tap"] = dS.v
                    E.wait("tensor", ve, st["apply"][buf])
                    E.wait("tensor", se, st["xht"][buf])
                    for c in range(8):
                        E.op("tensor", lambda tt, _c=c, _b=buf: tt.transpose(
                            out=pT[_b][:, 128 * _c:128 * _c + 128],
                            in_=XHB[_b][:, 128 * _c:128 * _c + 128],
                            identity=EYEB[:]), inc=(pe if c == 7 else None))
                    st["peT"][buf] = pe.v
                    E.wait("scalar", pe, st["peT"][buf])
                    E.op("scalar", lambda s, _b=buf: s.copy(out=XHT[_b][:], in_=pT[_b][:]), inc=se)
                    st["xht"][buf] = se.v

                def stage_ck(t):
                    buf = t % 2
                    E.wait("tensor", se, st["xht"][buf])
                    E.wait("tensor", ve, st["ckt"][buf])
                    for c in range(8):
                        E.op("tensor", lambda tt, _c=c, _b=buf: tt.matmul(
                            out=pCK[:], lhsT=WKVB[:, 128 * _c:128 * _c + 128],
                            rhs=XHT[_b][:, 128 * _c:128 * _c + 128],
                            start=(_c == 0), stop=(_c == 7)), inc=(pe if c == 7 else None))
                    st["ckmm"][buf] = pe.v

                def stage_ckevac(t):
                    buf = t % 2
                    E.wait("vector", pe, st["ckmm"][buf])
                    E.op("vector", lambda v, _b=buf: v.tensor_copy(out=CKT[_b][:], in_=pCK[:]), inc=ve)
                    st["ckt"][buf] = ve.v

                def stage_kvq(t):
                    buf = t % 2
                    E.wait("tensor", ve, st["ckt"][buf])
                    E.wait("tensor", se, st["kevac"])
                    for c in range(4):
                        E.op("tensor", lambda tt, _c=c, _b=buf: tt.matmul(
                            out=pK[:, 128 * _c:128 * _c + 128],
                            lhsT=WKUP[:, 128 * _c:128 * _c + 128], rhs=CKT[_b][:],
                            start=True, stop=True), inc=(pe if c == 3 else None))
                    pe_k = pe.v
                    E.wait("tensor", se, st["srope"])
                    E.op("tensor", lambda tt, _b=buf: tt.matmul(out=pROPE[:, 0:128], lhsT=WKR[:, 0:16],
                                                                rhs=CKT[_b][:], start=True, stop=True))
                    E.op("tensor", lambda tt, _b=buf: tt.matmul(out=pROPE[:, 128:256], lhsT=WKR[:, 16:32],
                                                                rhs=CKT[_b][:], start=True, stop=True), inc=pe)
                    E.op("tensor", lambda tt, _b=buf: tt.matmul(out=pV[:], lhsT=CKT[_b][:], rhs=WVUP[:],
                                                                start=True, stop=True), inc=pe)
                    pe_v = pe.v
                    for c in range(4):
                        pq = pQa if c % 2 == 0 else pQb
                        E.wait("tensor", se, st["q_evac"][c % 2])
                        for kk in range(8):
                            E.op("tensor", lambda tt, _c=c, _k=kk, _pq=pq, _b=buf: tt.matmul(
                                out=_pq[:],
                                lhsT=WQB[:, 512 * _k + 128 * _c:512 * _k + 128 * _c + 128],
                                rhs=XHT[_b][:, 128 * _k:128 * _k + 128],
                                start=(_k == 0), stop=(_k == 7)), inc=(pe if kk == 7 else None))
                        E.wait("scalar", pe)
                        E.op("scalar", lambda s, _c=c, _t=t, _pq=pq: s.copy(
                            out=QT[_c][:, 128 * _t:128 * _t + 128], in_=_pq[:]), inc=se)
                        st["q_evac"][c % 2] = se.v
                    for kk in range(8):
                        E.op("tensor", lambda tt, _k=kk, _b=buf: tt.matmul(
                            out=pROPE[:, 256:384], lhsT=WQR[:, 32 * _k:32 * _k + 16],
                            rhs=XHT[_b][:, 128 * _k:128 * _k + 128],
                            start=(_k == 0), stop=(_k == 7)))
                    for kk in range(8):
                        E.op("tensor", lambda tt, _k=kk, _b=buf: tt.matmul(
                            out=pROPE[:, 384:512], lhsT=WQR[:, 32 * _k + 16:32 * _k + 32],
                            rhs=XHT[_b][:, 128 * _k:128 * _k + 128],
                            start=(_k == 0), stop=(_k == 7)), inc=(pe if kk == 7 else None))
                    pe_rope = pe.v
                    st["kvq_pe"][buf] = pe.v
                    E.wait("scalar", pe, pe_k)
                    for c in range(4):
                        E.op("scalar", lambda s, _c=c, _t=t: s.copy(
                            out=KT[_c][:, 128 * _t:128 * _t + 128],
                            in_=pK[:, 128 * _c:128 * _c + 128]), inc=(se if c == 3 else None))
                    st["kevac"] = se.v
                    E.wait("scalar", pe, pe_v)
                    E.op("scalar", lambda s, _t=t: s.copy(
                        out=VS[:].rearrange("p (u h c) -> p u h c", h=NHC, c=65)[:, _t, :, 0:64],
                        in_=pV[:].rearrange("p (h c) -> p h c", c=64)), inc=se)
                    E.op("vector", lambda v, _t=t: v.memset(
                        VS[:].rearrange("p (u h c) -> p u h c", h=NHC, c=65)[:, _t, :, 64:65],
                        1.0), inc=ve)
                    E.wait("scalar", pe, pe_rope)
                    E.wait("scalar", ve, st["rope_ve"])
                    E.op("scalar", lambda s: s.copy(out=SROPE[:], in_=pROPE[:]), inc=se)
                    st["srope"] = se.v
                    E.wait("vector", se, st["srope"])
                    cs = slice(128 * t, 128 * t + 128)
                    for (x1o, x2o, OUTT) in ((0, 128, KH), (256, 384, QH)):
                        E.wait("vector", ve)
                        E.op("vector", lambda v, _a=x1o, _c=cs: v.tensor_mul(
                            out=RT1[:], in0=SROPE[:, _a:_a + 128], in1=COS[:, _c]))
                        E.op("vector", lambda v, _b2=x2o, _c=cs: v.tensor_mul(
                            out=RT2[:], in0=SROPE[:, _b2:_b2 + 128], in1=SIN[:, _c]), inc=ve)
                        E.wait("vector", ve)
                        E.op("vector", lambda v, _O=OUTT, _c=cs: v.tensor_sub(
                            out=_O[0:16, _c], in0=RT1[:], in1=RT2[:]), inc=ve)
                        E.wait("vector", ve)
                        E.op("vector", lambda v, _b2=x2o, _c=cs: v.tensor_mul(
                            out=RT1[:], in0=SROPE[:, _b2:_b2 + 128], in1=COS[:, _c]))
                        E.op("vector", lambda v, _a=x1o, _c=cs: v.tensor_mul(
                            out=RT2[:], in0=SROPE[:, _a:_a + 128], in1=SIN[:, _c]), inc=ve)
                        E.wait("vector", ve)
                        E.op("vector", lambda v, _O=OUTT, _c=cs: v.tensor_add(
                            out=_O[96:112, _c], in0=RT1[:], in1=RT2[:]), inc=ve)
                    st["rope_ve"] = ve.v

                for i in range(NT + 2):
                    if i < NT:
                        stage_ln(i)
                    if 1 <= i <= NT:
                        stage_T(i - 1)
                    if i >= 2:
                        stage_kvq(i - 2)
                    if 1 <= i <= NT:
                        stage_ck(i - 1)
                        stage_ckevac(i - 1)

                if t_kt0 is not None:
                    E.wait("sync", ve)
                    E.wait("sync", pe)
                    E.wait("sync", gp)
                    E.wait("sync", se)
                    E.dma("sync", lambda s: s.dma_start(out=t_kt0, in_=KT[0][:]), inc=dS)
                    E.dma("sync", lambda s: s.dma_start(out=t_qt0, in_=QT[0][:]), inc=dS)
                    if t_kh is not None:
                        E.dma("sync", lambda s: s.dma_start(out=t_kh, in_=KH[:]), inc=dS)
                    if t_qh is not None:
                        E.dma("sync", lambda s: s.dma_start(out=t_qh, in_=QH[:]), inc=dS)
                    E.dma("sync", lambda s: s.dma_start(out=t_vs, in_=VS[:]), inc=dS)
                    E.wait("sync", dS)

                for eng in ENGS:
                    E.wait_all(eng)

            run_block(nc, sems, p0_walk)

        if phases == 1:
            return nc, tap_out

        with ExitStack() as p1:
            sb1 = lambda n, s, d: p1.enter_context(nc.sbuf_tensor(n, s, d, side="right"))
            ps1 = lambda n, s, d: p1.enter_context(nc.psum_tensor(n, s, d))
            KH2 = [KH, sb1("KHb", [112, T], BF16)]
            QH2 = [QH, sb1("QHb", [112, T], BF16)]
            PB = [sb1(f"PB{i}", [128, 2 * 512], BF16) for i in range(3)]
            RECB = sb1("RECB", [1, 512], BF16)
            BCS = sb1("BCS", [64, 512], F32)
            S_ = [ps1(f"S{i}", [128, 2 * 512], F32) for i in range(3)]
            AVp = ps1("AVp", [128, 1024], F32)

            t_col0 = tap("col0", [128, T], BF16)
            t_col1 = tap("col1", [128, T], BF16)

            def p1_walk(E):
                E.op("vector", lambda v: v.tensor_copy(out=KH2[1][0:32, :], in_=KH2[0][0:32, :]))
                E.op("vector", lambda v: v.tensor_copy(out=KH2[1][96:112, :], in_=KH2[0][96:112, :]))
                E.op("vector", lambda v: v.tensor_copy(out=QH2[1][0:32, :], in_=QH2[0][0:32, :]))
                E.op("vector", lambda v: v.tensor_copy(out=QH2[1][96:112, :], in_=QH2[0][96:112, :]), inc=ve)

                div_done = [0, 0]
                last_exp = [0, 0, 0]
                asm_pe_gate = [0, 0]
                asm_done = [0, 0]
                qt_par = 0

                def assemble(h):
                    hb = h % 2
                    c, r = h // 2, 64 * (h % 2)
                    E.wait("vector", pe, asm_pe_gate[hb])
                    for (DST, SRC) in ((KH2[hb], KT[c]), (QH2[hb], QT[c])):
                        E.op("vector", lambda v, _D=DST, _S=SRC, _r=r: v.tensor_copy(
                            out=_D[32:64, :], in_=_S[_r:_r + 32, :]))
                        E.op("vector", lambda v, _D=DST, _S=SRC, _r=r: v.tensor_copy(
                            out=_D[64:96, :], in_=_S[_r + 32:_r + 64, :]))
                    E.op("vector", lambda v: v.engine_nop(), inc=ve)
                    asm_done[hb] = ve.v

                assemble(0)
                for h in range(NHC):
                    hb = h % 2
                    KHc, QHc = KH2[hb], QH2[hb]
                    if h + 1 < NHC:
                        assemble(h + 1)
                    gctr = 0
                    for qt in range(NQT):
                        ah = qt_par = (h * NQT + qt) % 2
                        av0 = 512 * ah
                        nk = 4 * qt + 4
                        groups = [list(range(g, min(g + 2, nk))) for g in range(0, nk, 2)]
                        ng = len(groups)
                        gpar = [(gctr + i) % 3 for i in range(ng)]
                        gctr = (gctr + ng) % 3
                        pe_sc = {}
                        av_cnt = 0

                        def emit_scores(gi):
                            grp = groups[gi]
                            par = gpar[gi]
                            E.wait("tensor", ve, asm_done[hb])
                            E.wait("tensor", se, last_exp[par])
                            for j, kt in enumerate(grp):
                                diag = kt >= 4 * qt
                                last_j = (j == len(grp) - 1)
                                E.op("tensor", lambda tt, _kt=kt, _qt=qt, _p=par, _j=j, _d=diag, _K=KHc, _Q=QHc: tt.matmul(
                                    out=S_[_p][:, 512 * _j:512 * _j + 512],
                                    lhsT=_K[:, 128 * _kt:128 * _kt + 128],
                                    rhs=_Q[:, 512 * _qt:512 * _qt + 512],
                                    start=True, stop=(not _d)),
                                    inc=(pe if (last_j and not diag) else None))
                                if diag:
                                    dlt = kt - 4 * qt
                                    E.op("tensor", lambda tt, _p=par, _j=j, _dl=dlt: tt.matmul(
                                        out=S_[_p][:, 512 * _j:512 * _j + 512],
                                        lhsT=EYEB[:], rhs=MSK[:, 512 * _dl:512 * _dl + 512],
                                        start=False, stop=True),
                                        inc=(pe if last_j else None))
                            pe_sc[gi] = pe.v

                        def emit_exp(gi):
                            grp = groups[gi]
                            par = gpar[gi]
                            E.wait("scalar", pe, pe_sc[gi])
                            E.op("scalar", lambda s, _p=par, _n=len(grp): s.activation(
                                out=PB[_p][:, 0:512 * _n], in_=S_[_p][:, 0:512 * _n],
                                func=AF.Exp, scale=C_SCALE[:], bias=C_MEXP[:]), inc=se)
                            last_exp[par] = se.v

                        def emit_av(gi):
                            nonlocal av_cnt
                            grp = groups[gi]
                            par = gpar[gi]
                            E.wait("tensor", se, last_exp[par])
                            if gi == 0:
                                E.wait("tensor", ve, div_done[ah])
                            for j, kt in enumerate(grp):
                                first = (av_cnt == 0)
                                last = (av_cnt == nk - 1)
                                av_cnt += 1
                                E.op("tensor", lambda tt, _kt=kt, _h=h, _p=par, _j=j, _f=first, _l=last, _a=av0: tt.matmul(
                                    out=AVp[0:65, _a:_a + 512],
                                    lhsT=VS[:, 520 * _kt + 65 * _h:520 * _kt + 65 * _h + 65],
                                    rhs=PB[_p][:, 512 * _j:512 * _j + 512],
                                    start=_f, stop=_l), inc=(pe if last else None))

                        emit_scores(0)
                        emit_exp(0)
                        if ng > 1:
                            emit_scores(1)
                            emit_exp(1)
                        for gi in range(2, ng):
                            emit_scores(gi)
                            emit_av(gi - 2)
                            emit_exp(gi)
                        if ng > 1:
                            emit_av(ng - 2)
                        emit_av(ng - 1)
                        pe_av = pe.v
                        E.wait("vector", pe, pe_av)

                        def _recip_bf16(v, a):
                            with nc.allow_low_precision(reason="bf16 store of fp32 reciprocal"):
                                return v.reciprocal(out=RECB[:], in_=AVp[64:65, a:a + 512])
                        E.op("vector", lambda v, _a=av0: _recip_bf16(v, _a), inc=ve)
                        E.wait("tensor", ve)
                        E.op("tensor", lambda tt, _a=av0: tt.matmul(
                            out=AVp[64:128, _a:_a + 512], lhsT=ONE1[:], rhs=RECB[:],
                            start=True, stop=True), inc=pe)
                        E.wait("vector", pe)
                        E.op("vector", lambda v, _a=av0: v.tensor_copy(
                            out=BCS[:], in_=AVp[64:128, _a:_a + 512]), inc=ve)
                        E.wait("vector", ve)
                        E.op("vector", lambda v, _h=h, _qt=qt, _a=av0: v.scalar_tensor_tensor(
                            out=COL[_h // 2][64 * (_h % 2):64 * (_h % 2) + 64,
                                             512 * _qt:512 * _qt + 512],
                            in0=AVp[0:64, _a:_a + 512], scalar=1.0, in1=BCS[:],
                            op0=ALU.mult, op1=ALU.mult), inc=ve)
                        div_done[ah] = ve.v
                    asm_pe_gate[hb] = pe.v

                if t_col0 is not None:
                    E.wait("sync", ve)
                    E.dma("sync", lambda s: s.dma_start(out=t_col0, in_=COL[0][:]), inc=dS)
                    if t_col1 is not None:
                        E.dma("sync", lambda s: s.dma_start(out=t_col1, in_=COL[1][:]), inc=dS)
                    E.wait("sync", dS)

                for eng in ENGS:
                    E.wait_all(eng)

            run_block(nc, sems, p1_walk)

        if phases == 2:
            return nc, tap_out

        S01.close()

        S23 = ExitStack()
        X2T = [S23.enter_context(nc.sbuf_tensor(f"X2T{i}", [128, OWN], BF16, side="right"))
               for i in range(8)]
        WGU = [S23.enter_context(nc.sbuf_tensor(f"WGU{i}", [128, 2 * 1024], BF16, side="right"))
               for i in range(2)]
        with ExitStack() as p2:
            sb2 = lambda n, s, d: p2.enter_context(nc.sbuf_tensor(n, s, d, side="right"))
            ps2 = lambda n, s, d: p2.enter_context(nc.psum_tensor(n, s, d))
            WOB = sb2("WOB", [128, 4 * 1024], BF16)
            PRT = [sb2(f"PRT{i}", [128, D], BF16) for i in range(2)]
            XO = [sb2(f"XO{i}", [128, D], F32) for i in range(2)]
            AT = [sb2(f"AT{i}", [128, D], BF16) for i in range(2)]
            R1 = sb2("R1", [128, D], F32)
            X2B = sb2("X2B", [128, D], BF16)
            VAR2 = sb2("VAR2", [128, 1], F32)
            RSTD2 = sb2("RSTD2", [128, 1], F32)
            BNS2 = sb2("BNS2", [128, 12], F32)
            BNA2 = sb2("BNA2", [128, 2], F32)
            pWO = [ps2(f"pWO{i}", [128, 1024], F32) for i in range(2)]
            pT2 = ps2("pT2", [128, 1024], BF16)

            t_r1 = tap("r1", [OWN, D], F32)
            t_x2 = tap("x2", [128, OWN], BF16)

            def p2_walk2(E):
                p2_state = {}
                for fc in range(4):
                    E.dma("gpsimd", lambda g, _f=fc: g.dma_start(
                        out=WOB[:, 1024 * _f:1024 * _f + 1024],
                        in_=wo[128 * _f:128 * _f + 128, :]), inc=dG)
                dl_wo = dG.v
                prt_free = [0, 0]
                evac_done = [0, 0]
                tile_order = [t for t in range(NT) if (t % 16) // 8 == 0] + \
                             [t for t in range(NT) if (t % 16) // 8 == 1]
                chunk0_done = [0, 0]
                for ti, t in enumerate(tile_order):
                    buf = ti % 2
                    E.wait("tensor", dG, dl_wo)
                    E.wait("tensor", ve, evac_done[buf])
                    for half in range(2):
                        for fc in range(4):
                            E.op("tensor", lambda t_, _f=fc, _t=t, _h=half, _b=buf: t_.matmul(
                                out=pWO[_b][:, 512 * _h:512 * _h + 512],
                                lhsT=COL[_f][:, 128 * _t:128 * _t + 128],
                                rhs=WOB[:, 1024 * _f + 512 * _h:1024 * _f + 512 * _h + 512],
                                start=(_f == 0), stop=(_f == 3)),
                                inc=(pe if (half == 1 and fc == 3) else None))
                    pe_wo = pe.v
                    E.wait("vector", pe, pe_wo)
                    E.wait("vector", dP[buf], prt_free[buf])
                    E.op("vector", lambda v, _b=buf: v.tensor_copy(out=PRT[_b][:], in_=pWO[_b][:]), inc=ve)
                    evac_done[buf] = ve.v
                    srow = 2048 * ((t % 16) // 8) + 1024 * (t // 16) + 128 * (t % 8)
                    E.wait("sync", ve, evac_done[buf])
                    E.dma("sync", lambda s, _r=srow, _b=buf: s.dma_start(
                        out=snd.ap()[_r:_r + 128, :], in_=PRT[_b][:]), inc=dP[buf])
                    prt_free[buf] = dP[buf].v
                    if ti == 15:
                        chunk0_done = [dP[0].v, dP[1].v]
                        E.wait("gpsimd", dP[0], chunk0_done[0])
                        E.wait("gpsimd", dP[1], chunk0_done[1])
                        E.op("gpsimd", lambda g: g.collective_compute(
                            "ReduceScatter", ALU.add,
                            replica_groups=[[0, 1], [2, 3], [4, 5], [6, 7]],
                            ins=[snd.ap()[0:2048, :].opt()],
                            outs=[rcv.ap()[0:1024, :].opt()]), inc=cc)
                E.wait("gpsimd", dP[0])
                E.wait("gpsimd", dP[1])
                E.op("gpsimd", lambda g: g.collective_compute(
                    "ReduceScatter", ALU.add,
                    replica_groups=[[0, 1], [2, 3], [4, 5], [6, 7]],
                    ins=[snd.ap()[2048:4096, :].opt()],
                    outs=[rcv.ap()[1024:2048, :].opt()]), inc=cc2)
                E.dma("gpsimd", lambda g: g.dma_start(
                    out=WGU[0][:, 0:1024].rearrange("p (c j) -> p c j", j=128),
                    in_=wgT.rearrange("(c p) j -> p c j", p=128)[:, :, 0:128]), inc=dW[0])
                E.dma("gpsimd", lambda g: g.dma_start(
                    out=WGU[0][:, 1024:2048].rearrange("p (c j) -> p c j", j=128),
                    in_=wuT.rearrange("(c p) j -> p c j", p=128)[:, :, 0:128]), inc=dW[0])

                r1_done = 0
                xo_free = [0, 0]
                at_free = [0, 0]
                pe_t2 = 0
                for t in range(16):
                    buf = t % 2
                    E.wait("sync", ve, xo_free[buf])
                    E.dma("sync", lambda s, _t=t, _b=buf: s.dma_start(
                        out=XO[_b][:], in_=x_own[128 * _t:128 * _t + 128, :]), inc=dX[buf])
                    E.wait("sync", cc if t < 8 else cc2)
                    E.wait("sync", ve, at_free[buf])
                    E.dma("sync", lambda s, _t=t, _b=buf: s.dma_start(
                        out=AT[_b][:], in_=rcv.ap()[128 * _t:128 * _t + 128, :]), inc=dA[buf])
                    E.wait("vector", dX[buf])
                    E.wait("vector", dA[buf])
                    E.wait("vector", dS, r1_done)
                    E.op("vector", lambda v, _b=buf: v.tensor_add(out=R1[:], in0=XO[_b][:], in1=AT[_b][:]), inc=ve)
                    xo_free[buf] = ve.v
                    at_free[buf] = ve.v
                    E.wait("sync", ve)
                    E.dma("sync", lambda s, _t=t: s.dma_start(
                        out=r1spill.ap()[128 * _t:128 * _t + 128, :], in_=R1[:]), inc=dS)
                    r1_done = dS.v
                    if t_r1 is not None:
                        E.dma("sync", lambda s, _t=t: s.dma_start(
                            out=t_r1[128 * _t:128 * _t + 128, :], in_=R1[:]), inc=dS)
                        r1_done = dS.v
                    E.wait("vector", ve)
                    E.op("vector", lambda v: v.bn_stats(out=BNS2[:, 0:6], in_=R1[:, 0:512]))
                    E.op("vector", lambda v: v.bn_stats(out=BNS2[:, 6:12], in_=R1[:, 512:1024]), inc=ve)
                    E.wait("vector", ve)
                    E.op("vector", lambda v: v.bn_aggr(out=BNA2[:], in_=BNS2[:]), inc=ve)
                    E.wait("vector", ve)
                    E.op("vector", lambda v: v.tensor_scalar_add(out=VAR2[:], in0=BNA2[:, 1:2], scalar1=1e-5), inc=ve)
                    E.wait("scalar", ve)
                    E.op("scalar", lambda s: s.activation(out=RSTD2[:], in_=VAR2[:], func=AF.Ln), inc=se)
                    E.wait("scalar", se)
                    E.op("scalar", lambda s: s.activation(out=RSTD2[:], in_=RSTD2[:], func=AF.Exp, scale=C_HALF[:]), inc=se)
                    E.wait("vector", se)
                    E.wait("vector", pe, pe_t2)
                    E.op("vector", lambda v: v.tensor_scalar(
                        out=X2B[:], in0=R1[:], scalar1=BNA2[:, 0:1], scalar2=RSTD2[:],
                        op0=ALU.subtract, op1=ALU.mult), inc=ve)
                    ve_x2 = ve.v
                    E.wait("tensor", ve, ve_x2)
                    for c in range(8):
                        E.op("tensor", lambda t_, _c=c: t_.transpose(
                            out=pT2[:, 128 * _c:128 * _c + 128],
                            in_=X2B[:, 128 * _c:128 * _c + 128],
                            identity=EYEB[:]), inc=(pe if c == 7 else None))
                    pe_t2 = pe.v
                    E.wait("vector", pe, pe_t2)
                    for c in range(8):
                        E.op("vector", lambda v, _c=c, _t=t: v.tensor_copy(
                            out=X2T[_c][:, 128 * _t:128 * _t + 128],
                            in_=pT2[:, 128 * _c:128 * _c + 128]),
                            inc=(ve if c == 7 else None))

                if t_x2 is not None:
                    E.wait("sync", ve)
                    E.dma("sync", lambda s: s.dma_start(out=t_x2, in_=X2T[0][:]), inc=dS)
                    E.wait("sync", dS)

                for eng in ENGS:
                    E.wait_all(eng)

            run_block(nc, sems, p2_walk2)

        if phases == 3:
            return nc, tap_out

        with ExitStack() as p3:
            sb3 = lambda n, s, d: p3.enter_context(nc.sbuf_tensor(n, s, d, side="right"))
            ps3 = lambda n, s, d: p3.enter_context(nc.psum_tensor(n, s, d))
            HB = p3.enter_context(nc.sbuf_tensor("HB", [128, 24 * OWN], BF16, side="left"))

            WDA1 = p3.enter_context(nc.sbuf_tensor("WDA1", [128, 12 * 1024], BF16, side="left"))
            p3a = ExitStack()
            GS = p3a.enter_context(nc.sbuf_tensor("GS", [128, 512], BF16, side="right"))
            US = p3a.enter_context(nc.sbuf_tensor("US", [128, 512], BF16, side="right"))
            pG = p3a.enter_context(nc.psum_tensor("pG", [128, 512], F32))
            pU = p3a.enter_context(nc.psum_tensor("pU", [128, 512], F32))

            dl_wd_early = [0]

            def p3a_walk(E):
                wgu_free = [0, 0]
                silu_done = 0
                umul_done = 0
                for hc in range(12):
                    E.dma("gpsimd", lambda g, _h=hc: g.dma_start(
                        out=WDA1[:, 1024 * _h:1024 * _h + 1024],
                        in_=wdT[128 * _h:128 * _h + 128, :]), inc=dG)
                dl_wd_early[0] = dG.v
                for hc in range(24):
                    wb = hc % 2
                    if hc > 0:
                        E.wait("gpsimd", pe, wgu_free[wb])
                        E.dma("gpsimd", lambda g, _h=hc, _b=wb: g.dma_start(
                            out=WGU[_b][:, 0:1024].rearrange("p (c j) -> p c j", j=128),
                            in_=wgT.rearrange("(c p) j -> p c j", p=128)[:, :, 128 * _h:128 * _h + 128]), inc=dW[wb])
                        E.dma("gpsimd", lambda g, _h=hc, _b=wb: g.dma_start(
                            out=WGU[_b][:, 1024:2048].rearrange("p (c j) -> p c j", j=128),
                            in_=wuT.rearrange("(c p) j -> p c j", p=128)[:, :, 128 * _h:128 * _h + 128]), inc=dW[wb])
                    dl_gu = dW[wb].v
                    for tt in range(4):
                        E.wait("tensor", dW[wb], dl_gu)
                        E.wait("tensor", se, silu_done)
                        for c in range(8):
                            E.op("tensor", lambda t_, _c=c, _t=tt, _b=wb: t_.matmul(
                                out=pG[:], lhsT=WGU[_b][:, 128 * _c:128 * _c + 128],
                                rhs=X2T[_c][:, 512 * _t:512 * _t + 512],
                                start=(_c == 0), stop=(_c == 7)), inc=(pe if c == 7 else None))
                        pe_g = pe.v
                        E.wait("tensor", ve, umul_done)
                        for c in range(8):
                            E.op("tensor", lambda t_, _c=c, _t=tt, _b=wb: t_.matmul(
                                out=pU[:], lhsT=WGU[_b][:, 1024 + 128 * _c:1024 + 128 * _c + 128],
                                rhs=X2T[_c][:, 512 * _t:512 * _t + 512],
                                start=(_c == 0), stop=(_c == 7)), inc=(pe if c == 7 else None))
                        pe_u = pe.v
                        if tt == 3:
                            wgu_free[wb] = pe.v
                        E.wait("scalar", pe, pe_g)
                        E.wait("scalar", ve, umul_done)
                        E.op("scalar", lambda s: s.activation(out=GS[:], in_=pG[:], func=AF.Silu,
                                                              scale=C_S127[:]), inc=se)
                        silu_done = se.v
                        E.wait("vector", pe, pe_u)
                        E.op("vector", lambda v: v.tensor_copy(out=US[:], in_=pU[:]), inc=ve)
                        E.wait("vector", se, silu_done)
                        E.wait("vector", ve)
                        E.op("vector", lambda v, _h=hc, _t=tt: v.tensor_mul(
                            out=HB[:, OWN * _h + 512 * _t:OWN * _h + 512 * _t + 512],
                            in0=GS[:], in1=US[:]), inc=ve)
                        umul_done = ve.v
                for eng in ENGS:
                    E.wait_all(eng)

            run_block(nc, sems, p3a_walk)
            p3a.close()
            S23.close()

            WDA2 = p3.enter_context(nc.sbuf_tensor("WDA2", [128, 12 * 1024], BF16, side="right"))
            sbl = lambda n, s, d: p3.enter_context(nc.sbuf_tensor(n, s, d, side="left"))
            FDT = sbl("FDT", [128, 4096], F32)
            R1T = [sbl(f"R1T{i}", [128, D], F32) for i in range(2)]
            OStg = sbl("OStg", [128, D], F32)
            pD = ps3("pD", [128, 3072], F32)
            pR = ps3("pR", [128, 1024], F32)

            def p3b_walk(E):
                for hc in range(12):
                    E.dma("gpsimd", lambda g, _h=hc: g.dma_start(
                        out=WDA2[:, 1024 * _h:1024 * _h + 1024],
                        in_=wdT[128 * (_h + 12):128 * (_h + 12) + 128, :]), inc=dG)
                dl_wd1 = dl_wd_early[0]
                dl_wd = dG.v

                fdt_free = 0
                r1t_free = [0, 0]
                ost_free = 0
                stt_done = 0
                for tt in range(4):
                    E.wait("tensor", dG, dl_wd1)
                    E.wait("tensor", se, fdt_free)
                    E.wait("tensor", ve, stt_done)
                    for hc in range(24):
                        if hc == 12:
                            E.wait("tensor", dG, dl_wd)
                        WDAx = WDA1 if hc < 12 else WDA2
                        hco = hc % 12
                        for dc in range(8):
                            dst = pD[:, 512 * dc:512 * dc + 512] if dc < 6 else \
                                  pR[:, 512 * (dc - 6):512 * (dc - 6) + 512]
                            E.op("tensor", lambda t_, _h=hc, _ho=hco, _W=WDAx, _d=dc, _t=tt, _dst=dst: t_.matmul(
                                out=_dst,
                                lhsT=_W[:, 1024 * _ho + 128 * _d:1024 * _ho + 128 * _d + 128],
                                rhs=HB[:, OWN * _h + 512 * _t:OWN * _h + 512 * _t + 512],
                                start=(_h == 0), stop=(_h == 23)),
                                inc=(pe if (hc == 23 and dc == 7) else None))
                    pe_d = pe.v
                    E.wait("scalar", pe, pe_d)
                    for dc in range(8):
                        src = pD[:, 512 * dc:512 * dc + 512] if dc < 6 else \
                              pR[:, 512 * (dc - 6):512 * (dc - 6) + 512]
                        E.op("scalar", lambda s, _d=dc, _src=src: s.activation(
                            out=FDT[:, 512 * _d:512 * _d + 512], in_=_src,
                            func=AF.Copy, scale=C_S127SQ[:]),
                            inc=(se if dc == 7 else None))
                    fdt_free = se.v
                    for s4 in range(4):
                        sb_ = s4 % 2
                        E.wait("sync", ve, r1t_free[sb_])
                        E.dma("sync", lambda s, _t=tt, _s=s4, _b=sb_: s.dma_start(
                            out=R1T[_b][:],
                            in_=r1spill.ap()[512 * _t + 128 * _s:512 * _t + 128 * _s + 128, :]),
                            inc=dX[sb_])
                        dl_r1 = (sb_, dX[sb_].v)
                        E.wait("tensor", se, fdt_free)
                        if s4 >= 2:
                            E.wait("tensor", ve)
                        for dc in range(8):
                            E.op("tensor", lambda t_, _d=dc, _s=s4: t_.transpose(
                                out=pD[:, 1024 * (_s % 2) + 128 * _d:1024 * (_s % 2) + 128 * _d + 128],
                                in_=FDT[:, 512 * _d + 128 * _s:512 * _d + 128 * _s + 128],
                                identity=EYEF[:]), inc=(pe if dc == 7 else None))
                        pe_tr = pe.v
                        E.wait("vector", pe, pe_tr)
                        E.wait("vector", dX[dl_r1[0]], dl_r1[1])
                        E.wait("vector", dS, ost_free)
                        E.op("vector", lambda v, _s=s4, _b=sb_: v.scalar_tensor_tensor(
                            out=OStg[:], in0=pD[:, 1024 * (_s % 2):1024 * (_s % 2) + 1024],
                            scalar=1.0, in1=R1T[_b][:],
                            op0=ALU.mult, op1=ALU.add), inc=ve)
                        ve_o = ve.v
                        r1t_free[sb_] = ve_o
                        E.wait("sync", ve, ve_o)
                        E.dma("sync", lambda s, _t=tt, _s=s4: s.dma_start(
                            out=y[512 * _t + 128 * _s:512 * _t + 128 * _s + 128, :],
                            in_=OStg[:]), inc=dS)
                        ost_free = dS.v
                    stt_done = ve.v

                for eng in ENGS:
                    E.wait_all(eng)

            run_block(nc, sems, p3b_walk)

    return nc, tap_out


def _host_inputs(core, inputs):
    b, p = core // 2, core % 2
    h0 = NHC * p * HD
    x = np.ascontiguousarray(np.asarray(inputs["L0_attn_x"])[b])
    out = dict(
        x_seq=x,
        x_own=np.ascontiguousarray(x[OWN * p:OWN * p + OWN]),
        wkv=np.ascontiguousarray(np.asarray(inputs["L0_attn_W_kv_down"])),
        wkup=np.ascontiguousarray(np.asarray(inputs["L0_attn_W_k_up"])[:, h0:h0 + NHC * HD]),
        wvup=np.ascontiguousarray(np.asarray(inputs["L0_attn_W_v_up"])[:, h0:h0 + NHC * HD]),
        wq=np.ascontiguousarray(np.asarray(inputs["L0_attn_W_q"])[:, h0:h0 + NHC * HD]),
        wqr=np.ascontiguousarray(np.asarray(inputs["L0_attn_W_q_rope"])),
        wkr=np.ascontiguousarray(np.asarray(inputs["L0_attn_W_k_rope"])),
        wo=np.ascontiguousarray(np.asarray(inputs["L0_attn_W_o"])[h0:h0 + NHC * HD, :]),
        wgT=np.ascontiguousarray(np.asarray(inputs["L0_ffn_w_gate"]).T),
        wuT=np.ascontiguousarray(np.asarray(inputs["L0_ffn_w_up"]).T),
        wdT=np.ascontiguousarray(np.asarray(inputs["L0_ffn_w_down"]).T),
    )
    inv_freq = 1.0 / 10000.0 ** (np.arange(0, DR, 2, dtype=np.float32) / DR)
    tpos = np.arange(T, dtype=np.float32)
    freqs = np.outer(tpos, inv_freq).astype(np.float32)
    out["cosT"] = np.ascontiguousarray(np.cos(freqs).T.astype(np.float32))
    out["sinT"] = np.ascontiguousarray(np.sin(freqs).T.astype(np.float32))
    out["eye"] = np.eye(128, dtype=np.float32)
    masks_ = np.zeros((128, 4, 512), dtype=np.float32)
    k_loc = np.arange(128)[:, None]
    q_loc = np.arange(512)[None, :]
    for i, dlt in enumerate((0, 128, 256, 384)):
        masks_[:, i, :] = np.where(k_loc + dlt > q_loc, -1e9, 0.0)
    out["masks"] = np.ascontiguousarray(masks_.reshape(128, 4 * 512))
    return out


def kernel(**inputs):
    nc, _ = build()
    in_maps = [_host_inputs(c, inputs) for c in range(8)]
    res = run_bass_kernel_spmd(nc, in_maps, core_ids=list(range(8)))
    out = np.zeros((4, T, D), dtype=np.float32)
    for c in range(8):
        b, p = c // 2, c % 2
        out[b, OWN * p:OWN * p + OWN] = np.asarray(res.results[c]["y"], dtype=np.float32)
    return out


# version 06
# speedup vs baseline: 1.5385x; 1.0049x over previous
import math
from contextlib import ExitStack

import numpy as np
import concourse.bass as bass
import concourse.mybir as mybir
from concourse.bass_utils import run_bass_kernel_spmd

F32 = mybir.dt.float32
BF16 = mybir.dt.bfloat16
I32 = mybir.dt.int32
AF = mybir.ActivationFunctionType
ALU = mybir.AluOpType
AX = mybir.AxisListType

T = 4096
D = 1024
NHC = 8
HD = 64
DC = 128
DR = 32
FF = 3072
NT = T // 128
OWN = 2048
NQT = T // 512
SCALE = 1.0 / math.sqrt(HD + DR)
MEXP = 24.0
S127SQ = (1.0 / 127.0) ** 2

ENGS = ("sync", "gpsimd", "vector", "scalar", "tensor")


class Sem:
    def __init__(self, ctx, nc, name):
        self.h = ctx.enter_context(nc.semaphore(name))
        self.v = 0
        self.name = name


class Emitter:
    def __init__(self, who, eng, sems):
        self.who = who
        self.eng = eng
        self.sems = sems

    def op(self, engine, fn, inc=None, n=1):
        if self.who == engine:
            instr = fn(self.eng)
            if inc is not None:
                instr.then_inc(inc.h, n)
        if inc is not None:
            inc.v += n
        return inc.v if inc is not None else None

    def dma(self, engine, fn, inc):
        return self.op(engine, fn, inc, 16)

    def wait(self, engine, sem, val=None):
        if self.who == engine:
            self.eng.wait_ge(sem.h, sem.v if val is None else val)

    def wait_all(self, engine):
        for s in self.sems:
            if s.v > 0 and self.who == engine:
                self.eng.wait_ge(s.h, s.v)


def run_block(nc, sems, walk):
    with nc.Block() as blk:
        snap = [s.v for s in sems]
        for engname in ENGS:
            def body(eng, _en=engname):
                for s, v in zip(sems, snap):
                    s.v = v
                walk(Emitter(_en, eng, sems))
            getattr(blk, engname)(body)


def build(debug_taps=(), phases=4):
    nc = bass.Bass("TRN2", target_bir_lowering=False, debug=False)

    x_seq = nc.dram_tensor("x_seq", [T, D], F32, kind="ExternalInput").ap()
    x_own = nc.dram_tensor("x_own", [OWN, D], F32, kind="ExternalInput").ap()
    wkv = nc.dram_tensor("wkv", [D, DC], F32, kind="ExternalInput").ap()
    wkup = nc.dram_tensor("wkup", [DC, NHC * HD], F32, kind="ExternalInput").ap()
    wvup = nc.dram_tensor("wvup", [DC, NHC * HD], F32, kind="ExternalInput").ap()
    wq = nc.dram_tensor("wq", [D, NHC * HD], F32, kind="ExternalInput").ap()
    wqr = nc.dram_tensor("wqr", [D, DR], F32, kind="ExternalInput").ap()
    wkr = nc.dram_tensor("wkr", [DC, DR], F32, kind="ExternalInput").ap()
    wo = nc.dram_tensor("wo", [NHC * HD, D], F32, kind="ExternalInput").ap()
    wgT = nc.dram_tensor("wgT", [D, FF], I32, kind="ExternalInput").ap()
    wuT = nc.dram_tensor("wuT", [D, FF], I32, kind="ExternalInput").ap()
    wdT = nc.dram_tensor("wdT", [FF, D], I32, kind="ExternalInput").ap()
    cosT = nc.dram_tensor("cosT", [DR // 2, T], F32, kind="ExternalInput").ap()
    sinT = nc.dram_tensor("sinT", [DR // 2, T], F32, kind="ExternalInput").ap()
    eye = nc.dram_tensor("eye", [128, 128], F32, kind="ExternalInput").ap()
    masks = nc.dram_tensor("masks", [128, 4 * 512], F32, kind="ExternalInput").ap()

    y = nc.dram_tensor("y", [OWN, D], F32, kind="ExternalOutput").ap()

    snd = nc.dram_tensor("snd", [T, D], BF16)
    rcv = nc.dram_tensor("rcv", [OWN, D], BF16)
    r1spill = nc.dram_tensor("r1spill", [OWN, D], F32)

    tap_out = {}

    def tap(name, shape, dtype):
        if name in debug_taps:
            tap_out[name] = nc.dram_tensor("tap_" + name, shape, dtype,
                                           kind="ExternalOutput").ap()
            return tap_out[name]
        return None

    with ExitStack() as ctx:
        sbt = lambda n, s, d: ctx.enter_context(nc.sbuf_tensor(n, s, d))

        COL = [sbt(f"COL{i}", [128, T], BF16) for i in range(4)]
        S01 = ExitStack()
        s01t = lambda n, s, d: S01.enter_context(nc.sbuf_tensor(n, s, d, side="right"))
        KT = [s01t(f"KT{i}", [128, T], BF16) for i in range(4)]
        QT = [s01t(f"QT{i}", [128, T], BF16) for i in range(4)]
        KH = s01t("KH", [112, T], BF16)
        QH = s01t("QH", [112, T], BF16)
        MSK = s01t("MSK", [128, 4 * 512], BF16)
        VS = s01t("VS", [128, NT * (NHC * 65)], BF16)
        EYEB = sbt("EYEB", [128, 128], BF16)
        EYEF = sbt("EYEF", [128, 128], F32)
        ONE1 = sbt("ONE1", [1, 64], BF16)
        C_SCALE = sbt("C_SCALE", [128, 1], F32)
        C_MEXP = sbt("C_MEXP", [128, 1], F32)
        C_HALF = sbt("C_HALF", [128, 1], F32)
        C_S127 = sbt("C_S127", [128, 1], F32)
        C_S127SQ = sbt("C_S127SQ", [128, 1], F32)
        C_EPS = sbt("C_EPS", [128, 1], F32)

        sems = []

        def mk(name):
            s = Sem(ctx, nc, name)
            sems.append(s)
            return s

        dL = mk("dL")
        dG = mk("dG")
        dS = mk("dS")
        dX = [mk("dXa"), mk("dXb"), mk("dXc")]
        dA = [mk("dAa"), mk("dAb")]
        dP = [mk("dPa"), mk("dPb")]
        dW = [mk("dWa"), mk("dWb")]
        pe = mk("pe")
        ve = mk("ve")
        se = mk("se")
        cc = mk("cc")
        cc2 = mk("cc2")
        gp = mk("gp")

        with ExitStack() as p0:
            sb0 = lambda n, s, d: p0.enter_context(nc.sbuf_tensor(n, s, d, side="right"))
            ps0 = lambda n, s, d: p0.enter_context(nc.psum_tensor(n, s, d))
            XT = [sb0(f"XT{i}", [128, D], F32) for i in range(3)]
            XHB = [sb0(f"XHB{i}", [128, D], BF16) for i in range(2)]
            XHT = [sb0(f"XHT{i}", [128, D], BF16) for i in range(2)]
            CKT = [sb0(f"CKT{i}", [128, 128], BF16) for i in range(2)]
            VAR = [sb0(f"VAR{i}", [128, 1], F32) for i in range(2)]
            RSTD = [sb0(f"RSTD{i}", [128, 1], F32) for i in range(2)]
            BNS = [sb0(f"BNS{i}", [128, 12], F32) for i in range(2)]
            BNA = [sb0(f"BNA{i}", [128, 2], F32) for i in range(2)]
            WKVB = sb0("WKVB", [128, 8 * 128], BF16)
            WKUP = sb0("WKUP", [128, 512], BF16)
            WVUP = sb0("WVUP", [128, 512], BF16)
            WQB = sb0("WQB", [128, 8 * 512], BF16)
            WQR = sb0("WQR", [128, 8 * 32], BF16)
            WKR = sb0("WKR", [128, 32], BF16)
            sbr = lambda n, s, d: p0.enter_context(nc.sbuf_tensor(n, s, d, side="right"))
            COS = sbr("COS", [DR // 2, T], BF16)
            SIN = sbr("SIN", [DR // 2, T], BF16)
            SROPE = sbr("SROPE", [16, 512], BF16)
            RT1 = sbr("RT1", [DR // 2, 128], BF16)
            RT2 = sbr("RT2", [DR // 2, 128], BF16)
            pT = [ps0(f"pT{i}", [128, 1024], BF16) for i in range(2)]
            pCK = ps0("pCK", [128, 128], F32)
            pK = ps0("pK", [128, 512], F32)
            pV = ps0("pV", [128, 512], F32)
            pQa = ps0("pQa", [128, 128], F32)
            pQb = ps0("pQb", [128, 128], F32)
            pROPE = ps0("pROPE", [16, 512], F32)

            t_xhb = tap("xhb", [128, D], BF16)
            t_kt0 = tap("kt0", [128, T], BF16)
            t_qt0 = tap("qt0", [128, T], BF16)
            t_kh = tap("kh", [112, T], BF16)
            t_qh = tap("qh", [112, T], BF16)
            t_vs = tap("vs", [128, NT * NHC * 65], BF16)

            def p0_walk(E):
                E.op("vector", lambda v: v.memset(ONE1[:], 1.0))
                E.op("vector", lambda v: v.memset(C_SCALE[:], SCALE))
                E.op("vector", lambda v: v.memset(C_MEXP[:], -MEXP))
                E.op("vector", lambda v: v.memset(C_HALF[:], -0.5))
                E.op("vector", lambda v: v.memset(C_S127[:], 1.0 / 127.0))
                E.op("vector", lambda v: v.memset(C_S127SQ[:], S127SQ))
                E.op("vector", lambda v: v.memset(C_EPS[:], 1e-5))
                E.op("vector", lambda v: v.memset(KH[0:32, :], 0.0))
                E.op("vector", lambda v: v.memset(KH[32:64, :], 0.0))
                E.op("vector", lambda v: v.memset(KH[64:96, :], 0.0))
                E.op("vector", lambda v: v.memset(QH[0:32, :], 0.0))
                E.op("vector", lambda v: v.memset(QH[32:64, :], 0.0))
                E.op("vector", lambda v: v.memset(QH[64:96, :], 0.0), inc=ve)

                E.dma("gpsimd", lambda g: g.dma_start(out=COS[:], in_=cosT), inc=dG)
                E.dma("gpsimd", lambda g: g.dma_start(out=SIN[:], in_=sinT), inc=dG)
                E.dma("sync", lambda s: s.dma_start(out=EYEF[:], in_=eye), inc=dL)
                E.dma("gpsimd", lambda g: g.dma_start(out=EYEB[:], in_=eye), inc=dG)
                E.dma("gpsimd", lambda g: g.dma_start(out=MSK[:], in_=masks), inc=dG)
                E.dma("gpsimd", lambda g: g.dma_start(
                    out=WKVB[:].rearrange("p (c j) -> p c j", j=128),
                    in_=wkv.rearrange("(c p) j -> p c j", p=128)), inc=dG)
                E.dma("gpsimd", lambda g: g.dma_start(out=WKUP[:], in_=wkup), inc=dG)
                E.dma("gpsimd", lambda g: g.dma_start(out=WVUP[:], in_=wvup), inc=dG)
                E.dma("gpsimd", lambda g: g.dma_start(
                    out=WQB[:].rearrange("p (c j) -> p c j", j=512),
                    in_=wq.rearrange("(c p) j -> p c j", p=128)), inc=dG)
                E.dma("gpsimd", lambda g: g.dma_start(
                    out=WQR[:].rearrange("p (c j) -> p c j", j=32),
                    in_=wqr.rearrange("(c p) j -> p c j", p=128)), inc=dG)
                E.dma("gpsimd", lambda g: g.dma_start(out=WKR[:], in_=wkr), inc=dG)
                dl_w = dG.v

                st = {
                    "apply": [0, 0],
                    "peT": [0, 0],
                    "xht": [0, 0],
                    "rstd": [0, 0],
                    "ckmm": [0, 0],
                    "ckt": [0, 0],
                    "kvq_pe": [0, 0],
                    "kevac": 0,
                    "q_evac": [0, 0],
                    "srope": 0,
                    "rope_ve": 0,
                    "xhb_tap": 0,
                    "xload": [0, 0, 0],
                    "apply3": [0, 0, 0],
                }

                def stage_load(t):
                    b3 = t % 3
                    E.wait("sync", ve, st["apply3"][b3])
                    E.dma("sync", lambda s, _t=t, _b=b3: s.dma_start(
                        out=XT[_b][:], in_=x_seq[128 * _t:128 * _t + 128, :]), inc=dX[b3])
                    st["xload"][b3] = dX[b3].v

                def stage_ln(t):
                    buf = t % 2
                    b3 = t % 3
                    E.wait("vector", dX[b3], st["xload"][b3])
                    if t == 0:
                        E.wait("vector", dG, dl_w)
                    E.op("vector", lambda v, _b=buf, _x=b3: v.bn_stats(out=BNS[_b][:, 0:6], in_=XT[_x][:, 0:512]))
                    E.op("vector", lambda v, _b=buf, _x=b3: v.bn_stats(out=BNS[_b][:, 6:12], in_=XT[_x][:, 512:1024]), inc=ve)
                    E.wait("vector", ve)
                    E.op("vector", lambda v, _b=buf: v.bn_aggr(out=BNA[_b][:], in_=BNS[_b][:]), inc=ve)
                    E.wait("scalar", ve)
                    E.op("scalar", lambda s, _b=buf: s.activation(out=RSTD[_b][:], in_=BNA[_b][:, 1:2],
                                                                  func=AF.Ln, bias=C_EPS[:]), inc=se)
                    E.wait("scalar", se)
                    E.op("scalar", lambda s, _b=buf: s.activation(out=RSTD[_b][:], in_=RSTD[_b][:],
                                                                  func=AF.Exp, scale=C_HALF[:]), inc=se)
                    st["rstd"][buf] = se.v

                def stage_T(t):
                    buf = t % 2
                    b3 = t % 3
                    E.wait("vector", se, st["rstd"][buf])
                    E.wait("vector", pe, st["peT"][buf])
                    if st["xhb_tap"]:
                        E.wait("vector", dS, st["xhb_tap"])
                    E.op("vector", lambda v, _b=buf, _x=b3: v.tensor_scalar(
                        out=XHB[_b][:], in0=XT[_x][:], scalar1=BNA[_b][:, 0:1], scalar2=RSTD[_b][:],
                        op0=ALU.subtract, op1=ALU.mult), inc=ve)
                    st["apply3"][b3] = ve.v
                    st["apply"][buf] = ve.v
                    if t_xhb is not None and t == 0:
                        E.wait("sync", ve, st["apply"][0])
                        E.dma("sync", lambda s: s.dma_start(out=t_xhb, in_=XHB[0][:]), inc=dS)
                        st["xhb_tap"] = dS.v
                    E.wait("tensor", ve, st["apply"][buf])
                    E.wait("tensor", se, st["xht"][buf])
                    for c in range(8):
                        E.op("tensor", lambda tt, _c=c, _b=buf: tt.transpose(
                            out=pT[_b][:, 128 * _c:128 * _c + 128],
                            in_=XHB[_b][:, 128 * _c:128 * _c + 128],
                            identity=EYEB[:]), inc=(pe if c == 7 else None))
                    st["peT"][buf] = pe.v
                    E.wait("scalar", pe, st["peT"][buf])
                    E.op("scalar", lambda s, _b=buf: s.copy(out=XHT[_b][:], in_=pT[_b][:]), inc=se)
                    st["xht"][buf] = se.v

                def stage_ck(t):
                    buf = t % 2
                    E.wait("tensor", se, st["xht"][buf])
                    E.wait("tensor", ve, st["ckt"][buf])
                    for c in range(8):
                        E.op("tensor", lambda tt, _c=c, _b=buf: tt.matmul(
                            out=pCK[:], lhsT=WKVB[:, 128 * _c:128 * _c + 128],
                            rhs=XHT[_b][:, 128 * _c:128 * _c + 128],
                            start=(_c == 0), stop=(_c == 7)), inc=(pe if c == 7 else None))
                    st["ckmm"][buf] = pe.v

                def stage_ckevac(t):
                    buf = t % 2
                    E.wait("vector", pe, st["ckmm"][buf])
                    E.op("vector", lambda v, _b=buf: v.tensor_copy(out=CKT[_b][:], in_=pCK[:]), inc=ve)
                    st["ckt"][buf] = ve.v

                def stage_kvq(t):
                    buf = t % 2
                    E.wait("tensor", ve, st["ckt"][buf])
                    E.wait("tensor", se, st["kevac"])
                    for c in range(4):
                        E.op("tensor", lambda tt, _c=c, _b=buf: tt.matmul(
                            out=pK[:, 128 * _c:128 * _c + 128],
                            lhsT=WKUP[:, 128 * _c:128 * _c + 128], rhs=CKT[_b][:],
                            start=True, stop=True), inc=(pe if c == 3 else None))
                    pe_k = pe.v
                    E.wait("tensor", se, st["srope"])
                    E.op("tensor", lambda tt, _b=buf: tt.matmul(out=pROPE[:, 0:128], lhsT=WKR[:, 0:16],
                                                                rhs=CKT[_b][:], start=True, stop=True))
                    E.op("tensor", lambda tt, _b=buf: tt.matmul(out=pROPE[:, 128:256], lhsT=WKR[:, 16:32],
                                                                rhs=CKT[_b][:], start=True, stop=True), inc=pe)
                    E.op("tensor", lambda tt, _b=buf: tt.matmul(out=pV[:], lhsT=CKT[_b][:], rhs=WVUP[:],
                                                                start=True, stop=True), inc=pe)
                    pe_v = pe.v
                    for c in range(4):
                        pq = pQa if c % 2 == 0 else pQb
                        E.wait("tensor", se, st["q_evac"][c % 2])
                        for kk in range(8):
                            E.op("tensor", lambda tt, _c=c, _k=kk, _pq=pq, _b=buf: tt.matmul(
                                out=_pq[:],
                                lhsT=WQB[:, 512 * _k + 128 * _c:512 * _k + 128 * _c + 128],
                                rhs=XHT[_b][:, 128 * _k:128 * _k + 128],
                                start=(_k == 0), stop=(_k == 7)), inc=(pe if kk == 7 else None))
                        E.wait("scalar", pe)
                        E.op("scalar", lambda s, _c=c, _t=t, _pq=pq: s.copy(
                            out=QT[_c][:, 128 * _t:128 * _t + 128], in_=_pq[:]), inc=se)
                        st["q_evac"][c % 2] = se.v
                    for kk in range(8):
                        E.op("tensor", lambda tt, _k=kk, _b=buf: tt.matmul(
                            out=pROPE[:, 256:384], lhsT=WQR[:, 32 * _k:32 * _k + 16],
                            rhs=XHT[_b][:, 128 * _k:128 * _k + 128],
                            start=(_k == 0), stop=(_k == 7)))
                    for kk in range(8):
                        E.op("tensor", lambda tt, _k=kk, _b=buf: tt.matmul(
                            out=pROPE[:, 384:512], lhsT=WQR[:, 32 * _k + 16:32 * _k + 32],
                            rhs=XHT[_b][:, 128 * _k:128 * _k + 128],
                            start=(_k == 0), stop=(_k == 7)), inc=(pe if kk == 7 else None))
                    pe_rope = pe.v
                    st["kvq_pe"][buf] = pe.v
                    E.wait("scalar", pe, pe_k)
                    for c in range(4):
                        E.op("scalar", lambda s, _c=c, _t=t: s.copy(
                            out=KT[_c][:, 128 * _t:128 * _t + 128],
                            in_=pK[:, 128 * _c:128 * _c + 128]), inc=(se if c == 3 else None))
                    st["kevac"] = se.v
                    E.wait("scalar", pe, pe_v)
                    E.op("scalar", lambda s, _t=t: s.copy(
                        out=VS[:].rearrange("p (u h c) -> p u h c", h=NHC, c=65)[:, _t, :, 0:64],
                        in_=pV[:].rearrange("p (h c) -> p h c", c=64)), inc=se)
                    E.op("vector", lambda v, _t=t: v.memset(
                        VS[:].rearrange("p (u h c) -> p u h c", h=NHC, c=65)[:, _t, :, 64:65],
                        1.0), inc=ve)
                    E.wait("scalar", pe, pe_rope)
                    E.wait("scalar", ve, st["rope_ve"])
                    E.op("scalar", lambda s: s.copy(out=SROPE[:], in_=pROPE[:]), inc=se)
                    st["srope"] = se.v
                    E.wait("vector", se, st["srope"])
                    cs = slice(128 * t, 128 * t + 128)
                    for (x1o, x2o, OUTT) in ((0, 128, KH), (256, 384, QH)):
                        E.wait("vector", ve)
                        E.op("vector", lambda v, _a=x1o, _c=cs: v.tensor_mul(
                            out=RT1[:], in0=SROPE[:, _a:_a + 128], in1=COS[:, _c]))
                        E.op("vector", lambda v, _b2=x2o, _c=cs: v.tensor_mul(
                            out=RT2[:], in0=SROPE[:, _b2:_b2 + 128], in1=SIN[:, _c]), inc=ve)
                        E.wait("vector", ve)
                        E.op("vector", lambda v, _O=OUTT, _c=cs: v.tensor_sub(
                            out=_O[0:16, _c], in0=RT1[:], in1=RT2[:]), inc=ve)
                        E.wait("vector", ve)
                        E.op("vector", lambda v, _b2=x2o, _c=cs: v.tensor_mul(
                            out=RT1[:], in0=SROPE[:, _b2:_b2 + 128], in1=COS[:, _c]))
                        E.op("vector", lambda v, _a=x1o, _c=cs: v.tensor_mul(
                            out=RT2[:], in0=SROPE[:, _a:_a + 128], in1=SIN[:, _c]), inc=ve)
                        E.wait("vector", ve)
                        E.op("vector", lambda v, _O=OUTT, _c=cs: v.tensor_add(
                            out=_O[96:112, _c], in0=RT1[:], in1=RT2[:]), inc=ve)
                    st["rope_ve"] = ve.v

                stage_load(0)
                stage_load(1)
                for i in range(NT + 2):
                    if i < NT:
                        stage_ln(i)
                    if 1 <= i <= NT:
                        stage_T(i - 1)
                    if i + 2 < NT:
                        stage_load(i + 2)
                    if i >= 2:
                        stage_kvq(i - 2)
                    if 1 <= i <= NT:
                        stage_ck(i - 1)
                        stage_ckevac(i - 1)

                if t_kt0 is not None:
                    E.wait("sync", ve)
                    E.wait("sync", pe)
                    E.wait("sync", gp)
                    E.wait("sync", se)
                    E.dma("sync", lambda s: s.dma_start(out=t_kt0, in_=KT[0][:]), inc=dS)
                    E.dma("sync", lambda s: s.dma_start(out=t_qt0, in_=QT[0][:]), inc=dS)
                    if t_kh is not None:
                        E.dma("sync", lambda s: s.dma_start(out=t_kh, in_=KH[:]), inc=dS)
                    if t_qh is not None:
                        E.dma("sync", lambda s: s.dma_start(out=t_qh, in_=QH[:]), inc=dS)
                    E.dma("sync", lambda s: s.dma_start(out=t_vs, in_=VS[:]), inc=dS)
                    E.wait("sync", dS)

                for eng in ENGS:
                    E.wait_all(eng)

            run_block(nc, sems, p0_walk)

        if phases == 1:
            return nc, tap_out

        with ExitStack() as p1:
            sb1 = lambda n, s, d: p1.enter_context(nc.sbuf_tensor(n, s, d, side="right"))
            ps1 = lambda n, s, d: p1.enter_context(nc.psum_tensor(n, s, d))
            KH2 = [KH, sb1("KHb", [112, T], BF16)]
            QH2 = [QH, sb1("QHb", [112, T], BF16)]
            PB = [sb1(f"PB{i}", [128, 2 * 512], BF16) for i in range(3)]
            RECB = sb1("RECB", [1, 512], BF16)
            BCS = sb1("BCS", [64, 512], F32)
            S_ = [ps1(f"S{i}", [128, 2 * 512], F32) for i in range(3)]
            AVp = ps1("AVp", [128, 1024], F32)

            t_col0 = tap("col0", [128, T], BF16)
            t_col1 = tap("col1", [128, T], BF16)

            def p1_walk(E):
                E.op("vector", lambda v: v.tensor_copy(out=KH2[1][0:32, :], in_=KH2[0][0:32, :]))
                E.op("vector", lambda v: v.tensor_copy(out=KH2[1][96:112, :], in_=KH2[0][96:112, :]))
                E.op("vector", lambda v: v.tensor_copy(out=QH2[1][0:32, :], in_=QH2[0][0:32, :]))
                E.op("vector", lambda v: v.tensor_copy(out=QH2[1][96:112, :], in_=QH2[0][96:112, :]), inc=ve)

                div_done = [0, 0]
                last_exp = [0, 0, 0]
                asm_pe_gate = [0, 0]
                asm_done = [0, 0]
                qt_par = 0

                def assemble(h):
                    hb = h % 2
                    c, r = h // 2, 64 * (h % 2)
                    E.wait("vector", pe, asm_pe_gate[hb])
                    for (DST, SRC) in ((KH2[hb], KT[c]), (QH2[hb], QT[c])):
                        E.op("vector", lambda v, _D=DST, _S=SRC, _r=r: v.tensor_copy(
                            out=_D[32:64, :], in_=_S[_r:_r + 32, :]))
                        E.op("vector", lambda v, _D=DST, _S=SRC, _r=r: v.tensor_copy(
                            out=_D[64:96, :], in_=_S[_r + 32:_r + 64, :]))
                    E.op("vector", lambda v: v.engine_nop(), inc=ve)
                    asm_done[hb] = ve.v

                assemble(0)
                for h in range(NHC):
                    hb = h % 2
                    KHc, QHc = KH2[hb], QH2[hb]
                    if h + 1 < NHC:
                        assemble(h + 1)
                    gctr = 0
                    for qt in range(NQT):
                        ah = qt_par = (h * NQT + qt) % 2
                        av0 = 512 * ah
                        nk = 4 * qt + 4
                        groups = [list(range(g, min(g + 2, nk))) for g in range(0, nk, 2)]
                        ng = len(groups)
                        gpar = [(gctr + i) % 3 for i in range(ng)]
                        gctr = (gctr + ng) % 3
                        pe_sc = {}
                        av_cnt = 0

                        def emit_scores(gi):
                            grp = groups[gi]
                            par = gpar[gi]
                            E.wait("tensor", ve, asm_done[hb])
                            E.wait("tensor", se, last_exp[par])
                            for j, kt in enumerate(grp):
                                diag = kt >= 4 * qt
                                last_j = (j == len(grp) - 1)
                                E.op("tensor", lambda tt, _kt=kt, _qt=qt, _p=par, _j=j, _d=diag, _K=KHc, _Q=QHc: tt.matmul(
                                    out=S_[_p][:, 512 * _j:512 * _j + 512],
                                    lhsT=_K[:, 128 * _kt:128 * _kt + 128],
                                    rhs=_Q[:, 512 * _qt:512 * _qt + 512],
                                    start=True, stop=(not _d)),
                                    inc=(pe if (last_j and not diag) else None))
                                if diag:
                                    dlt = kt - 4 * qt
                                    E.op("tensor", lambda tt, _p=par, _j=j, _dl=dlt: tt.matmul(
                                        out=S_[_p][:, 512 * _j:512 * _j + 512],
                                        lhsT=EYEB[:], rhs=MSK[:, 512 * _dl:512 * _dl + 512],
                                        start=False, stop=True),
                                        inc=(pe if last_j else None))
                            pe_sc[gi] = pe.v

                        def emit_exp(gi):
                            grp = groups[gi]
                            par = gpar[gi]
                            E.wait("scalar", pe, pe_sc[gi])
                            E.op("scalar", lambda s, _p=par, _n=len(grp): s.activation(
                                out=PB[_p][:, 0:512 * _n], in_=S_[_p][:, 0:512 * _n],
                                func=AF.Exp, scale=C_SCALE[:], bias=C_MEXP[:]), inc=se)
                            last_exp[par] = se.v

                        def emit_av(gi):
                            nonlocal av_cnt
                            grp = groups[gi]
                            par = gpar[gi]
                            E.wait("tensor", se, last_exp[par])
                            if gi == 0:
                                E.wait("tensor", ve, div_done[ah])
                            for j, kt in enumerate(grp):
                                first = (av_cnt == 0)
                                last = (av_cnt == nk - 1)
                                av_cnt += 1
                                E.op("tensor", lambda tt, _kt=kt, _h=h, _p=par, _j=j, _f=first, _l=last, _a=av0: tt.matmul(
                                    out=AVp[0:65, _a:_a + 512],
                                    lhsT=VS[:, 520 * _kt + 65 * _h:520 * _kt + 65 * _h + 65],
                                    rhs=PB[_p][:, 512 * _j:512 * _j + 512],
                                    start=_f, stop=_l), inc=(pe if last else None))

                        emit_scores(0)
                        emit_exp(0)
                        if ng > 1:
                            emit_scores(1)
                            emit_exp(1)
                        for gi in range(2, ng):
                            emit_scores(gi)
                            emit_av(gi - 2)
                            emit_exp(gi)
                        if ng > 1:
                            emit_av(ng - 2)
                        emit_av(ng - 1)
                        pe_av = pe.v
                        E.wait("vector", pe, pe_av)

                        def _recip_bf16(v, a):
                            with nc.allow_low_precision(reason="bf16 store of fp32 reciprocal"):
                                return v.reciprocal(out=RECB[:], in_=AVp[64:65, a:a + 512])
                        E.op("vector", lambda v, _a=av0: _recip_bf16(v, _a), inc=ve)
                        E.wait("tensor", ve)
                        E.op("tensor", lambda tt, _a=av0: tt.matmul(
                            out=AVp[64:128, _a:_a + 512], lhsT=ONE1[:], rhs=RECB[:],
                            start=True, stop=True), inc=pe)
                        E.wait("vector", pe)
                        E.op("vector", lambda v, _a=av0: v.tensor_copy(
                            out=BCS[:], in_=AVp[64:128, _a:_a + 512]), inc=ve)
                        E.wait("vector", ve)
                        E.op("vector", lambda v, _h=h, _qt=qt, _a=av0: v.scalar_tensor_tensor(
                            out=COL[_h // 2][64 * (_h % 2):64 * (_h % 2) + 64,
                                             512 * _qt:512 * _qt + 512],
                            in0=AVp[0:64, _a:_a + 512], scalar=1.0, in1=BCS[:],
                            op0=ALU.mult, op1=ALU.mult), inc=ve)
                        div_done[ah] = ve.v
                    asm_pe_gate[hb] = pe.v

                if t_col0 is not None:
                    E.wait("sync", ve)
                    E.dma("sync", lambda s: s.dma_start(out=t_col0, in_=COL[0][:]), inc=dS)
                    if t_col1 is not None:
                        E.dma("sync", lambda s: s.dma_start(out=t_col1, in_=COL[1][:]), inc=dS)
                    E.wait("sync", dS)

                for eng in ENGS:
                    E.wait_all(eng)

            run_block(nc, sems, p1_walk)

        if phases == 2:
            return nc, tap_out

        S01.close()

        S23 = ExitStack()
        X2T = [S23.enter_context(nc.sbuf_tensor(f"X2T{i}", [128, OWN], BF16, side="right"))
               for i in range(8)]
        WGU = [S23.enter_context(nc.sbuf_tensor(f"WGU{i}", [128, 2 * 1024], BF16, side="right"))
               for i in range(2)]
        with ExitStack() as p2:
            sb2 = lambda n, s, d: p2.enter_context(nc.sbuf_tensor(n, s, d, side="right"))
            ps2 = lambda n, s, d: p2.enter_context(nc.psum_tensor(n, s, d))
            WOB = sb2("WOB", [128, 4 * 1024], BF16)
            PRT = [sb2(f"PRT{i}", [128, D], BF16) for i in range(2)]
            XO = [sb2(f"XO{i}", [128, D], F32) for i in range(2)]
            AT = [sb2(f"AT{i}", [128, D], BF16) for i in range(2)]
            R1 = sb2("R1", [128, D], F32)
            X2B = sb2("X2B", [128, D], BF16)
            VAR2 = sb2("VAR2", [128, 1], F32)
            RSTD2 = sb2("RSTD2", [128, 1], F32)
            BNS2 = sb2("BNS2", [128, 12], F32)
            BNA2 = sb2("BNA2", [128, 2], F32)
            pWO = [ps2(f"pWO{i}", [128, 1024], F32) for i in range(2)]
            pT2 = ps2("pT2", [128, 1024], BF16)

            t_r1 = tap("r1", [OWN, D], F32)
            t_x2 = tap("x2", [128, OWN], BF16)

            def p2_walk2(E):
                p2_state = {}
                for fc in range(4):
                    E.dma("gpsimd", lambda g, _f=fc: g.dma_start(
                        out=WOB[:, 1024 * _f:1024 * _f + 1024],
                        in_=wo[128 * _f:128 * _f + 128, :]), inc=dG)
                dl_wo = dG.v
                prt_free = [0, 0]
                evac_done = [0, 0]
                tile_order = [t for t in range(NT) if (t % 16) // 8 == 0] + \
                             [t for t in range(NT) if (t % 16) // 8 == 1]
                chunk0_done = [0, 0]
                for ti, t in enumerate(tile_order):
                    buf = ti % 2
                    E.wait("tensor", dG, dl_wo)
                    E.wait("tensor", ve, evac_done[buf])
                    for half in range(2):
                        for fc in range(4):
                            E.op("tensor", lambda t_, _f=fc, _t=t, _h=half, _b=buf: t_.matmul(
                                out=pWO[_b][:, 512 * _h:512 * _h + 512],
                                lhsT=COL[_f][:, 128 * _t:128 * _t + 128],
                                rhs=WOB[:, 1024 * _f + 512 * _h:1024 * _f + 512 * _h + 512],
                                start=(_f == 0), stop=(_f == 3)),
                                inc=(pe if (half == 1 and fc == 3) else None))
                    pe_wo = pe.v
                    E.wait("vector", pe, pe_wo)
                    E.wait("vector", dP[buf], prt_free[buf])
                    E.op("vector", lambda v, _b=buf: v.tensor_copy(out=PRT[_b][:], in_=pWO[_b][:]), inc=ve)
                    evac_done[buf] = ve.v
                    srow = 2048 * ((t % 16) // 8) + 1024 * (t // 16) + 128 * (t % 8)
                    E.wait("sync", ve, evac_done[buf])
                    E.dma("sync", lambda s, _r=srow, _b=buf: s.dma_start(
                        out=snd.ap()[_r:_r + 128, :], in_=PRT[_b][:]), inc=dP[buf])
                    prt_free[buf] = dP[buf].v
                    if ti == 15:
                        chunk0_done = [dP[0].v, dP[1].v]
                        E.wait("gpsimd", dP[0], chunk0_done[0])
                        E.wait("gpsimd", dP[1], chunk0_done[1])
                        E.op("gpsimd", lambda g: g.collective_compute(
                            "ReduceScatter", ALU.add,
                            replica_groups=[[0, 1], [2, 3], [4, 5], [6, 7]],
                            ins=[snd.ap()[0:2048, :].opt()],
                            outs=[rcv.ap()[0:1024, :].opt()]), inc=cc)
                E.wait("gpsimd", dP[0])
                E.wait("gpsimd", dP[1])
                E.op("gpsimd", lambda g: g.collective_compute(
                    "ReduceScatter", ALU.add,
                    replica_groups=[[0, 1], [2, 3], [4, 5], [6, 7]],
                    ins=[snd.ap()[2048:4096, :].opt()],
                    outs=[rcv.ap()[1024:2048, :].opt()]), inc=cc2)
                E.dma("gpsimd", lambda g: g.dma_start(
                    out=WGU[0][:, 0:1024].rearrange("p (c j) -> p c j", j=128),
                    in_=wgT.rearrange("(c p) j -> p c j", p=128)[:, :, 0:128]), inc=dW[0])
                E.dma("gpsimd", lambda g: g.dma_start(
                    out=WGU[0][:, 1024:2048].rearrange("p (c j) -> p c j", j=128),
                    in_=wuT.rearrange("(c p) j -> p c j", p=128)[:, :, 0:128]), inc=dW[0])

                r1_done = 0
                xo_free = [0, 0]
                at_free = [0, 0]
                pe_t2 = 0
                for t in range(16):
                    buf = t % 2
                    E.wait("sync", ve, xo_free[buf])
                    E.dma("sync", lambda s, _t=t, _b=buf: s.dma_start(
                        out=XO[_b][:], in_=x_own[128 * _t:128 * _t + 128, :]), inc=dX[buf])
                    E.wait("sync", cc if t < 8 else cc2)
                    E.wait("sync", ve, at_free[buf])
                    E.dma("sync", lambda s, _t=t, _b=buf: s.dma_start(
                        out=AT[_b][:], in_=rcv.ap()[128 * _t:128 * _t + 128, :]), inc=dA[buf])
                    E.wait("vector", dX[buf])
                    E.wait("vector", dA[buf])
                    E.wait("vector", dS, r1_done)
                    E.op("vector", lambda v, _b=buf: v.tensor_add(out=R1[:], in0=XO[_b][:], in1=AT[_b][:]), inc=ve)
                    xo_free[buf] = ve.v
                    at_free[buf] = ve.v
                    E.wait("sync", ve)
                    E.dma("sync", lambda s, _t=t: s.dma_start(
                        out=r1spill.ap()[128 * _t:128 * _t + 128, :], in_=R1[:]), inc=dS)
                    r1_done = dS.v
                    if t_r1 is not None:
                        E.dma("sync", lambda s, _t=t: s.dma_start(
                            out=t_r1[128 * _t:128 * _t + 128, :], in_=R1[:]), inc=dS)
                        r1_done = dS.v
                    E.wait("vector", ve)
                    E.op("vector", lambda v: v.bn_stats(out=BNS2[:, 0:6], in_=R1[:, 0:512]))
                    E.op("vector", lambda v: v.bn_stats(out=BNS2[:, 6:12], in_=R1[:, 512:1024]), inc=ve)
                    E.wait("vector", ve)
                    E.op("vector", lambda v: v.bn_aggr(out=BNA2[:], in_=BNS2[:]), inc=ve)
                    E.wait("vector", ve)
                    E.op("vector", lambda v: v.tensor_scalar_add(out=VAR2[:], in0=BNA2[:, 1:2], scalar1=1e-5), inc=ve)
                    E.wait("scalar", ve)
                    E.op("scalar", lambda s: s.activation(out=RSTD2[:], in_=VAR2[:], func=AF.Ln), inc=se)
                    E.wait("scalar", se)
                    E.op("scalar", lambda s: s.activation(out=RSTD2[:], in_=RSTD2[:], func=AF.Exp, scale=C_HALF[:]), inc=se)
                    E.wait("vector", se)
                    E.wait("vector", pe, pe_t2)
                    E.op("vector", lambda v: v.tensor_scalar(
                        out=X2B[:], in0=R1[:], scalar1=BNA2[:, 0:1], scalar2=RSTD2[:],
                        op0=ALU.subtract, op1=ALU.mult), inc=ve)
                    ve_x2 = ve.v
                    E.wait("tensor", ve, ve_x2)
                    for c in range(8):
                        E.op("tensor", lambda t_, _c=c: t_.transpose(
                            out=pT2[:, 128 * _c:128 * _c + 128],
                            in_=X2B[:, 128 * _c:128 * _c + 128],
                            identity=EYEB[:]), inc=(pe if c == 7 else None))
                    pe_t2 = pe.v
                    E.wait("vector", pe, pe_t2)
                    for c in range(8):
                        E.op("vector", lambda v, _c=c, _t=t: v.tensor_copy(
                            out=X2T[_c][:, 128 * _t:128 * _t + 128],
                            in_=pT2[:, 128 * _c:128 * _c + 128]),
                            inc=(ve if c == 7 else None))

                if t_x2 is not None:
                    E.wait("sync", ve)
                    E.dma("sync", lambda s: s.dma_start(out=t_x2, in_=X2T[0][:]), inc=dS)
                    E.wait("sync", dS)

                for eng in ENGS:
                    E.wait_all(eng)

            run_block(nc, sems, p2_walk2)

        if phases == 3:
            return nc, tap_out

        with ExitStack() as p3:
            sb3 = lambda n, s, d: p3.enter_context(nc.sbuf_tensor(n, s, d, side="right"))
            ps3 = lambda n, s, d: p3.enter_context(nc.psum_tensor(n, s, d))
            HB = p3.enter_context(nc.sbuf_tensor("HB", [128, 24 * OWN], BF16, side="left"))

            WDA1 = p3.enter_context(nc.sbuf_tensor("WDA1", [128, 12 * 1024], BF16, side="left"))
            p3a = ExitStack()
            GS = p3a.enter_context(nc.sbuf_tensor("GS", [128, 512], BF16, side="right"))
            US = p3a.enter_context(nc.sbuf_tensor("US", [128, 512], BF16, side="right"))
            pG = p3a.enter_context(nc.psum_tensor("pG", [128, 512], F32))
            pU = p3a.enter_context(nc.psum_tensor("pU", [128, 512], F32))

            dl_wd_early = [0]

            def p3a_walk(E):
                wgu_free = [0, 0]
                silu_done = 0
                umul_done = 0
                for hc in range(12):
                    E.dma("gpsimd", lambda g, _h=hc: g.dma_start(
                        out=WDA1[:, 1024 * _h:1024 * _h + 1024],
                        in_=wdT[128 * _h:128 * _h + 128, :]), inc=dG)
                dl_wd_early[0] = dG.v
                for hc in range(24):
                    wb = hc % 2
                    if hc > 0:
                        E.wait("gpsimd", pe, wgu_free[wb])
                        E.dma("gpsimd", lambda g, _h=hc, _b=wb: g.dma_start(
                            out=WGU[_b][:, 0:1024].rearrange("p (c j) -> p c j", j=128),
                            in_=wgT.rearrange("(c p) j -> p c j", p=128)[:, :, 128 * _h:128 * _h + 128]), inc=dW[wb])
                        E.dma("gpsimd", lambda g, _h=hc, _b=wb: g.dma_start(
                            out=WGU[_b][:, 1024:2048].rearrange("p (c j) -> p c j", j=128),
                            in_=wuT.rearrange("(c p) j -> p c j", p=128)[:, :, 128 * _h:128 * _h + 128]), inc=dW[wb])
                    dl_gu = dW[wb].v
                    for tt in range(4):
                        E.wait("tensor", dW[wb], dl_gu)
                        E.wait("tensor", se, silu_done)
                        for c in range(8):
                            E.op("tensor", lambda t_, _c=c, _t=tt, _b=wb: t_.matmul(
                                out=pG[:], lhsT=WGU[_b][:, 128 * _c:128 * _c + 128],
                                rhs=X2T[_c][:, 512 * _t:512 * _t + 512],
                                start=(_c == 0), stop=(_c == 7)), inc=(pe if c == 7 else None))
                        pe_g = pe.v
                        E.wait("tensor", ve, umul_done)
                        for c in range(8):
                            E.op("tensor", lambda t_, _c=c, _t=tt, _b=wb: t_.matmul(
                                out=pU[:], lhsT=WGU[_b][:, 1024 + 128 * _c:1024 + 128 * _c + 128],
                                rhs=X2T[_c][:, 512 * _t:512 * _t + 512],
                                start=(_c == 0), stop=(_c == 7)), inc=(pe if c == 7 else None))
                        pe_u = pe.v
                        if tt == 3:
                            wgu_free[wb] = pe.v
                        E.wait("scalar", pe, pe_g)
                        E.wait("scalar", ve, umul_done)
                        E.op("scalar", lambda s: s.activation(out=GS[:], in_=pG[:], func=AF.Silu,
                                                              scale=C_S127[:]), inc=se)
                        silu_done = se.v
                        E.wait("vector", pe, pe_u)
                        E.op("vector", lambda v: v.tensor_copy(out=US[:], in_=pU[:]), inc=ve)
                        E.wait("vector", se, silu_done)
                        E.wait("vector", ve)
                        E.op("vector", lambda v, _h=hc, _t=tt: v.tensor_mul(
                            out=HB[:, OWN * _h + 512 * _t:OWN * _h + 512 * _t + 512],
                            in0=GS[:], in1=US[:]), inc=ve)
                        umul_done = ve.v
                for eng in ENGS:
                    E.wait_all(eng)

            run_block(nc, sems, p3a_walk)
            p3a.close()
            S23.close()

            WDA2 = p3.enter_context(nc.sbuf_tensor("WDA2", [128, 12 * 1024], BF16, side="right"))
            sbl = lambda n, s, d: p3.enter_context(nc.sbuf_tensor(n, s, d, side="left"))
            FDT = sbl("FDT", [128, 4096], F32)
            R1T = [sbl(f"R1T{i}", [128, D], F32) for i in range(2)]
            OStg = sbl("OStg", [128, D], F32)
            pD = ps3("pD", [128, 3072], F32)
            pR = ps3("pR", [128, 1024], F32)

            def p3b_walk(E):
                for hc in range(12):
                    E.dma("gpsimd", lambda g, _h=hc: g.dma_start(
                        out=WDA2[:, 1024 * _h:1024 * _h + 1024],
                        in_=wdT[128 * (_h + 12):128 * (_h + 12) + 128, :]), inc=dG)
                dl_wd1 = dl_wd_early[0]
                dl_wd = dG.v

                fdt_free = 0
                r1t_free = [0, 0]
                ost_free = 0
                stt_done = 0
                for tt in range(4):
                    E.wait("tensor", dG, dl_wd1)
                    E.wait("tensor", se, fdt_free)
                    E.wait("tensor", ve, stt_done)
                    for hc in range(24):
                        if hc == 12:
                            E.wait("tensor", dG, dl_wd)
                        WDAx = WDA1 if hc < 12 else WDA2
                        hco = hc % 12
                        for dc in range(8):
                            dst = pD[:, 512 * dc:512 * dc + 512] if dc < 6 else \
                                  pR[:, 512 * (dc - 6):512 * (dc - 6) + 512]
                            E.op("tensor", lambda t_, _h=hc, _ho=hco, _W=WDAx, _d=dc, _t=tt, _dst=dst: t_.matmul(
                                out=_dst,
                                lhsT=_W[:, 1024 * _ho + 128 * _d:1024 * _ho + 128 * _d + 128],
                                rhs=HB[:, OWN * _h + 512 * _t:OWN * _h + 512 * _t + 512],
                                start=(_h == 0), stop=(_h == 23)),
                                inc=(pe if (hc == 23 and dc == 7) else None))
                    pe_d = pe.v
                    E.wait("scalar", pe, pe_d)
                    for dc in range(8):
                        src = pD[:, 512 * dc:512 * dc + 512] if dc < 6 else \
                              pR[:, 512 * (dc - 6):512 * (dc - 6) + 512]
                        E.op("scalar", lambda s, _d=dc, _src=src: s.activation(
                            out=FDT[:, 512 * _d:512 * _d + 512], in_=_src,
                            func=AF.Copy, scale=C_S127SQ[:]),
                            inc=(se if dc == 7 else None))
                    fdt_free = se.v
                    for s4 in range(4):
                        sb_ = s4 % 2
                        E.wait("sync", ve, r1t_free[sb_])
                        E.dma("sync", lambda s, _t=tt, _s=s4, _b=sb_: s.dma_start(
                            out=R1T[_b][:],
                            in_=r1spill.ap()[512 * _t + 128 * _s:512 * _t + 128 * _s + 128, :]),
                            inc=dX[sb_])
                        dl_r1 = (sb_, dX[sb_].v)
                        E.wait("tensor", se, fdt_free)
                        if s4 >= 2:
                            E.wait("tensor", ve)
                        for dc in range(8):
                            E.op("tensor", lambda t_, _d=dc, _s=s4: t_.transpose(
                                out=pD[:, 1024 * (_s % 2) + 128 * _d:1024 * (_s % 2) + 128 * _d + 128],
                                in_=FDT[:, 512 * _d + 128 * _s:512 * _d + 128 * _s + 128],
                                identity=EYEF[:]), inc=(pe if dc == 7 else None))
                        pe_tr = pe.v
                        E.wait("vector", pe, pe_tr)
                        E.wait("vector", dX[dl_r1[0]], dl_r1[1])
                        E.wait("vector", dS, ost_free)
                        E.op("vector", lambda v, _s=s4, _b=sb_: v.scalar_tensor_tensor(
                            out=OStg[:], in0=pD[:, 1024 * (_s % 2):1024 * (_s % 2) + 1024],
                            scalar=1.0, in1=R1T[_b][:],
                            op0=ALU.mult, op1=ALU.add), inc=ve)
                        ve_o = ve.v
                        r1t_free[sb_] = ve_o
                        E.wait("sync", ve, ve_o)
                        E.dma("sync", lambda s, _t=tt, _s=s4: s.dma_start(
                            out=y[512 * _t + 128 * _s:512 * _t + 128 * _s + 128, :],
                            in_=OStg[:]), inc=dS)
                        ost_free = dS.v
                    stt_done = ve.v

                for eng in ENGS:
                    E.wait_all(eng)

            run_block(nc, sems, p3b_walk)

    return nc, tap_out


def _host_inputs(core, inputs):
    b, p = core // 2, core % 2
    h0 = NHC * p * HD
    x = np.ascontiguousarray(np.asarray(inputs["L0_attn_x"])[b])
    out = dict(
        x_seq=x,
        x_own=np.ascontiguousarray(x[OWN * p:OWN * p + OWN]),
        wkv=np.ascontiguousarray(np.asarray(inputs["L0_attn_W_kv_down"])),
        wkup=np.ascontiguousarray(np.asarray(inputs["L0_attn_W_k_up"])[:, h0:h0 + NHC * HD]),
        wvup=np.ascontiguousarray(np.asarray(inputs["L0_attn_W_v_up"])[:, h0:h0 + NHC * HD]),
        wq=np.ascontiguousarray(np.asarray(inputs["L0_attn_W_q"])[:, h0:h0 + NHC * HD]),
        wqr=np.ascontiguousarray(np.asarray(inputs["L0_attn_W_q_rope"])),
        wkr=np.ascontiguousarray(np.asarray(inputs["L0_attn_W_k_rope"])),
        wo=np.ascontiguousarray(np.asarray(inputs["L0_attn_W_o"])[h0:h0 + NHC * HD, :]),
        wgT=np.ascontiguousarray(np.asarray(inputs["L0_ffn_w_gate"]).T),
        wuT=np.ascontiguousarray(np.asarray(inputs["L0_ffn_w_up"]).T),
        wdT=np.ascontiguousarray(np.asarray(inputs["L0_ffn_w_down"]).T),
    )
    inv_freq = 1.0 / 10000.0 ** (np.arange(0, DR, 2, dtype=np.float32) / DR)
    tpos = np.arange(T, dtype=np.float32)
    freqs = np.outer(tpos, inv_freq).astype(np.float32)
    out["cosT"] = np.ascontiguousarray(np.cos(freqs).T.astype(np.float32))
    out["sinT"] = np.ascontiguousarray(np.sin(freqs).T.astype(np.float32))
    out["eye"] = np.eye(128, dtype=np.float32)
    masks_ = np.zeros((128, 4, 512), dtype=np.float32)
    k_loc = np.arange(128)[:, None]
    q_loc = np.arange(512)[None, :]
    for i, dlt in enumerate((0, 128, 256, 384)):
        masks_[:, i, :] = np.where(k_loc + dlt > q_loc, -1e9, 0.0)
    out["masks"] = np.ascontiguousarray(masks_.reshape(128, 4 * 512))
    return out


def kernel(**inputs):
    nc, _ = build()
    in_maps = [_host_inputs(c, inputs) for c in range(8)]
    res = run_bass_kernel_spmd(nc, in_maps, core_ids=list(range(8)))
    out = np.zeros((4, T, D), dtype=np.float32)
    for c in range(8):
        b, p = c // 2, c % 2
        out[b, OWN * p:OWN * p + OWN] = np.asarray(res.results[c]["y"], dtype=np.float32)
    return out


# version 07
# speedup vs baseline: 1.6429x; 1.0679x over previous
import math
from contextlib import ExitStack

import numpy as np
import concourse.bass as bass
import concourse.mybir as mybir
from concourse.bass_utils import run_bass_kernel_spmd

F32 = mybir.dt.float32
BF16 = mybir.dt.bfloat16
I32 = mybir.dt.int32
AF = mybir.ActivationFunctionType
ALU = mybir.AluOpType
AX = mybir.AxisListType

T = 4096
D = 1024
NHC = 8
HD = 64
DC = 128
DR = 32
FF = 3072
NT = T // 128
OWN = 2048
NQT = T // 512
SCALE = 1.0 / math.sqrt(HD + DR)
MEXP = 24.0
S127SQ = (1.0 / 127.0) ** 2

ENGS = ("sync", "gpsimd", "vector", "scalar", "tensor")


class Sem:
    def __init__(self, ctx, nc, name):
        self.h = ctx.enter_context(nc.semaphore(name))
        self.v = 0
        self.name = name


class Emitter:
    def __init__(self, who, eng, sems):
        self.who = who
        self.eng = eng
        self.sems = sems

    def op(self, engine, fn, inc=None, n=1):
        if self.who == engine:
            instr = fn(self.eng)
            if inc is not None:
                instr.then_inc(inc.h, n)
        if inc is not None:
            inc.v += n
        return inc.v if inc is not None else None

    def dma(self, engine, fn, inc):
        return self.op(engine, fn, inc, 16)

    def wait(self, engine, sem, val=None):
        if self.who == engine:
            self.eng.wait_ge(sem.h, sem.v if val is None else val)

    def wait_all(self, engine):
        for s in self.sems:
            if s.v > 0 and self.who == engine:
                self.eng.wait_ge(s.h, s.v)


def run_block(nc, sems, walk):
    with nc.Block() as blk:
        snap = [s.v for s in sems]
        for engname in ENGS:
            def body(eng, _en=engname):
                for s, v in zip(sems, snap):
                    s.v = v
                walk(Emitter(_en, eng, sems))
            getattr(blk, engname)(body)


def build(debug_taps=(), phases=4):
    nc = bass.Bass("TRN2", target_bir_lowering=False, debug=False)

    x_seq = nc.dram_tensor("x_seq", [T, D], F32, kind="ExternalInput").ap()
    x_own = nc.dram_tensor("x_own", [OWN, D], F32, kind="ExternalInput").ap()
    wkv = nc.dram_tensor("wkv", [D, DC], F32, kind="ExternalInput").ap()
    wkup = nc.dram_tensor("wkup", [DC, NHC * HD], F32, kind="ExternalInput").ap()
    wvup = nc.dram_tensor("wvup", [DC, NHC * HD], F32, kind="ExternalInput").ap()
    wq = nc.dram_tensor("wq", [D, NHC * HD], F32, kind="ExternalInput").ap()
    wqr = nc.dram_tensor("wqr", [D, DR], F32, kind="ExternalInput").ap()
    wkr = nc.dram_tensor("wkr", [DC, DR], F32, kind="ExternalInput").ap()
    wo = nc.dram_tensor("wo", [NHC * HD, D], F32, kind="ExternalInput").ap()
    wgT = nc.dram_tensor("wgT", [D, FF], I32, kind="ExternalInput").ap()
    wuT = nc.dram_tensor("wuT", [D, FF], I32, kind="ExternalInput").ap()
    wdT = nc.dram_tensor("wdT", [FF, D], I32, kind="ExternalInput").ap()
    cosT = nc.dram_tensor("cosT", [DR // 2, T], F32, kind="ExternalInput").ap()
    sinT = nc.dram_tensor("sinT", [DR // 2, T], F32, kind="ExternalInput").ap()
    eye = nc.dram_tensor("eye", [128, 128], F32, kind="ExternalInput").ap()
    masks = nc.dram_tensor("masks", [128, 4 * 512], F32, kind="ExternalInput").ap()

    y = nc.dram_tensor("y", [OWN, D], F32, kind="ExternalOutput").ap()

    snd = nc.dram_tensor("snd", [T, D], BF16)
    rcv = nc.dram_tensor("rcv", [OWN, D], BF16)
    r1spill = nc.dram_tensor("r1spill", [OWN, D], F32)

    tap_out = {}

    def tap(name, shape, dtype):
        if name in debug_taps:
            tap_out[name] = nc.dram_tensor("tap_" + name, shape, dtype,
                                           kind="ExternalOutput").ap()
            return tap_out[name]
        return None

    with ExitStack() as ctx:
        sbt = lambda n, s, d: ctx.enter_context(nc.sbuf_tensor(n, s, d))

        COL = [sbt(f"COL{i}", [128, T], BF16) for i in range(4)]
        S01 = ExitStack()
        s01t = lambda n, s, d: S01.enter_context(nc.sbuf_tensor(n, s, d, side="right"))
        KT = [s01t(f"KT{i}", [128, T], BF16) for i in range(4)]
        QT = [s01t(f"QT{i}", [128, T], BF16) for i in range(4)]
        KH = s01t("KH", [112, T], BF16)
        QH = s01t("QH", [112, T], BF16)
        MSK = s01t("MSK", [128, 4 * 512], BF16)
        VS = s01t("VS", [128, NT * (NHC * 65)], BF16)
        EYEB = sbt("EYEB", [128, 128], BF16)
        EYEF = sbt("EYEF", [128, 128], F32)
        ONE1 = sbt("ONE1", [1, 64], BF16)
        C_SCALE = sbt("C_SCALE", [128, 1], F32)
        C_MEXP = sbt("C_MEXP", [128, 1], F32)
        C_HALF = sbt("C_HALF", [128, 1], F32)
        C_S127 = sbt("C_S127", [128, 1], F32)
        C_S127SQ = sbt("C_S127SQ", [128, 1], F32)
        C_EPS = sbt("C_EPS", [128, 1], F32)

        sems = []

        def mk(name):
            s = Sem(ctx, nc, name)
            sems.append(s)
            return s

        dL = mk("dL")
        dG = mk("dG")
        dS = mk("dS")
        dX = [mk("dXa"), mk("dXb"), mk("dXc")]
        dA = [mk("dAa"), mk("dAb")]
        dP = [mk("dPa"), mk("dPb")]
        dW = [mk("dWa"), mk("dWb")]
        pe = mk("pe")
        ve = mk("ve")
        se = mk("se")
        cc = mk("cc")
        cc2 = mk("cc2")
        gp = mk("gp")

        with ExitStack() as p0:
            sb0 = lambda n, s, d: p0.enter_context(nc.sbuf_tensor(n, s, d, side="right"))
            ps0 = lambda n, s, d: p0.enter_context(nc.psum_tensor(n, s, d))
            XT = [sb0(f"XT{i}", [128, D], F32) for i in range(3)]
            XHB = [sb0(f"XHB{i}", [128, D], BF16) for i in range(2)]
            XHT = [sb0(f"XHT{i}", [128, D], BF16) for i in range(2)]
            CKT = [sb0(f"CKT{i}", [128, 128], BF16) for i in range(2)]
            VAR = [sb0(f"VAR{i}", [128, 1], F32) for i in range(2)]
            RSTD = [sb0(f"RSTD{i}", [128, 1], F32) for i in range(2)]
            BNS = [sb0(f"BNS{i}", [128, 12], F32) for i in range(2)]
            BNA = [sb0(f"BNA{i}", [128, 2], F32) for i in range(2)]
            WKVB = sb0("WKVB", [128, 8 * 128], BF16)
            WKUP = sb0("WKUP", [128, 512], BF16)
            WVUP = sb0("WVUP", [128, 512], BF16)
            WQB = sb0("WQB", [128, 8 * 512], BF16)
            WQR = sb0("WQR", [128, 8 * 32], BF16)
            WKR = sb0("WKR", [128, 32], BF16)
            sbr = lambda n, s, d: p0.enter_context(nc.sbuf_tensor(n, s, d, side="right"))
            COS = sbr("COS", [DR // 2, T], BF16)
            SIN = sbr("SIN", [DR // 2, T], BF16)
            SROPE = [sbr(f"SROPE{i}", [16, 512], BF16) for i in range(2)]
            RT1 = sbr("RT1", [DR // 2, 128], BF16)
            RT2 = sbr("RT2", [DR // 2, 128], BF16)
            RT3 = sbr("RT3", [DR // 2, 128], BF16)
            RT4 = sbr("RT4", [DR // 2, 128], BF16)
            pT = [ps0(f"pT{i}", [128, 1024], BF16) for i in range(2)]
            pCK = ps0("pCK", [128, 128], F32)
            pK = ps0("pK", [128, 512], F32)
            pV = ps0("pV", [128, 512], F32)
            pQa = ps0("pQa", [128, 128], F32)
            pQb = ps0("pQb", [128, 128], F32)
            pROPE = ps0("pROPE", [16, 512], F32)

            t_xhb = tap("xhb", [128, D], BF16)
            t_kt0 = tap("kt0", [128, T], BF16)
            t_qt0 = tap("qt0", [128, T], BF16)
            t_kh = tap("kh", [112, T], BF16)
            t_qh = tap("qh", [112, T], BF16)
            t_vs = tap("vs", [128, NT * NHC * 65], BF16)

            def p0_walk(E):
                E.op("vector", lambda v: v.memset(ONE1[:], 1.0))
                E.op("vector", lambda v: v.memset(C_SCALE[:], SCALE))
                E.op("vector", lambda v: v.memset(C_MEXP[:], -MEXP))
                E.op("vector", lambda v: v.memset(C_HALF[:], -0.5))
                E.op("vector", lambda v: v.memset(C_S127[:], 1.0 / 127.0))
                E.op("vector", lambda v: v.memset(C_S127SQ[:], S127SQ))
                E.op("vector", lambda v: v.memset(C_EPS[:], 1e-5))
                E.op("vector", lambda v: v.memset(KH[0:32, :], 0.0))
                E.op("vector", lambda v: v.memset(KH[32:64, :], 0.0))
                E.op("vector", lambda v: v.memset(KH[64:96, :], 0.0))
                E.op("vector", lambda v: v.memset(QH[0:32, :], 0.0))
                E.op("vector", lambda v: v.memset(QH[32:64, :], 0.0))
                E.op("vector", lambda v: v.memset(QH[64:96, :], 0.0), inc=ve)

                E.dma("gpsimd", lambda g: g.dma_start(out=COS[:], in_=cosT), inc=dG)
                E.dma("gpsimd", lambda g: g.dma_start(out=SIN[:], in_=sinT), inc=dG)
                E.dma("sync", lambda s: s.dma_start(out=EYEF[:], in_=eye), inc=dL)
                E.dma("gpsimd", lambda g: g.dma_start(out=EYEB[:], in_=eye), inc=dG)
                E.dma("gpsimd", lambda g: g.dma_start(out=MSK[:], in_=masks), inc=dG)
                E.dma("gpsimd", lambda g: g.dma_start(
                    out=WKVB[:].rearrange("p (c j) -> p c j", j=128),
                    in_=wkv.rearrange("(c p) j -> p c j", p=128)), inc=dG)
                E.dma("gpsimd", lambda g: g.dma_start(out=WKUP[:], in_=wkup), inc=dG)
                E.dma("gpsimd", lambda g: g.dma_start(out=WVUP[:], in_=wvup), inc=dG)
                E.dma("gpsimd", lambda g: g.dma_start(
                    out=WQB[:].rearrange("p (c j) -> p c j", j=512),
                    in_=wq.rearrange("(c p) j -> p c j", p=128)), inc=dG)
                E.dma("gpsimd", lambda g: g.dma_start(
                    out=WQR[:].rearrange("p (c j) -> p c j", j=32),
                    in_=wqr.rearrange("(c p) j -> p c j", p=128)), inc=dG)
                E.dma("gpsimd", lambda g: g.dma_start(out=WKR[:], in_=wkr), inc=dG)
                dl_w = dG.v

                st = {
                    "apply": [0, 0],
                    "peT": [0, 0],
                    "xht": [0, 0],
                    "rstd": [0, 0],
                    "ckmm": [0, 0],
                    "ckt": [0, 0],
                    "kvq_pe": [0, 0],
                    "kevac": 0,
                    "q_evac": [0, 0],
                    "srope": 0,
                    "srope2": [0, 0],
                    "rope_ve": [0, 0],
                    "xhb_tap": 0,
                    "xload": [0, 0, 0],
                    "apply3": [0, 0, 0],
                }

                def stage_load(t):
                    b3 = t % 3
                    E.wait("sync", ve, st["apply3"][b3])
                    E.dma("sync", lambda s, _t=t, _b=b3: s.dma_start(
                        out=XT[_b][:], in_=x_seq[128 * _t:128 * _t + 128, :]), inc=dX[b3])
                    st["xload"][b3] = dX[b3].v

                def stage_ln(t):
                    buf = t % 2
                    b3 = t % 3
                    E.wait("vector", dX[b3], st["xload"][b3])
                    if t == 0:
                        E.wait("vector", dG, dl_w)
                    E.op("vector", lambda v, _b=buf, _x=b3: v.bn_stats(out=BNS[_b][:, 0:6], in_=XT[_x][:, 0:512]))
                    E.op("vector", lambda v, _b=buf, _x=b3: v.bn_stats(out=BNS[_b][:, 6:12], in_=XT[_x][:, 512:1024]), inc=ve)
                    E.wait("vector", ve)
                    E.op("vector", lambda v, _b=buf: v.bn_aggr(out=BNA[_b][:], in_=BNS[_b][:]), inc=ve)
                    E.wait("scalar", ve)
                    E.op("scalar", lambda s, _b=buf: s.activation(out=RSTD[_b][:], in_=BNA[_b][:, 1:2],
                                                                  func=AF.Ln, bias=C_EPS[:]), inc=se)
                    E.wait("scalar", se)
                    E.op("scalar", lambda s, _b=buf: s.activation(out=RSTD[_b][:], in_=RSTD[_b][:],
                                                                  func=AF.Exp, scale=C_HALF[:]), inc=se)
                    st["rstd"][buf] = se.v

                def stage_T(t):
                    buf = t % 2
                    b3 = t % 3
                    E.wait("vector", se, st["rstd"][buf])
                    E.wait("vector", pe, st["peT"][buf])
                    if st["xhb_tap"]:
                        E.wait("vector", dS, st["xhb_tap"])
                    E.op("vector", lambda v, _b=buf, _x=b3: v.tensor_scalar(
                        out=XHB[_b][:], in0=XT[_x][:], scalar1=BNA[_b][:, 0:1], scalar2=RSTD[_b][:],
                        op0=ALU.subtract, op1=ALU.mult), inc=ve)
                    st["apply3"][b3] = ve.v
                    st["apply"][buf] = ve.v
                    if t_xhb is not None and t == 0:
                        E.wait("sync", ve, st["apply"][0])
                        E.dma("sync", lambda s: s.dma_start(out=t_xhb, in_=XHB[0][:]), inc=dS)
                        st["xhb_tap"] = dS.v
                    E.wait("tensor", ve, st["apply"][buf])
                    E.wait("tensor", se, st["xht"][buf])
                    for c in range(8):
                        E.op("tensor", lambda tt, _c=c, _b=buf: tt.transpose(
                            out=pT[_b][:, 128 * _c:128 * _c + 128],
                            in_=XHB[_b][:, 128 * _c:128 * _c + 128],
                            identity=EYEB[:]), inc=(pe if c == 7 else None))
                    st["peT"][buf] = pe.v
                    E.wait("scalar", pe, st["peT"][buf])
                    E.op("scalar", lambda s, _b=buf: s.copy(out=XHT[_b][:], in_=pT[_b][:]), inc=se)
                    st["xht"][buf] = se.v

                def stage_ck(t):
                    buf = t % 2
                    E.wait("tensor", se, st["xht"][buf])
                    E.wait("tensor", ve, st["ckt"][buf])
                    for c in range(8):
                        E.op("tensor", lambda tt, _c=c, _b=buf: tt.matmul(
                            out=pCK[:], lhsT=WKVB[:, 128 * _c:128 * _c + 128],
                            rhs=XHT[_b][:, 128 * _c:128 * _c + 128],
                            start=(_c == 0), stop=(_c == 7)), inc=(pe if c == 7 else None))
                    st["ckmm"][buf] = pe.v

                def stage_ckevac(t):
                    buf = t % 2
                    E.wait("vector", pe, st["ckmm"][buf])
                    E.op("vector", lambda v, _b=buf: v.tensor_copy(out=CKT[_b][:], in_=pCK[:]), inc=ve)
                    st["ckt"][buf] = ve.v

                def stage_kvq(t):
                    buf = t % 2
                    E.wait("tensor", ve, st["ckt"][buf])
                    E.wait("tensor", se, st["kevac"])
                    for c in range(4):
                        E.op("tensor", lambda tt, _c=c, _b=buf: tt.matmul(
                            out=pK[:, 128 * _c:128 * _c + 128],
                            lhsT=WKUP[:, 128 * _c:128 * _c + 128], rhs=CKT[_b][:],
                            start=True, stop=True), inc=(pe if c == 3 else None))
                    pe_k = pe.v
                    E.wait("tensor", se, st["srope"])
                    E.op("tensor", lambda tt, _b=buf: tt.matmul(out=pROPE[:, 0:128], lhsT=WKR[:, 0:16],
                                                                rhs=CKT[_b][:], start=True, stop=True))
                    E.op("tensor", lambda tt, _b=buf: tt.matmul(out=pROPE[:, 128:256], lhsT=WKR[:, 16:32],
                                                                rhs=CKT[_b][:], start=True, stop=True), inc=pe)
                    E.op("tensor", lambda tt, _b=buf: tt.matmul(out=pV[:], lhsT=CKT[_b][:], rhs=WVUP[:],
                                                                start=True, stop=True), inc=pe)
                    pe_v = pe.v
                    for c in range(4):
                        pq = pQa if c % 2 == 0 else pQb
                        E.wait("tensor", se, st["q_evac"][c % 2])
                        for kk in range(8):
                            E.op("tensor", lambda tt, _c=c, _k=kk, _pq=pq, _b=buf: tt.matmul(
                                out=_pq[:],
                                lhsT=WQB[:, 512 * _k + 128 * _c:512 * _k + 128 * _c + 128],
                                rhs=XHT[_b][:, 128 * _k:128 * _k + 128],
                                start=(_k == 0), stop=(_k == 7)), inc=(pe if kk == 7 else None))
                        E.wait("scalar", pe)
                        E.op("scalar", lambda s, _c=c, _t=t, _pq=pq: s.copy(
                            out=QT[_c][:, 128 * _t:128 * _t + 128], in_=_pq[:]), inc=se)
                        st["q_evac"][c % 2] = se.v
                    for kk in range(8):
                        E.op("tensor", lambda tt, _k=kk, _b=buf: tt.matmul(
                            out=pROPE[:, 256:384], lhsT=WQR[:, 32 * _k:32 * _k + 16],
                            rhs=XHT[_b][:, 128 * _k:128 * _k + 128],
                            start=(_k == 0), stop=(_k == 7)))
                    for kk in range(8):
                        E.op("tensor", lambda tt, _k=kk, _b=buf: tt.matmul(
                            out=pROPE[:, 384:512], lhsT=WQR[:, 32 * _k + 16:32 * _k + 32],
                            rhs=XHT[_b][:, 128 * _k:128 * _k + 128],
                            start=(_k == 0), stop=(_k == 7)), inc=(pe if kk == 7 else None))
                    pe_rope = pe.v
                    st["kvq_pe"][buf] = pe.v
                    E.wait("scalar", pe, pe_k)
                    for c in range(4):
                        E.op("scalar", lambda s, _c=c, _t=t: s.copy(
                            out=KT[_c][:, 128 * _t:128 * _t + 128],
                            in_=pK[:, 128 * _c:128 * _c + 128]), inc=(se if c == 3 else None))
                    st["kevac"] = se.v
                    E.wait("scalar", pe, pe_v)
                    E.op("scalar", lambda s, _t=t: s.copy(
                        out=VS[:].rearrange("p (u h c) -> p u h c", h=NHC, c=65)[:, _t, :, 0:64],
                        in_=pV[:].rearrange("p (h c) -> p h c", c=64)), inc=se)
                    E.op("vector", lambda v, _t=t: v.memset(
                        VS[:].rearrange("p (u h c) -> p u h c", h=NHC, c=65)[:, _t, :, 64:65],
                        1.0), inc=ve)
                    E.wait("scalar", pe, pe_rope)
                    E.wait("scalar", ve, st["rope_ve"][t % 2])
                    E.op("scalar", lambda s, _b=t % 2: s.copy(out=SROPE[_b][:], in_=pROPE[:]), inc=se)
                    st["srope"] = se.v
                    st["srope2"][t % 2] = se.v

                def stage_rope(t):
                    sb_ = t % 2
                    E.wait("vector", se, st["srope2"][sb_])
                    E.wait("vector", ve)
                    cs = slice(128 * t, 128 * t + 128)
                    for (x1o, x2o, OUTT) in ((0, 128, KH), (256, 384, QH)):
                        E.op("vector", lambda v, _a=x1o, _c=cs, _b=sb_: v.tensor_mul(
                            out=RT1[:], in0=SROPE[_b][:, _a:_a + 128], in1=COS[:, _c]))
                        E.op("vector", lambda v, _b2=x2o, _c=cs, _b=sb_: v.tensor_mul(
                            out=RT2[:], in0=SROPE[_b][:, _b2:_b2 + 128], in1=SIN[:, _c]))
                        E.op("vector", lambda v, _b2=x2o, _c=cs, _b=sb_: v.tensor_mul(
                            out=RT3[:], in0=SROPE[_b][:, _b2:_b2 + 128], in1=COS[:, _c]))
                        E.op("vector", lambda v, _a=x1o, _c=cs, _b=sb_: v.tensor_mul(
                            out=RT4[:], in0=SROPE[_b][:, _a:_a + 128], in1=SIN[:, _c]), inc=ve)
                        E.wait("vector", ve)
                        E.op("vector", lambda v, _O=OUTT, _c=cs: v.tensor_sub(
                            out=_O[0:16, _c], in0=RT1[:], in1=RT2[:]))
                        E.op("vector", lambda v, _O=OUTT, _c=cs: v.tensor_add(
                            out=_O[96:112, _c], in0=RT3[:], in1=RT4[:]), inc=ve)
                        E.wait("vector", ve)
                    st["rope_ve"][sb_] = ve.v

                stage_load(0)
                stage_load(1)
                for i in range(NT + 3):
                    if i < NT:
                        stage_ln(i)
                    if 1 <= i <= NT:
                        stage_T(i - 1)
                    if i + 2 < NT:
                        stage_load(i + 2)
                    if 3 <= i <= NT + 2:
                        stage_rope(i - 3)
                    if 2 <= i <= NT + 1:
                        stage_kvq(i - 2)
                    if 1 <= i <= NT:
                        stage_ck(i - 1)
                        stage_ckevac(i - 1)

                if t_kt0 is not None:
                    E.wait("sync", ve)
                    E.wait("sync", pe)
                    E.wait("sync", gp)
                    E.wait("sync", se)
                    E.dma("sync", lambda s: s.dma_start(out=t_kt0, in_=KT[0][:]), inc=dS)
                    E.dma("sync", lambda s: s.dma_start(out=t_qt0, in_=QT[0][:]), inc=dS)
                    if t_kh is not None:
                        E.dma("sync", lambda s: s.dma_start(out=t_kh, in_=KH[:]), inc=dS)
                    if t_qh is not None:
                        E.dma("sync", lambda s: s.dma_start(out=t_qh, in_=QH[:]), inc=dS)
                    E.dma("sync", lambda s: s.dma_start(out=t_vs, in_=VS[:]), inc=dS)
                    E.wait("sync", dS)

                for eng in ENGS:
                    E.wait_all(eng)

            run_block(nc, sems, p0_walk)

        if phases == 1:
            return nc, tap_out

        with ExitStack() as p1:
            sb1 = lambda n, s, d: p1.enter_context(nc.sbuf_tensor(n, s, d, side="right"))
            ps1 = lambda n, s, d: p1.enter_context(nc.psum_tensor(n, s, d))
            KH2 = [KH, sb1("KHb", [112, T], BF16)]
            QH2 = [QH, sb1("QHb", [112, T], BF16)]
            PB = [sb1(f"PB{i}", [128, 2 * 512], BF16) for i in range(3)]
            RECB = sb1("RECB", [1, 512], BF16)
            BCS = sb1("BCS", [64, 512], F32)
            S_ = [ps1(f"S{i}", [128, 2 * 512], F32) for i in range(3)]
            AVp = ps1("AVp", [128, 1024], F32)

            t_col0 = tap("col0", [128, T], BF16)
            t_col1 = tap("col1", [128, T], BF16)

            def p1_walk(E):
                E.op("vector", lambda v: v.tensor_copy(out=KH2[1][0:32, :], in_=KH2[0][0:32, :]))
                E.op("vector", lambda v: v.tensor_copy(out=KH2[1][96:112, :], in_=KH2[0][96:112, :]))
                E.op("vector", lambda v: v.tensor_copy(out=QH2[1][0:32, :], in_=QH2[0][0:32, :]))
                E.op("vector", lambda v: v.tensor_copy(out=QH2[1][96:112, :], in_=QH2[0][96:112, :]), inc=ve)

                div_done = [0, 0]
                last_exp = [0, 0, 0]
                asm_pe_gate = [0, 0]
                asm_done = [0, 0]
                qt_par = 0

                def assemble(h):
                    hb = h % 2
                    c, r = h // 2, 64 * (h % 2)
                    E.wait("vector", pe, asm_pe_gate[hb])
                    for (DST, SRC) in ((KH2[hb], KT[c]), (QH2[hb], QT[c])):
                        E.op("vector", lambda v, _D=DST, _S=SRC, _r=r: v.tensor_copy(
                            out=_D[32:64, :], in_=_S[_r:_r + 32, :]))
                        E.op("vector", lambda v, _D=DST, _S=SRC, _r=r: v.tensor_copy(
                            out=_D[64:96, :], in_=_S[_r + 32:_r + 64, :]))
                    E.op("vector", lambda v: v.engine_nop(), inc=ve)
                    asm_done[hb] = ve.v

                assemble(0)
                for h in range(NHC):
                    hb = h % 2
                    KHc, QHc = KH2[hb], QH2[hb]
                    if h + 1 < NHC:
                        assemble(h + 1)
                    gctr = 0
                    for qt in range(NQT):
                        ah = qt_par = (h * NQT + qt) % 2
                        av0 = 512 * ah
                        nk = 4 * qt + 4
                        groups = [list(range(g, min(g + 2, nk))) for g in range(0, nk, 2)]
                        ng = len(groups)
                        gpar = [(gctr + i) % 3 for i in range(ng)]
                        gctr = (gctr + ng) % 3
                        pe_sc = {}
                        av_cnt = 0

                        def emit_scores(gi):
                            grp = groups[gi]
                            par = gpar[gi]
                            E.wait("tensor", ve, asm_done[hb])
                            E.wait("tensor", se, last_exp[par])
                            for j, kt in enumerate(grp):
                                diag = kt >= 4 * qt
                                last_j = (j == len(grp) - 1)
                                E.op("tensor", lambda tt, _kt=kt, _qt=qt, _p=par, _j=j, _d=diag, _K=KHc, _Q=QHc: tt.matmul(
                                    out=S_[_p][:, 512 * _j:512 * _j + 512],
                                    lhsT=_K[:, 128 * _kt:128 * _kt + 128],
                                    rhs=_Q[:, 512 * _qt:512 * _qt + 512],
                                    start=True, stop=(not _d)),
                                    inc=(pe if (last_j and not diag) else None))
                                if diag:
                                    dlt = kt - 4 * qt
                                    E.op("tensor", lambda tt, _p=par, _j=j, _dl=dlt: tt.matmul(
                                        out=S_[_p][:, 512 * _j:512 * _j + 512],
                                        lhsT=EYEB[:], rhs=MSK[:, 512 * _dl:512 * _dl + 512],
                                        start=False, stop=True),
                                        inc=(pe if last_j else None))
                            pe_sc[gi] = pe.v

                        def emit_exp(gi):
                            grp = groups[gi]
                            par = gpar[gi]
                            E.wait("scalar", pe, pe_sc[gi])
                            E.op("scalar", lambda s, _p=par, _n=len(grp): s.activation(
                                out=PB[_p][:, 0:512 * _n], in_=S_[_p][:, 0:512 * _n],
                                func=AF.Exp, scale=C_SCALE[:], bias=C_MEXP[:]), inc=se)
                            last_exp[par] = se.v

                        def emit_av(gi):
                            nonlocal av_cnt
                            grp = groups[gi]
                            par = gpar[gi]
                            E.wait("tensor", se, last_exp[par])
                            if gi == 0:
                                E.wait("tensor", ve, div_done[ah])
                            for j, kt in enumerate(grp):
                                first = (av_cnt == 0)
                                last = (av_cnt == nk - 1)
                                av_cnt += 1
                                E.op("tensor", lambda tt, _kt=kt, _h=h, _p=par, _j=j, _f=first, _l=last, _a=av0: tt.matmul(
                                    out=AVp[0:65, _a:_a + 512],
                                    lhsT=VS[:, 520 * _kt + 65 * _h:520 * _kt + 65 * _h + 65],
                                    rhs=PB[_p][:, 512 * _j:512 * _j + 512],
                                    start=_f, stop=_l), inc=(pe if last else None))

                        emit_scores(0)
                        emit_exp(0)
                        if ng > 1:
                            emit_scores(1)
                            emit_exp(1)
                        for gi in range(2, ng):
                            emit_scores(gi)
                            emit_av(gi - 2)
                            emit_exp(gi)
                        if ng > 1:
                            emit_av(ng - 2)
                        emit_av(ng - 1)
                        pe_av = pe.v
                        E.wait("vector", pe, pe_av)

                        def _recip_bf16(v, a):
                            with nc.allow_low_precision(reason="bf16 store of fp32 reciprocal"):
                                return v.reciprocal(out=RECB[:], in_=AVp[64:65, a:a + 512])
                        E.op("vector", lambda v, _a=av0: _recip_bf16(v, _a), inc=ve)
                        E.wait("tensor", ve)
                        E.op("tensor", lambda tt, _a=av0: tt.matmul(
                            out=AVp[64:128, _a:_a + 512], lhsT=ONE1[:], rhs=RECB[:],
                            start=True, stop=True), inc=pe)
                        E.wait("vector", pe)
                        E.op("vector", lambda v, _a=av0: v.tensor_copy(
                            out=BCS[:], in_=AVp[64:128, _a:_a + 512]), inc=ve)
                        E.wait("vector", ve)
                        E.op("vector", lambda v, _h=h, _qt=qt, _a=av0: v.scalar_tensor_tensor(
                            out=COL[_h // 2][64 * (_h % 2):64 * (_h % 2) + 64,
                                             512 * _qt:512 * _qt + 512],
                            in0=AVp[0:64, _a:_a + 512], scalar=1.0, in1=BCS[:],
                            op0=ALU.mult, op1=ALU.mult), inc=ve)
                        div_done[ah] = ve.v
                    asm_pe_gate[hb] = pe.v

                if t_col0 is not None:
                    E.wait("sync", ve)
                    E.dma("sync", lambda s: s.dma_start(out=t_col0, in_=COL[0][:]), inc=dS)
                    if t_col1 is not None:
                        E.dma("sync", lambda s: s.dma_start(out=t_col1, in_=COL[1][:]), inc=dS)
                    E.wait("sync", dS)

                for eng in ENGS:
                    E.wait_all(eng)

            run_block(nc, sems, p1_walk)

        if phases == 2:
            return nc, tap_out

        S01.close()

        S23 = ExitStack()
        X2T = [S23.enter_context(nc.sbuf_tensor(f"X2T{i}", [128, OWN], BF16, side="right"))
               for i in range(8)]
        WGU = [S23.enter_context(nc.sbuf_tensor(f"WGU{i}", [128, 2 * 1024], BF16, side="right"))
               for i in range(2)]
        with ExitStack() as p2:
            sb2 = lambda n, s, d: p2.enter_context(nc.sbuf_tensor(n, s, d, side="right"))
            ps2 = lambda n, s, d: p2.enter_context(nc.psum_tensor(n, s, d))
            WOB = sb2("WOB", [128, 4 * 1024], BF16)
            PRT = [sb2(f"PRT{i}", [128, D], BF16) for i in range(2)]
            XO = [sb2(f"XO{i}", [128, D], F32) for i in range(2)]
            AT = [sb2(f"AT{i}", [128, D], BF16) for i in range(2)]
            R1 = sb2("R1", [128, D], F32)
            X2B = sb2("X2B", [128, D], BF16)
            VAR2 = sb2("VAR2", [128, 1], F32)
            RSTD2 = sb2("RSTD2", [128, 1], F32)
            BNS2 = sb2("BNS2", [128, 12], F32)
            BNA2 = sb2("BNA2", [128, 2], F32)
            pWO = [ps2(f"pWO{i}", [128, 1024], F32) for i in range(2)]
            pT2 = ps2("pT2", [128, 1024], BF16)

            t_r1 = tap("r1", [OWN, D], F32)
            t_x2 = tap("x2", [128, OWN], BF16)

            def p2_walk2(E):
                p2_state = {}
                for fc in range(4):
                    E.dma("gpsimd", lambda g, _f=fc: g.dma_start(
                        out=WOB[:, 1024 * _f:1024 * _f + 1024],
                        in_=wo[128 * _f:128 * _f + 128, :]), inc=dG)
                dl_wo = dG.v
                prt_free = [0, 0]
                evac_done = [0, 0]
                tile_order = [t for t in range(NT) if (t % 16) // 8 == 0] + \
                             [t for t in range(NT) if (t % 16) // 8 == 1]
                chunk0_done = [0, 0]
                for ti, t in enumerate(tile_order):
                    buf = ti % 2
                    E.wait("tensor", dG, dl_wo)
                    E.wait("tensor", ve, evac_done[buf])
                    for half in range(2):
                        for fc in range(4):
                            E.op("tensor", lambda t_, _f=fc, _t=t, _h=half, _b=buf: t_.matmul(
                                out=pWO[_b][:, 512 * _h:512 * _h + 512],
                                lhsT=COL[_f][:, 128 * _t:128 * _t + 128],
                                rhs=WOB[:, 1024 * _f + 512 * _h:1024 * _f + 512 * _h + 512],
                                start=(_f == 0), stop=(_f == 3)),
                                inc=(pe if (half == 1 and fc == 3) else None))
                    pe_wo = pe.v
                    E.wait("vector", pe, pe_wo)
                    E.wait("vector", dP[buf], prt_free[buf])
                    E.op("vector", lambda v, _b=buf: v.tensor_copy(out=PRT[_b][:], in_=pWO[_b][:]), inc=ve)
                    evac_done[buf] = ve.v
                    srow = 2048 * ((t % 16) // 8) + 1024 * (t // 16) + 128 * (t % 8)
                    E.wait("sync", ve, evac_done[buf])
                    E.dma("sync", lambda s, _r=srow, _b=buf: s.dma_start(
                        out=snd.ap()[_r:_r + 128, :], in_=PRT[_b][:]), inc=dP[buf])
                    prt_free[buf] = dP[buf].v
                    if ti == 15:
                        chunk0_done = [dP[0].v, dP[1].v]
                        E.wait("gpsimd", dP[0], chunk0_done[0])
                        E.wait("gpsimd", dP[1], chunk0_done[1])
                        E.op("gpsimd", lambda g: g.collective_compute(
                            "ReduceScatter", ALU.add,
                            replica_groups=[[0, 1], [2, 3], [4, 5], [6, 7]],
                            ins=[snd.ap()[0:2048, :].opt()],
                            outs=[rcv.ap()[0:1024, :].opt()]), inc=cc)
                E.wait("gpsimd", dP[0])
                E.wait("gpsimd", dP[1])
                E.op("gpsimd", lambda g: g.collective_compute(
                    "ReduceScatter", ALU.add,
                    replica_groups=[[0, 1], [2, 3], [4, 5], [6, 7]],
                    ins=[snd.ap()[2048:4096, :].opt()],
                    outs=[rcv.ap()[1024:2048, :].opt()]), inc=cc2)
                E.dma("gpsimd", lambda g: g.dma_start(
                    out=WGU[0][:, 0:1024].rearrange("p (c j) -> p c j", j=128),
                    in_=wgT.rearrange("(c p) j -> p c j", p=128)[:, :, 0:128]), inc=dW[0])
                E.dma("gpsimd", lambda g: g.dma_start(
                    out=WGU[0][:, 1024:2048].rearrange("p (c j) -> p c j", j=128),
                    in_=wuT.rearrange("(c p) j -> p c j", p=128)[:, :, 0:128]), inc=dW[0])

                r1_done = 0
                xo_free = [0, 0]
                at_free = [0, 0]
                pe_t2 = 0
                for t in range(16):
                    buf = t % 2
                    E.wait("sync", ve, xo_free[buf])
                    E.dma("sync", lambda s, _t=t, _b=buf: s.dma_start(
                        out=XO[_b][:], in_=x_own[128 * _t:128 * _t + 128, :]), inc=dX[buf])
                    E.wait("sync", cc if t < 8 else cc2)
                    E.wait("sync", ve, at_free[buf])
                    E.dma("sync", lambda s, _t=t, _b=buf: s.dma_start(
                        out=AT[_b][:], in_=rcv.ap()[128 * _t:128 * _t + 128, :]), inc=dA[buf])
                    E.wait("vector", dX[buf])
                    E.wait("vector", dA[buf])
                    E.wait("vector", dS, r1_done)
                    E.op("vector", lambda v, _b=buf: v.tensor_add(out=R1[:], in0=XO[_b][:], in1=AT[_b][:]), inc=ve)
                    xo_free[buf] = ve.v
                    at_free[buf] = ve.v
                    E.wait("sync", ve)
                    E.dma("sync", lambda s, _t=t: s.dma_start(
                        out=r1spill.ap()[128 * _t:128 * _t + 128, :], in_=R1[:]), inc=dS)
                    r1_done = dS.v
                    if t_r1 is not None:
                        E.dma("sync", lambda s, _t=t: s.dma_start(
                            out=t_r1[128 * _t:128 * _t + 128, :], in_=R1[:]), inc=dS)
                        r1_done = dS.v
                    E.wait("vector", ve)
                    E.op("vector", lambda v: v.bn_stats(out=BNS2[:, 0:6], in_=R1[:, 0:512]))
                    E.op("vector", lambda v: v.bn_stats(out=BNS2[:, 6:12], in_=R1[:, 512:1024]), inc=ve)
                    E.wait("vector", ve)
                    E.op("vector", lambda v: v.bn_aggr(out=BNA2[:], in_=BNS2[:]), inc=ve)
                    E.wait("vector", ve)
                    E.op("vector", lambda v: v.tensor_scalar_add(out=VAR2[:], in0=BNA2[:, 1:2], scalar1=1e-5), inc=ve)
                    E.wait("scalar", ve)
                    E.op("scalar", lambda s: s.activation(out=RSTD2[:], in_=VAR2[:], func=AF.Ln), inc=se)
                    E.wait("scalar", se)
                    E.op("scalar", lambda s: s.activation(out=RSTD2[:], in_=RSTD2[:], func=AF.Exp, scale=C_HALF[:]), inc=se)
                    E.wait("vector", se)
                    E.wait("vector", pe, pe_t2)
                    E.op("vector", lambda v: v.tensor_scalar(
                        out=X2B[:], in0=R1[:], scalar1=BNA2[:, 0:1], scalar2=RSTD2[:],
                        op0=ALU.subtract, op1=ALU.mult), inc=ve)
                    ve_x2 = ve.v
                    E.wait("tensor", ve, ve_x2)
                    for c in range(8):
                        E.op("tensor", lambda t_, _c=c: t_.transpose(
                            out=pT2[:, 128 * _c:128 * _c + 128],
                            in_=X2B[:, 128 * _c:128 * _c + 128],
                            identity=EYEB[:]), inc=(pe if c == 7 else None))
                    pe_t2 = pe.v
                    E.wait("vector", pe, pe_t2)
                    for c in range(8):
                        E.op("vector", lambda v, _c=c, _t=t: v.tensor_copy(
                            out=X2T[_c][:, 128 * _t:128 * _t + 128],
                            in_=pT2[:, 128 * _c:128 * _c + 128]),
                            inc=(ve if c == 7 else None))

                if t_x2 is not None:
                    E.wait("sync", ve)
                    E.dma("sync", lambda s: s.dma_start(out=t_x2, in_=X2T[0][:]), inc=dS)
                    E.wait("sync", dS)

                for eng in ENGS:
                    E.wait_all(eng)

            run_block(nc, sems, p2_walk2)

        if phases == 3:
            return nc, tap_out

        with ExitStack() as p3:
            sb3 = lambda n, s, d: p3.enter_context(nc.sbuf_tensor(n, s, d, side="right"))
            ps3 = lambda n, s, d: p3.enter_context(nc.psum_tensor(n, s, d))
            HB = p3.enter_context(nc.sbuf_tensor("HB", [128, 24 * OWN], BF16, side="left"))

            WDA1 = p3.enter_context(nc.sbuf_tensor("WDA1", [128, 12 * 1024], BF16, side="left"))
            p3a = ExitStack()
            GS = p3a.enter_context(nc.sbuf_tensor("GS", [128, 512], BF16, side="right"))
            US = p3a.enter_context(nc.sbuf_tensor("US", [128, 512], BF16, side="right"))
            pG = p3a.enter_context(nc.psum_tensor("pG", [128, 512], F32))
            pU = p3a.enter_context(nc.psum_tensor("pU", [128, 512], F32))

            dl_wd_early = [0]

            def p3a_walk(E):
                wgu_free = [0, 0]
                silu_done = 0
                umul_done = 0
                for hc in range(12):
                    E.dma("gpsimd", lambda g, _h=hc: g.dma_start(
                        out=WDA1[:, 1024 * _h:1024 * _h + 1024],
                        in_=wdT[128 * _h:128 * _h + 128, :]), inc=dG)
                dl_wd_early[0] = dG.v
                for hc in range(24):
                    wb = hc % 2
                    if hc > 0:
                        E.wait("gpsimd", pe, wgu_free[wb])
                        E.dma("gpsimd", lambda g, _h=hc, _b=wb: g.dma_start(
                            out=WGU[_b][:, 0:1024].rearrange("p (c j) -> p c j", j=128),
                            in_=wgT.rearrange("(c p) j -> p c j", p=128)[:, :, 128 * _h:128 * _h + 128]), inc=dW[wb])
                        E.dma("gpsimd", lambda g, _h=hc, _b=wb: g.dma_start(
                            out=WGU[_b][:, 1024:2048].rearrange("p (c j) -> p c j", j=128),
                            in_=wuT.rearrange("(c p) j -> p c j", p=128)[:, :, 128 * _h:128 * _h + 128]), inc=dW[wb])
                    dl_gu = dW[wb].v
                    for tt in range(4):
                        E.wait("tensor", dW[wb], dl_gu)
                        E.wait("tensor", se, silu_done)
                        for c in range(8):
                            E.op("tensor", lambda t_, _c=c, _t=tt, _b=wb: t_.matmul(
                                out=pG[:], lhsT=WGU[_b][:, 128 * _c:128 * _c + 128],
                                rhs=X2T[_c][:, 512 * _t:512 * _t + 512],
                                start=(_c == 0), stop=(_c == 7)), inc=(pe if c == 7 else None))
                        pe_g = pe.v
                        E.wait("tensor", ve, umul_done)
                        for c in range(8):
                            E.op("tensor", lambda t_, _c=c, _t=tt, _b=wb: t_.matmul(
                                out=pU[:], lhsT=WGU[_b][:, 1024 + 128 * _c:1024 + 128 * _c + 128],
                                rhs=X2T[_c][:, 512 * _t:512 * _t + 512],
                                start=(_c == 0), stop=(_c == 7)), inc=(pe if c == 7 else None))
                        pe_u = pe.v
                        if tt == 3:
                            wgu_free[wb] = pe.v
                        E.wait("scalar", pe, pe_g)
                        E.wait("scalar", ve, umul_done)
                        E.op("scalar", lambda s: s.activation(out=GS[:], in_=pG[:], func=AF.Silu,
                                                              scale=C_S127[:]), inc=se)
                        silu_done = se.v
                        E.wait("vector", pe, pe_u)
                        E.op("vector", lambda v: v.tensor_copy(out=US[:], in_=pU[:]), inc=ve)
                        E.wait("vector", se, silu_done)
                        E.wait("vector", ve)
                        E.op("vector", lambda v, _h=hc, _t=tt: v.tensor_mul(
                            out=HB[:, OWN * _h + 512 * _t:OWN * _h + 512 * _t + 512],
                            in0=GS[:], in1=US[:]), inc=ve)
                        umul_done = ve.v
                for eng in ENGS:
                    E.wait_all(eng)

            run_block(nc, sems, p3a_walk)
            p3a.close()
            S23.close()

            WDA2 = p3.enter_context(nc.sbuf_tensor("WDA2", [128, 12 * 1024], BF16, side="right"))
            sbl = lambda n, s, d: p3.enter_context(nc.sbuf_tensor(n, s, d, side="left"))
            FDT = sbl("FDT", [128, 4096], F32)
            R1T = [sbl(f"R1T{i}", [128, D], F32) for i in range(2)]
            OStg = sbl("OStg", [128, D], F32)
            pD = ps3("pD", [128, 3072], F32)
            pR = ps3("pR", [128, 1024], F32)

            def p3b_walk(E):
                for hc in range(12):
                    E.dma("gpsimd", lambda g, _h=hc: g.dma_start(
                        out=WDA2[:, 1024 * _h:1024 * _h + 1024],
                        in_=wdT[128 * (_h + 12):128 * (_h + 12) + 128, :]), inc=dG)
                dl_wd1 = dl_wd_early[0]
                dl_wd = dG.v

                fdt_free = 0
                r1t_free = [0, 0]
                ost_free = 0
                stt_done = 0
                for tt in range(4):
                    E.wait("tensor", dG, dl_wd1)
                    E.wait("tensor", se, fdt_free)
                    E.wait("tensor", ve, stt_done)
                    for hc in range(24):
                        if hc == 12:
                            E.wait("tensor", dG, dl_wd)
                        WDAx = WDA1 if hc < 12 else WDA2
                        hco = hc % 12
                        for dc in range(8):
                            dst = pD[:, 512 * dc:512 * dc + 512] if dc < 6 else \
                                  pR[:, 512 * (dc - 6):512 * (dc - 6) + 512]
                            E.op("tensor", lambda t_, _h=hc, _ho=hco, _W=WDAx, _d=dc, _t=tt, _dst=dst: t_.matmul(
                                out=_dst,
                                lhsT=_W[:, 1024 * _ho + 128 * _d:1024 * _ho + 128 * _d + 128],
                                rhs=HB[:, OWN * _h + 512 * _t:OWN * _h + 512 * _t + 512],
                                start=(_h == 0), stop=(_h == 23)),
                                inc=(pe if (hc == 23 and dc == 7) else None))
                    pe_d = pe.v
                    E.wait("scalar", pe, pe_d)
                    for dc in range(8):
                        src = pD[:, 512 * dc:512 * dc + 512] if dc < 6 else \
                              pR[:, 512 * (dc - 6):512 * (dc - 6) + 512]
                        E.op("scalar", lambda s, _d=dc, _src=src: s.activation(
                            out=FDT[:, 512 * _d:512 * _d + 512], in_=_src,
                            func=AF.Copy, scale=C_S127SQ[:]),
                            inc=(se if dc == 7 else None))
                    fdt_free = se.v
                    for s4 in range(4):
                        sb_ = s4 % 2
                        E.wait("sync", ve, r1t_free[sb_])
                        E.dma("sync", lambda s, _t=tt, _s=s4, _b=sb_: s.dma_start(
                            out=R1T[_b][:],
                            in_=r1spill.ap()[512 * _t + 128 * _s:512 * _t + 128 * _s + 128, :]),
                            inc=dX[sb_])
                        dl_r1 = (sb_, dX[sb_].v)
                        E.wait("tensor", se, fdt_free)
                        if s4 >= 2:
                            E.wait("tensor", ve)
                        for dc in range(8):
                            E.op("tensor", lambda t_, _d=dc, _s=s4: t_.transpose(
                                out=pD[:, 1024 * (_s % 2) + 128 * _d:1024 * (_s % 2) + 128 * _d + 128],
                                in_=FDT[:, 512 * _d + 128 * _s:512 * _d + 128 * _s + 128],
                                identity=EYEF[:]), inc=(pe if dc == 7 else None))
                        pe_tr = pe.v
                        E.wait("vector", pe, pe_tr)
                        E.wait("vector", dX[dl_r1[0]], dl_r1[1])
                        E.wait("vector", dS, ost_free)
                        E.op("vector", lambda v, _s=s4, _b=sb_: v.scalar_tensor_tensor(
                            out=OStg[:], in0=pD[:, 1024 * (_s % 2):1024 * (_s % 2) + 1024],
                            scalar=1.0, in1=R1T[_b][:],
                            op0=ALU.mult, op1=ALU.add), inc=ve)
                        ve_o = ve.v
                        r1t_free[sb_] = ve_o
                        E.wait("sync", ve, ve_o)
                        E.dma("sync", lambda s, _t=tt, _s=s4: s.dma_start(
                            out=y[512 * _t + 128 * _s:512 * _t + 128 * _s + 128, :],
                            in_=OStg[:]), inc=dS)
                        ost_free = dS.v
                    stt_done = ve.v

                for eng in ENGS:
                    E.wait_all(eng)

            run_block(nc, sems, p3b_walk)

    return nc, tap_out


def _host_inputs(core, inputs):
    b, p = core // 2, core % 2
    h0 = NHC * p * HD
    x = np.ascontiguousarray(np.asarray(inputs["L0_attn_x"])[b])
    out = dict(
        x_seq=x,
        x_own=np.ascontiguousarray(x[OWN * p:OWN * p + OWN]),
        wkv=np.ascontiguousarray(np.asarray(inputs["L0_attn_W_kv_down"])),
        wkup=np.ascontiguousarray(np.asarray(inputs["L0_attn_W_k_up"])[:, h0:h0 + NHC * HD]),
        wvup=np.ascontiguousarray(np.asarray(inputs["L0_attn_W_v_up"])[:, h0:h0 + NHC * HD]),
        wq=np.ascontiguousarray(np.asarray(inputs["L0_attn_W_q"])[:, h0:h0 + NHC * HD]),
        wqr=np.ascontiguousarray(np.asarray(inputs["L0_attn_W_q_rope"])),
        wkr=np.ascontiguousarray(np.asarray(inputs["L0_attn_W_k_rope"])),
        wo=np.ascontiguousarray(np.asarray(inputs["L0_attn_W_o"])[h0:h0 + NHC * HD, :]),
        wgT=np.ascontiguousarray(np.asarray(inputs["L0_ffn_w_gate"]).T),
        wuT=np.ascontiguousarray(np.asarray(inputs["L0_ffn_w_up"]).T),
        wdT=np.ascontiguousarray(np.asarray(inputs["L0_ffn_w_down"]).T),
    )
    inv_freq = 1.0 / 10000.0 ** (np.arange(0, DR, 2, dtype=np.float32) / DR)
    tpos = np.arange(T, dtype=np.float32)
    freqs = np.outer(tpos, inv_freq).astype(np.float32)
    out["cosT"] = np.ascontiguousarray(np.cos(freqs).T.astype(np.float32))
    out["sinT"] = np.ascontiguousarray(np.sin(freqs).T.astype(np.float32))
    out["eye"] = np.eye(128, dtype=np.float32)
    masks_ = np.zeros((128, 4, 512), dtype=np.float32)
    k_loc = np.arange(128)[:, None]
    q_loc = np.arange(512)[None, :]
    for i, dlt in enumerate((0, 128, 256, 384)):
        masks_[:, i, :] = np.where(k_loc + dlt > q_loc, -1e9, 0.0)
    out["masks"] = np.ascontiguousarray(masks_.reshape(128, 4 * 512))
    return out


def kernel(**inputs):
    nc, _ = build()
    in_maps = [_host_inputs(c, inputs) for c in range(8)]
    res = run_bass_kernel_spmd(nc, in_maps, core_ids=list(range(8)))
    out = np.zeros((4, T, D), dtype=np.float32)
    for c in range(8):
        b, p = c // 2, c % 2
        out[b, OWN * p:OWN * p + OWN] = np.asarray(res.results[c]["y"], dtype=np.float32)
    return out
